# version 02
# baseline (speedup 1.0000x reference)
import contextlib
import numpy as np
import concourse.bass as bass
import concourse.mybir as mybir
from concourse.bass_utils import run_bass_kernel_spmd

F32 = mybir.dt.float32
AF = mybir.ActivationFunctionType
ALU = mybir.AluOpType
AX = mybir.AxisListType

SUB = 9
B, N, D = 8, 2048, 1024
T = N
DC = D // 128
TC = T // 128
GT = B * TC
H0, DH0 = 16, 64
S0, DB = 64, 512
E = 8
F0 = 4096
LCAP0 = 384
RT0 = E * LCAP0 // 128
H1, HD1 = 8, 64
FF1 = 4096
HID1 = 2048
EPS_LN = 1e-5
BIGF = 1.0e9
STD_RATIO = 0.6
SPLAT_RATIO = 0.4
ETA = 0.1
DT_HEB = 0.01
GAMMA_FLOOR = 0.1


class Prog:
    def __init__(self, nc, stack):
        self.nc = nc
        self.c = {}
        self.h = {}
        for name in ("qT", "qV", "qS", "qG", "qD", "qE"):
            self.h[name] = stack.enter_context(nc.semaphore(name))
            self.c[name] = 0

    def inc(self, instr, q, n=1):
        instr.then_inc(self.h[q], n)
        self.c[q] += n
        return instr

    def dma(self, instr, q="qD"):
        return self.inc(instr, q, 16)

    def wait(self, eng, q, val=None):
        v = self.c[q] if val is None else val
        if v > 0:
            eng.wait_ge(self.h[q], v)

    def mark(self, q):
        return self.c[q]


def apof(x):
    return x if isinstance(x, bass.AP) else x.ap()


def raw_ap(t, off_delta, dims):
    a = apof(t)
    return bass.AP(tensor=a.tensor, offset=a.offset + off_delta, ap=dims)


def bcast_free(ap2d, count):
    a = ap2d
    return bass.AP(tensor=a.tensor, offset=a.offset,
                   ap=list(a.ap) + [[0, count]])


def build_nc(stage=99, sub=99):
    nc = bass.Bass("TRN2", target_bir_lowering=False, detect_race_conditions=False)
    dti = lambda nm, shp: nc.dram_tensor(nm, shp, F32, kind="ExternalInput").ap()

    xall = dti("xall", [B * N, D])
    xown = dti("xown", [N, D])
    coremask = dti("coremask", [1, 64])
    ident = dti("identc", [128, 128])
    u128 = dti("u128c", [128, 128])
    onesr = dti("ones_rowc", [1, 512])
    onesc = dti("ones_colc", [128, 1])
    iotar = dti("iota_rowc", [128, 512])
    iotac = dti("iota_colc", [128, 1])
    fvals = dti("fvalsc", [128, 16])
    Wq = dti("Wq", [D, D]); Wk = dti("Wk", [D, D]); Wv = dti("Wv", [D, D])
    Wo = dti("Wo", [D, D]); Wg = dti("Wg", [D, D])
    spT = dti("spT", [D, S0])
    ssc_r = dti("ssc_row", [1, S0])
    simp_r = dti("simp_row", [1, S0])
    We = dti("We", [D, DB]); be_r = dti("be_row", [1, DB])
    be_c = dti("be_col", [128, 4]); btT = dti("btT", [DB, S0])
    Wt = dti("Wt", [S0, DB, DB]); bt_r = dti("bt_r", [S0, DB])
    Wd = dti("Wd", [DB, D]); bd_r = dti("bd_row", [1, D])
    bg_c = dti("bg_col", [128, DC])
    W1 = dti("W1", [E, D, F0])
    W2 = dti("W2", [E, F0, D])
    Wr = dti("Wr", [D, E])
    ln_c = dti("ln_col", [128, 6 * DC])
    w_pot = dti("w_pot", [D, 2]); b_pot_r = dti("b_pot_row", [1, 2])
    w_bk = dti("w_bk", [2, D]); b_bk_c = dti("b_bk_col", [128, DC])
    w_qkv = dti("w_qkv", [D, 3 * H1 * HD1]); b_qkv_r = dti("b_qkv_row", [1, 3 * H1 * HD1])
    w_out = dti("w_out", [H1 * HD1, D]); b_out_c = dti("b_out_col", [128, DC])
    b_out_r = dti("b_out_row", [1, D])
    w_ff1 = dti("w_ff1", [D, FF1]); b_ff1_c = dti("b_ff1_col", [128, FF1 // 128])
    bff1_r = dti("bff1_row", [1, FF1])
    wff1t = dti("wff1t", [32, DC, 128, 128])
    wff2t = dti("wff2t", [32, DC, 128, 128])
    w_ff2 = dti("w_ff2", [FF1, D]); b_ff2_c = dti("b_ff2_col", [128, DC])
    rw = dti("rw", [D, E])
    rsc_c = dti("rsc_col", [128, DC])
    gwT = dti("gwT", [E, 2, D, HID1])
    lw = dti("lw", [E, HID1, D])
    pes_r = dti("pes_row", [1, E])

    out = nc.dram_tensor("out", [N, D], F32, kind="ExternalOutput").ap()
    xn_tm_d = nc.dram_tensor("xn_tm_d", [N, D], F32, kind="Internal").ap()
    scr = nc.dram_tensor("scr", [20, 4096], F32, kind="Internal").ap()
    xls = nc.dram_tensor("xls", [1024, 2048], F32, kind="Internal").ap()
    dbg = nc.dram_tensor("dbg", [128, 8192], F32,
                         kind="ExternalOutput").ap() if stage < 99 else None

    stack = contextlib.ExitStack()
    sbuf = lambda nm, shp, dtype=F32: stack.enter_context(nc.sbuf_tensor(nm, shp, dtype))
    psum = lambda nm, shp: stack.enter_context(nc.psum_tensor(nm, shp, F32))

    with stack:
        S_x = [sbuf(f"S_x{j}", [128, T]) for j in range(DC)]
        S_n = [sbuf(f"S_n{j}", [128, T]) for j in range(DC)]
        S_s = [sbuf(f"S_s{j}", [128, T]) for j in range(5)]
        xt = [sbuf(f"xt{k}", [128, D]) for k in range(2)]
        sqd = sbuf("sqd", [128, D])
        c_id = sbuf("c_id", [128, 128])
        c_u = sbuf("c_u", [128, 128])
        c_1r = sbuf("c_1r", [1, 512])
        c_1c = sbuf("c_1c", [128, 1])
        c_ir = sbuf("c_ir", [128, 512])
        c_ic = sbuf("c_ic", [128, 1])
        c_msk = sbuf("c_msk", [1, 64])
        c_fv = sbuf("c_fv", [128, 16])
        s_wr = sbuf("s_wr", [128, DC * E])
        r_m = sbuf("r_m", [128, 1]); r_sq = sbuf("r_sq", [128, 1])
        r_v1 = sbuf("r_v1", [128, 1])
        r_rsd = sbuf("r_rsd", [128, 1])
        glg = sbuf("glg", [128, GT * 8])
        olg = sbuf("olg", [128, TC * 8])
        gfo = sbuf("gfo", [128, GT * 8])
        ofo = sbuf("ofo", [128, TC * 8])
        r_t1 = sbuf("r_t1", [128, GT * 8])
        r_mx = sbuf("r_mx", [128, GT])
        r_cnt = sbuf("r_cnt", [1, 64])
        r_pfx = sbuf("r_pfx", [1, 8])
        pfx_bc = sbuf("pfx_bc", [128, 8])
        eL_bc = sbuf("eL_bc", [128, 8])
        own_p = sbuf("own_p", [128, TC])
        own_lrk = sbuf("own_lrk", [128, TC * 8])
        own_slot = sbuf("own_slot", [128, TC])
        own_sc = sbuf("own_sc", [128, TC])
        carry = sbuf("carry", [1, 8])
        rowx = sbuf("rowx", [1, 512])
        colx = sbuf("colx", [128, 8])
        denb = sbuf("denb", [1, 1024])
        mcol = sbuf("mcol", [128, 32])
        own_p2 = sbuf("own_p2", [128, 16])

        ps_big = psum("ps_big", [128, 2048])
        ps = [psum(f"ps{k}", [128, 512]) for k in range(4)]

        block = stack.enter_context(nc.Block())
        P = Prog(nc, stack)

        def on_T(f):
            @block.tensor
            def _(t):
                f(t)

        def on_V(f):
            @block.vector
            def _(v):
                f(v)

        def on_S(f):
            @block.scalar
            def _(s):
                f(s)

        def on_D(f):
            @block.sync
            def _(s):
                f(s)

        def on_G(f):
            @block.gpsimd
            def _(g):
                f(g)

        def row_bcast(row_ap, width, dst, tmp_ps=None):
            pieces = []

            def ft(t):
                P.wait(t, "qV")
                ins = None
                for o in range(0, width, 512):
                    w = min(512, width - o)
                    pp = ps[(o // 512) % 2]
                    ra2 = apof(row_ap)
                    ins = t.matmul(pp[:, 0:w], c_1r[0:1, 0:128],
                                   bass.AP(tensor=ra2.tensor, offset=ra2.offset + o,
                                           ap=[ra2.ap[0], [1, w]]),
                                   start=True, stop=True)
                    pieces.append((o, w, pp))
                P.inc(ins, "qT", 1)
            on_T(ft)
            done = P.mark("qT")

            def fv(v):
                P.wait(v, "qT", done)
                for (o, w, pp) in pieces:
                    v.tensor_copy(dst[:, o:o + w], pp[:, 0:w])
                P.inc(v.nop(), "qV", 1)
            on_V(fv)

        def col_to_bcast(col_ap_list, dst):
            for k, cap in enumerate(col_ap_list):
                pp = ps[k % 2]

                def fv(v, cap=cap):
                    P.wait(v, "qT")
                    P.inc(v.tensor_copy(sqd[:, 0:128], bcast_free(cap, 128)), "qV")
                on_V(fv)
                mk = P.mark("qV")

                def ft(t, pp=pp):
                    P.wait(t, "qV", mk)
                    P.inc(t.transpose(pp[:, 0:128], sqd[:, 0:128], c_id[:]), "qT")
                on_T(ft)
                mt = P.mark("qT")

                def fe(v, pp=pp, k=k):
                    P.wait(v, "qT", mt)
                    P.inc(v.tensor_copy(dst[:, k * 128:(k + 1) * 128], pp[:, 0:128]), "qV")
                on_V(fe)

        FCONSTS = (0.0, 1.0, 1e-5, 1e-6, 1e-8, 0.1, 0.5, -1.0, 1e-30)
        for fi, fv in enumerate(FCONSTS):
            nc.const_aps.register(F32, fv, c_fv[:, fi:fi + 1])

        def phaseA(s):
            for dst, src in ((c_id, ident), (c_u, u128), (c_1r, onesr),
                             (c_1c, onesc), (c_ir, iotar), (c_ic, iotac),
                             (c_msk, coremask), (c_fv, fvals)):
                P.dma(s.dma_start(out=dst[:], in_=src))
            for j in range(DC):
                P.dma(s.dma_start(out=s_wr[:, j * E:(j + 1) * E],
                                  in_=Wr[j * 128:(j + 1) * 128, :]))
        on_D(phaseA)

        def tile_stats(buf, ld):
            def fs(sc):
                P.wait(sc, "qD", ld)
                P.wait(sc, "qV")
                P.inc(sc.activation(sqd[:], buf[:], AF.Square), "qS")
            on_S(fs)
            sq_done = P.mark("qS")

            def fv(v):
                P.wait(v, "qD", ld)
                P.inc(v.reduce_sum(r_m[:], buf[:], axis=AX.X), "qV")
                P.wait(v, "qS", sq_done)
                P.inc(v.reduce_sum(r_sq[:], sqd[:], axis=AX.X), "qV")
                P.wait(v, "qV")
                v.tensor_scalar(r_v1[:], r_m[:], r_m[:, 0:1], None, ALU.mult)
                P.inc(v.tensor_scalar(r_m[:], r_m[:], 1.0 / D, None, ALU.mult), "qV")
                P.wait(v, "qV")
                P.inc(v.scalar_tensor_tensor(r_rsd[:], r_v1[:], -1.0 / D, r_sq[:],
                                             ALU.mult, ALU.add), "qV")
            on_V(fv)
            var_done = P.mark("qV")

            def fr(sc):
                P.wait(sc, "qV", var_done)
                P.inc(sc.activation(r_rsd[:], r_rsd[:], AF.Sqrt,
                                    bias=float(EPS_LN), scale=1.0 / D), "qS")
            on_S(fr)
            sq2 = P.mark("qS")

            def frr(v):
                P.wait(v, "qS", sq2)
                P.inc(v.reciprocal(r_rsd[:], r_rsd[:]), "qV")
                P.wait(v, "qV")
            on_V(frr)
            return P.mark("qV")

        def emit_tile(k, own):
            buf = xt[k % 2]
            src = xown[k * 128:(k + 1) * 128, :] if own else xall[k * 128:(k + 1) * 128, :]

            def fd(s):
                P.wait(s, "qV")
                P.wait(s, "qT")
                P.wait(s, "qS")
                P.dma(s.dma_start(out=buf[:], in_=src))
            on_D(fd)
            if SUB < 1:
                return
            rsd_done = tile_stats(buf, P.mark("qD"))
            if SUB < 2:
                return

            def fxn(v):
                P.wait(v, "qV", rsd_done)
                P.inc(v.tensor_scalar(sqd[:], buf[:], r_m[:, 0:1], r_rsd[:, 0:1],
                                      ALU.subtract, ALU.mult), "qV")
            on_V(fxn)
            xn_done = P.mark("qV")
            if SUB < 3:
                return

            if own:
                def fst(s):
                    P.wait(s, "qV", xn_done)
                    P.dma(s.dma_start(out=xn_tm_d[k * 128:(k + 1) * 128, :], in_=sqd[:]))
                on_D(fst)
            if SUB < 4:
                return

            srcs = [(sqd, True)] + ([(buf, False)] if own else [])
            for si, (sb_src, is_xn) in enumerate(srcs):
                for j in range(DC):
                    pt = ps[(si * DC + j) % 4]

                    def ft(t, j=j, pt=pt, sb_src=sb_src):
                        P.wait(t, "qV")
                        P.inc(t.transpose(pt[:, 0:128],
                                          sb_src[:, j * 128:(j + 1) * 128], c_id[:]), "qT")
                    on_T(ft)
                    mt = P.mark("qT")

                    def fe(v, j=j, pt=pt, k=k, own=own, is_xn=is_xn):
                        P.wait(v, "qT", mt)
                        if is_xn:
                            dst = S_n[j][:, k * 128:(k + 1) * 128] if own \
                                else S_s[0][:, j * 128:(j + 1) * 128]
                        else:
                            dst = S_x[j][:, k * 128:(k + 1) * 128]
                        P.inc(v.tensor_copy(dst, pt[:, 0:128]), "qV")
                    on_V(fe)
            trs_done = P.mark("qV")
            if SUB < 5:
                return
            if SUB < 6:
                return

            def fl(t, k=k, own=own):
                P.wait(t, "qV", trs_done)
                ins = None
                for j in range(DC):
                    src_c = S_n[j][:, k * 128:(k + 1) * 128] if own \
                        else S_s[0][:, j * 128:(j + 1) * 128]
                    ins = t.matmul(ps_big[:, 0:8], src_c, s_wr[:, j * E:(j + 1) * E],
                                   start=(j == 0), stop=(j == DC - 1))
                P.inc(ins, "qT", 1)
            on_T(fl)
            lg_done = P.mark("qT")

            def fc(v, k=k, own=own):
                P.wait(v, "qT", lg_done)
                dst = olg[:, k * 8:(k + 1) * 8] if own else glg[:, k * 8:(k + 1) * 8]
                P.inc(v.tensor_copy(dst, ps_big[:, 0:8]), "qV")
            on_V(fc)

        if stage >= 1:
            for k in range(TC):
                emit_tile(k, own=True)
        if stage >= 2:
            for k in range(GT):
                emit_tile(k, own=False)

        if dbg is not None and stage >= 2:
            def dbg_early(sd):
                P.wait(sd, "qV")
                P.wait(sd, "qT")
                P.dma(sd.dma_start(out=dbg[:, 5120:5120 + 1024], in_=glg[:, 0:1024]))
                P.dma(sd.dma_start(out=dbg[:, 6144:6144 + 1024], in_=S_s[0][:, 0:1024]))
            on_D(dbg_early)

        if stage >= 3:
            def argmax_fo(v, lg, fo, mxout, tmp, cbuf, nt):
                lg3 = lg[:, 0:nt * 8].rearrange("p (t e) -> p t e", e=8)
                fo3 = fo[:, 0:nt * 8].rearrange("p (t e) -> p t e", e=8)
                tm3 = tmp[:, 0:nt * 8].rearrange("p (t e) -> p t e", e=8)
                c3 = cbuf[:, 0:nt * 8].rearrange("p (t e) -> p t e", e=8)
                P.inc(v.reduce_max(mxout[:, 0:nt], lg3, axis=AX.X), "qV")
                P.wait(v, "qV")
                ma = apof(mxout)
                mxb = bass.AP(tensor=ma.tensor, offset=ma.offset,
                              ap=[ma.ap[0], [1, nt], [0, 8]])
                v.tensor_tensor(fo3, lg3, mxb, ALU.is_ge)
                v.tensor_copy(tm3[:, :, 0:1], fo3[:, :, 0:1])
                P.inc(v.tensor_tensor(tm3[:, :, 1:8], fo3[:, :, 1:8],
                                      fo3[:, :, 0:7], ALU.add), "qV")
                P.wait(v, "qV")
                v.tensor_copy(c3[:, :, 0:2], tm3[:, :, 0:2])
                P.inc(v.tensor_tensor(c3[:, :, 2:8], tm3[:, :, 2:8],
                                      tm3[:, :, 0:6], ALU.add), "qV")
                P.wait(v, "qV")
                v.tensor_copy(tm3[:, :, 0:4], c3[:, :, 0:4])
                P.inc(v.tensor_tensor(tm3[:, :, 4:8], c3[:, :, 4:8],
                                      c3[:, :, 0:4], ALU.add), "qV")
                P.wait(v, "qV")
                P.inc(v.tensor_scalar(tmp[:, 0:nt * 8], tmp[:, 0:nt * 8],
                                      1.0, None, ALU.is_equal), "qV")
                P.wait(v, "qV")
                v.tensor_mul(fo[:, 0:nt * 8], fo[:, 0:nt * 8], tmp[:, 0:nt * 8])

            def b3_global(v):
                P.wait(v, "qV")
                argmax_fo(v, glg, gfo, r_mx, r_t1, glg, GT)
                P.inc(v.tensor_copy(r_mx[:, 0:1], r_mx[:, 0:1]), "qV")
            on_V(b3_global)
            gfo_done = P.mark("qV")

            for cb in range(8):
                def ft(t, cb=cb):
                    P.wait(t, "qV")
                    P.inc(t.matmul(ps[0][:1, 0:128], c_1c[:, 0:1],
                                   gfo[:, cb * 128:(cb + 1) * 128],
                                   start=True, stop=True), "qT")
                on_T(ft)
                mt = P.mark("qT")

                def fv(v, cb=cb):
                    P.wait(v, "qT", mt)
                    P.inc(v.tensor_copy(rowx[0:1, 0:128], ps[0][0:1, 0:128]), "qV")
                    P.wait(v, "qV")
                    P.inc(v.tensor_add(rowx[0:1, 0:64], rowx[0:1, 0:64], rowx[0:1, 64:128]), "qV")
                    P.wait(v, "qV")
                    P.inc(v.tensor_add(rowx[0:1, 0:32], rowx[0:1, 0:32], rowx[0:1, 32:64]), "qV")
                    P.wait(v, "qV")
                    P.inc(v.tensor_add(rowx[0:1, 0:16], rowx[0:1, 0:16], rowx[0:1, 16:32]), "qV")
                    P.wait(v, "qV")
                    P.inc(v.tensor_add(rowx[0:1, 0:8], rowx[0:1, 0:8], rowx[0:1, 8:16]), "qV")
                    P.wait(v, "qV")
                    P.inc(v.tensor_copy(r_cnt[0:1, cb * 8:(cb + 1) * 8], rowx[0:1, 0:8]), "qV")
                on_V(fv)

            def b3_prefix(v):
                P.wait(v, "qV")
                P.inc(v.tensor_mul(r_cnt[0:1, :], r_cnt[0:1, :], c_msk[0:1, :]), "qV")
                P.wait(v, "qV")
                P.inc(v.tensor_add(r_cnt[0:1, 0:32], r_cnt[0:1, 0:32], r_cnt[0:1, 32:64]), "qV")
                P.wait(v, "qV")
                P.inc(v.tensor_add(r_cnt[0:1, 0:16], r_cnt[0:1, 0:16], r_cnt[0:1, 16:32]), "qV")
                P.wait(v, "qV")
                P.inc(v.tensor_add(r_cnt[0:1, 0:8], r_cnt[0:1, 0:8], r_cnt[0:1, 8:16]), "qV")
                P.wait(v, "qV")
                P.inc(v.tensor_copy(r_pfx[0:1, :], r_cnt[0:1, 0:8]), "qV")
            on_V(b3_prefix)
            row_bcast(r_pfx[0:1, 0:8], 8, pfx_bc)

            def b3_own1(v):
                argmax_fo(v, olg, ofo, r_mx, r_t1, r_t1[:, 512:1024], TC)
                lg3 = olg[:, 0:TC * 8].rearrange("p (t e) -> p t e", e=8)
                t13 = r_t1[:, 0:TC * 8].rearrange("p (t e) -> p t e", e=8)
                ra = apof(r_mx)
                mxb = bass.AP(tensor=ra.tensor, offset=ra.offset,
                              ap=[ra.ap[0], [1, TC], [0, 8]])
                P.inc(v.tensor_tensor(t13, lg3, mxb, ALU.subtract), "qV")
            on_V(b3_own1)
            own1_done = P.mark("qV")

            def b3_exp(sc):
                P.wait(sc, "qV", own1_done)
                P.inc(sc.activation(r_t1[:, 0:TC * 8], r_t1[:, 0:TC * 8], AF.Exp), "qS")
            on_S(b3_exp)
            exp_done = P.mark("qS")

            def b3_p(v):
                P.wait(v, "qS", exp_done)
                t13 = r_t1[:, 0:TC * 8].rearrange("p (t e) -> p t e", e=8)
                P.inc(v.reduce_sum(own_p[:, 0:TC], t13, axis=AX.X), "qV")
                P.wait(v, "qV")
                P.inc(v.reciprocal(own_p[:, 0:TC], own_p[:, 0:TC]), "qV")
            on_V(b3_p)

            def zcarry(g):
                g.memset(carry[:], 0.0)
                P.inc(g.nop(), "qG", 1)
            on_G(zcarry)
            carry_z = P.mark("qG")

            for k in range(TC):
                def ft(t, k=k):
                    P.wait(t, "qV")
                    if k == 0:
                        P.wait(t, "qG", carry_z)
                    t.matmul(ps[0][:, 0:8], c_1r[0:1, 0:128], carry[0:1, 0:8],
                             start=True, stop=False)
                    t.matmul(ps[0][:, 0:8], c_u[:, :], ofo[:, k * 8:(k + 1) * 8],
                             start=False, stop=True)
                    P.inc(t.matmul(ps[1][:1, 0:8], c_1c[:, 0:1], ofo[:, k * 8:(k + 1) * 8],
                                   start=True, stop=True), "qT")
                on_T(ft)
                mt = P.mark("qT")

                def fv(v, k=k):
                    P.wait(v, "qT", mt)
                    v.tensor_sub(own_lrk[:, k * 8:(k + 1) * 8], ps[0][:, 0:8],
                                 ofo[:, k * 8:(k + 1) * 8])
                    P.inc(v.tensor_add(carry[0:1, 0:8], carry[0:1, 0:8],
                                       ps[1][0:1, 0:8]), "qV")
                on_V(fv)

            def b3_slot2(v):
                P.wait(v, "qV")
                lrk = own_lrk[:, 0:TC * 8]
                lrk3 = lrk.rearrange("p (t e) -> p t e", e=8)
                t1 = r_t1[:, 0:TC * 8]
                t13 = t1.rearrange("p (t e) -> p t e", e=8)
                t2 = glg[:, 0:TC * 8]
                t23 = t2.rearrange("p (t e) -> p t e", e=8)
                v.tensor_scalar(eL_bc[:, 0:8], c_ir[:, 0:8], float(LCAP0), None, ALU.mult)
                pa = apof(pfx_bc)
                pfb = bass.AP(tensor=pa.tensor, offset=pa.offset,
                              ap=[pa.ap[0], [0, TC], [1, 8]])
                v.tensor_tensor(t13, lrk3, pfb, ALU.add)
                v.tensor_scalar(t1, t1, 2048.0, None, ALU.is_lt)
                v.tensor_mul(t1, t1, ofo[:, 0:TC * 8])
                v.tensor_scalar(t2, lrk, float(LCAP0), None, ALU.is_lt)
                v.tensor_mul(t1, t1, t2)
                P.inc(v.reduce_sum(own_sc[:, 0:TC],
                             t1.rearrange("p (t e) -> p t e", e=8), axis=AX.X), "qV")
                P.wait(v, "qV")
                ea = apof(eL_bc)
                eLb = bass.AP(tensor=ea.tensor, offset=ea.offset,
                              ap=[ea.ap[0], [0, TC], [1, 8]])
                v.tensor_tensor(t23, lrk3, eLb, ALU.add)
                v.tensor_mul(t2, t2, t1)
                P.inc(v.reduce_sum(own_slot[:, 0:TC],
                             t2.rearrange("p (t e) -> p t e", e=8), axis=AX.X), "qV")
                P.wait(v, "qV")
                P.inc(v.tensor_scalar(r_mx[:, 0:TC], own_sc[:, 0:TC], -BIGF, BIGF,
                                      ALU.mult, ALU.add), "qV")
                P.wait(v, "qV")
                v.tensor_add(own_slot[:, 0:TC], own_slot[:, 0:TC], r_mx[:, 0:TC])
                P.inc(v.tensor_mul(own_sc[:, 0:TC], own_sc[:, 0:TC], own_p[:, 0:TC]), "qV")
            on_V(b3_slot2)

        A_q, A_k, A_v, A_e, A_std = S_s[0], S_s[1], S_s[2], S_s[3], S_s[4]

        def emit_attn_group(g):
            wbufA, wbufB = xt[0], xt[1]

            def fw(sd):
                P.wait(sd, "qT")
                for j in range(DC):
                    P.dma(sd.dma_start(out=wbufA[:, j * 128:(j + 1) * 128],
                                       in_=Wq[j * 128:(j + 1) * 128,
                                              g * 128:(g + 1) * 128]))
                    P.dma(sd.dma_start(out=wbufB[:, j * 128:(j + 1) * 128],
                                       in_=Wk[j * 128:(j + 1) * 128,
                                              g * 128:(g + 1) * 128]))
            on_D(fw)
            wld = P.mark("qD")

            for Wsrc, dst, scale in ((0, A_q, 0.125), (1, A_k, None)):
                for q4 in range(4):
                    pp = ps[q4 % 2]

                    def fmm(t, q4=q4, pp=pp, Wsrc=Wsrc):
                        P.wait(t, "qV")
                        P.wait(t, "qD", wld)
                        wb = xt[Wsrc]
                        ins = None
                        for hh in range(2):
                            for j in range(DC):
                                ins = t.matmul(
                                    pp[hh * 64:(hh + 1) * 64, 0:512],
                                    wb[:, j * 128 + hh * 64:j * 128 + (hh + 1) * 64],
                                    S_n[j][:, q4 * 512:(q4 + 1) * 512],
                                    start=(j == 0), stop=(j == DC - 1),
                                    skip_group_check=True)
                        P.inc(ins, "qT", 1)
                    on_T(fmm)
                    mt = P.mark("qT")

                    def fev(v, q4=q4, pp=pp, dst=dst, scale=scale):
                        P.wait(v, "qT", mt)
                        if scale is None:
                            P.inc(v.tensor_copy(dst[:, q4 * 512:(q4 + 1) * 512],
                                                pp[:, 0:512]), "qV")
                        else:
                            P.inc(v.tensor_scalar(dst[:, q4 * 512:(q4 + 1) * 512],
                                                  pp[:, 0:512], scale, None,
                                                  ALU.mult), "qV")
                    on_V(fev)

            def fw2(sd):
                P.wait(sd, "qT")
                for j in range(DC):
                    P.dma(sd.dma_start(out=wbufA[:, j * 128:(j + 1) * 128],
                                       in_=Wv[j * 128:(j + 1) * 128,
                                              g * 128:(g + 1) * 128]))
                P.dma(sd.dma_start(out=wbufB[:, :],
                                   in_=Wo[g * 128:(g + 1) * 128, :]))
            on_D(fw2)
            wld2 = P.mark("qD")

            for kc in range(TC):
                pp = ps[kc % 2]

                def fvm(t, kc=kc, pp=pp):
                    P.wait(t, "qV")
                    P.wait(t, "qD", wld2)
                    ins = None
                    for j in range(DC):
                        ins = t.matmul(pp[:, 0:128],
                                       S_n[j][:, kc * 128:(kc + 1) * 128],
                                       wbufA[:, j * 128:(j + 1) * 128],
                                       start=(j == 0), stop=(j == DC - 1))
                    P.inc(ins, "qT", 1)
                on_T(fvm)
                mt = P.mark("qT")

                def fve(v, kc=kc, pp=pp):
                    P.wait(v, "qT", mt)
                    P.inc(v.tensor_copy(A_v[:, kc * 128:(kc + 1) * 128],
                                        pp[:, 0:128]), "qV")
                on_V(fve)

            def fcs(v):
                P.inc(v.reduce_max(colx[:, 0:1], A_q[:, :], axis=AX.X,
                                   apply_absolute_value=True), "qV")
                P.inc(v.reduce_max(colx[:, 1:2], A_k[:, :], axis=AX.X,
                                   apply_absolute_value=True), "qV")
            on_V(fcs)
            mcol = P.mark("qV")

            def fct(t):
                P.wait(t, "qV", mcol)
                P.inc(t.transpose(ps[2][0:2, 0:128], colx[:, 0:2], c_id[:]), "qT")
            on_T(fct)
            mtt = P.mark("qT")

            def fcm(v):
                P.wait(v, "qT", mtt)
                P.inc(v.tensor_copy(glg[0:2, 0:128], ps[2][0:2, 0:128]), "qV")
                P.wait(v, "qV")
                P.inc(v.reduce_max(glg[0:2, 128:129], glg[0:2, 0:64], axis=AX.X), "qV")
                P.inc(v.reduce_max(glg[0:2, 129:130], glg[0:2, 64:128], axis=AX.X), "qV")
                P.inc(v.tensor_copy(glg[0:1, 140:142], glg[0:1, 128:130]), "qV")
            on_V(fcm)
            mshuf0 = P.mark("qV")

            def fshuf(gg):
                P.wait(gg, "qV", mshuf0)
                P.dma(gg.dma_start(out=glg[0:1, 132:134], in_=glg[1:2, 128:130]), "qG")
            on_G(fshuf)
            mshuf = P.mark("qG")

            def fcm2(v):
                P.wait(v, "qG", mshuf)
                P.inc(v.tensor_tensor(glg[0:1, 130:132], glg[0:1, 140:142],
                                      glg[0:1, 132:134], ALU.mult), "qV")
                P.wait(v, "qV")
                P.inc(v.tensor_scalar(glg[0:1, 130:132], glg[0:1, 130:132],
                                      -64.0, None, ALU.mult), "qV")
            on_V(fcm)
            mneg = P.mark("qV")

            def fcb(t):
                P.wait(t, "qV", mneg)
                P.inc(t.matmul(ps[2][:, 0:2], c_1r[0:1, 0:128], glg[0:1, 130:132],
                               start=True, stop=True), "qT")
            on_T(fcb)
            mbb = P.mark("qT")

            def fcbe(v):
                P.wait(v, "qT", mbb)
                P.inc(v.tensor_copy(colx[:, 2:4], ps[2][:, 0:2]), "qV")
                P.wait(v, "qV")
            on_V(fcbe)
            prep_done = P.mark("qV")

            def den_ap(hh, q4):
                seg = hh * 2 + q4 // 2
                cc = (q4 % 2) * 512
                t = (glg, gfo, r_t1, denb)[seg]
                return t[0:1, cc:cc + 512]

            for hh in range(2):
                for q4 in range(4):
                    for kc in range(TC):
                        pp = ps[kc % 2]

                        def fsc(t, hh=hh, q4=q4, kc=kc, pp=pp):
                            P.wait(t, "qS")
                            if hh == 0 and q4 == 0 and kc == 0:
                                P.wait(t, "qV", prep_done)
                            P.inc(t.matmul(
                                pp[:, 0:512],
                                A_k[hh * 64:(hh + 1) * 64, kc * 128:(kc + 1) * 128],
                                A_q[hh * 64:(hh + 1) * 64, q4 * 512:(q4 + 1) * 512],
                                start=True, stop=True), "qT")
                        on_T(fsc)
                        msc = P.mark("qT")

                        def fex(sc, hh=hh, pp=pp, kc=kc):
                            P.wait(sc, "qT", msc)
                            if kc > 1:
                                P.wait(sc, "qT")
                            P.inc(sc.activation(A_e[:, (kc % 2) * 512:(kc % 2 + 1) * 512],
                                                pp[:, 0:512], AF.Exp,
                                                bias=colx[:, 2 + hh:3 + hh]), "qS")
                        on_S(fex)
                        mex = P.mark("qS")

                        def fav(t, hh=hh, q4=q4, kc=kc):
                            P.wait(t, "qS", mex)
                            esub = A_e[:, (kc % 2) * 512:(kc % 2 + 1) * 512]
                            t.matmul(ps_big[hh * 64:(hh + 1) * 64,
                                            q4 * 512:(q4 + 1) * 512],
                                     A_v[:, kc * 128 + hh * 64:kc * 128 + (hh + 1) * 64],
                                     esub,
                                     start=(kc == 0), stop=(kc == TC - 1),
                                     skip_group_check=True)
                            P.inc(t.matmul(ps[2 + hh][0:1, 0:512], c_1c[:, 0:1], esub,
                                           start=(kc == 0), stop=(kc == TC - 1),
                                           skip_group_check=True), "qT")
                        on_T(fav)
                    mdn = P.mark("qT")

                    def fdn(v, hh=hh, q4=q4):
                        P.wait(v, "qT", mdn)
                        P.inc(v.tensor_copy(den_ap(hh, q4),
                                            ps[2 + hh][0:1, 0:512]), "qV")
                    on_V(fdn)

            def fdr(v):
                P.wait(v, "qV")
                v.reciprocal(glg[0:1, 0:1024], glg[0:1, 0:1024])
                v.reciprocal(gfo[0:1, 0:1024], gfo[0:1, 0:1024])
                v.reciprocal(r_t1[0:1, 0:1024], r_t1[0:1, 0:1024])
                P.inc(v.reciprocal(denb[0:1, 0:1024], denb[0:1, 0:1024]), "qV")
                P.wait(v, "qV")
            on_V(fdr)
            mrec = P.mark("qV")

            for hh in range(2):
                for q4 in range(4):
                    def fbc(t, hh=hh, q4=q4):
                        P.wait(t, "qV", mrec)
                        P.wait(t, "qV")
                        P.inc(t.matmul(ps[0][hh * 64:(hh + 1) * 64, 0:512],
                                       c_1r[0:1, 0:64],
                                       den_ap(hh, q4),
                                       start=True, stop=True), "qT")
                    on_T(fbc)
                    mbc = P.mark("qT")

                    def fst(v, hh=hh, q4=q4):
                        P.wait(v, "qT", mbc)
                        P.inc(v.tensor_copy(A_e[hh * 64:(hh + 1) * 64, 0:512],
                                            ps[0][hh * 64:(hh + 1) * 64, 0:512]), "qV")
                        P.wait(v, "qV")
                        P.inc(v.tensor_tensor(
                            A_std[hh * 64:(hh + 1) * 64, q4 * 512:(q4 + 1) * 512],
                            ps_big[hh * 64:(hh + 1) * 64, q4 * 512:(q4 + 1) * 512],
                            A_e[hh * 64:(hh + 1) * 64, 0:512], ALU.mult), "qV")
                    on_V(fst)
            std_done = P.mark("qV")

            for do in range(DC):
                for q4 in range(4):
                    pp = ps[1 + (do * 4 + q4) % 2]

                    def fwo(t, do=do, q4=q4, pp=pp):
                        P.wait(t, "qV")
                        P.inc(t.matmul(pp[:, 0:512],
                                       wbufB[:, do * 128:(do + 1) * 128],
                                       A_std[:, q4 * 512:(q4 + 1) * 512],
                                       start=True, stop=True), "qT")
                    on_T(fwo)
                    mwo = P.mark("qT")

                    def fwe(v, do=do, q4=q4, pp=pp):
                        P.wait(v, "qT", mwo)
                        P.inc(v.scalar_tensor_tensor(
                            S_x[do][:, q4 * 512:(q4 + 1) * 512], pp[:, 0:512],
                            float(STD_RATIO), S_x[do][:, q4 * 512:(q4 + 1) * 512],
                            ALU.mult, ALU.add), "qV")
                    on_V(fwe)

        if stage >= 4:
            for g in range(H0 // 2):
                emit_attn_group(g)

        def emit_splat():
            wbufA, wbufB = xt[0], xt[1]
            D_sm = A_k
            D_tm = A_v

            def f0(sd):
                P.wait(sd, "qT")
                P.wait(sd, "qV")
                for j in range(DC):
                    P.dma(sd.dma_start(out=wbufA[:, j * 64:(j + 1) * 64],
                                       in_=spT[j * 128:(j + 1) * 128, :]))
                P.dma(sd.dma_start(out=denb[0:1, 128:192], in_=ssc_r))
                P.dma(sd.dma_start(out=denb[0:1, 192:256], in_=simp_r))
                P.dma(sd.dma_start(out=mcol[:, 0:DC], in_=bg_c))
                P.dma(sd.dma_start(out=mcol[:, 8:12], in_=be_c))
            on_D(f0)
            ld0 = P.mark("qD")

            for j in range(DC):
                def fsq(sc, j=j):
                    P.wait(sc, "qT")
                    P.inc(sc.activation(A_q[:, :], S_n[j][:, :], AF.Square), "qS")
                on_S(fsq)
                msq = P.mark("qS")

                def fmm(t, j=j):
                    P.wait(t, "qS", msq)
                    if j == 0:
                        P.wait(t, "qV")
                    ins = None
                    for q4 in range(4):
                        ins = t.matmul(ps_big[0:1, q4 * 512:(q4 + 1) * 512],
                                       c_1c[:, 0:1], A_q[:, q4 * 512:(q4 + 1) * 512],
                                       start=(j == 0), stop=(j == DC - 1),
                                       skip_group_check=True)
                    P.inc(ins, "qT", 1)
                on_T(fmm)
            mx2 = P.mark("qT")

            def fx2e(v):
                P.wait(v, "qT", mx2)
                v.tensor_copy(glg[0:1, 0:1024], ps_big[0:1, 0:1024])
                P.inc(v.tensor_copy(gfo[0:1, 0:1024], ps_big[0:1, 1024:2048]), "qV")
            on_V(fx2e)

            def fs2a(sc):
                P.wait(sc, "qD", ld0)
                P.wait(sc, "qT", mx2)
                P.inc(sc.activation(sqd[:, 0:512], wbufA[:, 0:512], AF.Square), "qS")
            on_S(fs2a)
            ms2 = P.mark("qS")

            def fs2m(t):
                P.wait(t, "qS", ms2)
                P.wait(t, "qV")
                ins = None
                for j in range(DC):
                    ins = t.matmul(ps[2][0:1, 0:64], c_1c[:, 0:1],
                                   sqd[:, j * 64:(j + 1) * 64],
                                   start=(j == 0), stop=(j == DC - 1))
                P.inc(ins, "qT", 1)
            on_T(fs2m)
            ms2m = P.mark("qT")

            def fs2e(v):
                P.wait(v, "qT", ms2m)
                P.inc(v.tensor_copy(denb[0:1, 0:64], ps[2][0:1, 0:64]), "qV")
            on_V(fs2e)
            ms2e = P.mark("qV")

            def fct(t):
                P.wait(t, "qV", ms2e)
                t.transpose(ps[3][0:64, 0:1], denb[0:1, 0:64], c_id[0:1, 0:1])
                t.transpose(ps[3][0:64, 1:2], denb[0:1, 128:192], c_id[0:1, 0:1])
                P.inc(t.transpose(ps[3][0:64, 2:3], denb[0:1, 192:256],
                                  c_id[0:1, 0:1]), "qT")
            on_T(fct)
            mct = P.mark("qT")

            def fcc(v):
                P.wait(v, "qT", mct)
                P.inc(v.tensor_copy(sqd[:, 960:963], ps[3][:, 0:3]), "qV")
                P.wait(v, "qV")
                P.inc(v.tensor_tensor(sqd[0:64, 961:962], sqd[0:64, 961:962],
                                      sqd[0:64, 961:962], ALU.mult), "qV")
                P.wait(v, "qV")
                P.inc(v.tensor_scalar(sqd[0:64, 961:962], sqd[0:64, 961:962],
                                      1e-12, None, ALU.max), "qV")
                P.wait(v, "qV")
                P.inc(v.reciprocal(sqd[0:64, 961:962], sqd[0:64, 961:962]), "qV")
                P.wait(v, "qV")
                P.inc(v.tensor_scalar(sqd[0:64, 961:962], sqd[0:64, 961:962],
                                      -0.5, None, ALU.mult), "qV")
                P.wait(v, "qV")
            on_V(fcc)
            mcc0 = P.mark("qV")

            def fcabs(sc):
                P.wait(sc, "qV", mcc0)
                P.inc(sc.activation(sqd[0:64, 962:963], sqd[0:64, 962:963],
                                    AF.Abs), "qS")
            on_S(fcabs)

            def fccw(v):
                P.wait(v, "qS")
            on_V(fccw)
            mcc = P.mark("qV")

            for q4 in range(4):
                def fdm(t, q4=q4):
                    P.wait(t, "qV")
                    ins = None
                    for j in range(DC):
                        ins = t.matmul(ps[2][0:64, 0:512], wbufA[:, j * 64:(j + 1) * 64],
                                       S_n[j][:, q4 * 512:(q4 + 1) * 512],
                                       start=(j == 0), stop=(j == DC - 1))
                    x2seg = glg[0:1, q4 * 512:(q4 + 1) * 512] if q4 < 2 \
                        else gfo[0:1, (q4 - 2) * 512:(q4 - 1) * 512]
                    P.inc(t.matmul(ps[3][0:64, 0:512], c_1r[0:1, 0:64], x2seg,
                                   start=True, stop=True), "qT")
                on_T(fdm)
                mdm = P.mark("qT")

                def fie(v, q4=q4):
                    P.wait(v, "qT", mdm)
                    P.inc(v.tensor_copy(A_q[64:128, 0:512], ps[3][0:64, 0:512]), "qV")
                    P.wait(v, "qV")
                    P.inc(v.scalar_tensor_tensor(A_q[0:64, 0:512], ps[2][0:64, 0:512],
                                                 -2.0, A_q[64:128, 0:512],
                                                 ALU.mult, ALU.add), "qV")
                    P.wait(v, "qV")
                    P.inc(v.tensor_scalar(A_q[0:64, 0:512], A_q[0:64, 0:512],
                                          sqd[0:64, 960:961], 0.0, ALU.add, ALU.max), "qV")
                on_V(fie)
                mie = P.mark("qV")

                def fex(sc, q4=q4):
                    P.wait(sc, "qV", mie)
                    P.inc(sc.activation(D_sm[0:64, q4 * 512:(q4 + 1) * 512],
                                        A_q[0:64, 0:512], AF.Exp,
                                        scale=sqd[0:64, 961:962]), "qS")
                on_S(fex)
                mexq = P.mark("qS")

                def fim(v, q4=q4):
                    P.wait(v, "qS", mexq)
                    P.inc(v.tensor_scalar(D_sm[0:64, q4 * 512:(q4 + 1) * 512],
                                          D_sm[0:64, q4 * 512:(q4 + 1) * 512],
                                          sqd[0:64, 962:963], None, ALU.mult), "qV")
                on_V(fim)

            def fb1(t):
                P.wait(t, "qV")
                t.matmul(ps[2][:, 0:64], c_1r[0:1, 0:128], denb[0:1, 0:64],
                         start=True, stop=True)
                P.inc(t.matmul(ps[3][:, 0:64], c_1r[0:1, 0:128], denb[0:1, 192:256],
                               start=True, stop=True), "qT")
            on_T(fb1)
            mb1 = P.mark("qT")

            def fb1e(v):
                P.wait(v, "qT", mb1)
                v.tensor_copy(A_q[:, 576:640], ps[2][:, 0:64])
                P.inc(v.tensor_copy(A_q[:, 704:768], ps[3][:, 0:64]), "qV")
                P.wait(v, "qV")
            on_V(fb1e)
            mimp = P.mark("qV")

            def fb1abs(sc):
                P.wait(sc, "qV", mimp)
                P.inc(sc.activation(A_q[:, 704:768], A_q[:, 704:768], AF.Abs), "qS")
            on_S(fb1abs)

            def fb1w(v):
                P.wait(v, "qS")
            on_V(fb1w)
            def fnh(v):
                P.inc(v.tensor_tensor(denb[0:1, 256:320], denb[0:1, 128:192],
                                      denb[0:1, 128:192], ALU.mult), "qV")
                P.wait(v, "qV")
                P.inc(v.tensor_scalar(denb[0:1, 256:320], denb[0:1, 256:320],
                                      1e-12, None, ALU.max), "qV")
                P.wait(v, "qV")
                P.inc(v.reciprocal(denb[0:1, 256:320], denb[0:1, 256:320]), "qV")
                P.wait(v, "qV")
                P.inc(v.tensor_scalar(denb[0:1, 256:320], denb[0:1, 256:320],
                                      -0.5, None, ALU.mult), "qV")
                P.wait(v, "qV")
            on_V(fnh)
            mnh = P.mark("qV")

            def fb2(t):
                P.wait(t, "qV", mnh)
                P.inc(t.matmul(ps[2][:, 64:128], c_1r[0:1, 0:128], denb[0:1, 256:320],
                               start=True, stop=True), "qT")
            on_T(fb2)
            mb2 = P.mark("qT")

            def fb2e(v):
                P.wait(v, "qT", mb2)
                P.inc(v.tensor_copy(A_q[:, 640:704], ps[2][:, 64:128]), "qV")
                P.wait(v, "qV")
            on_V(fb2e)

            for kc in range(TC):
                seg = glg if kc < 8 else gfo
                off = (kc % 8) * 128
                def fxb(t, kc=kc, seg=seg, off=off):
                    P.wait(t, "qV")
                    P.inc(t.matmul(ps[2][:, 128:256], c_1r[0:1, 0:128],
                                   seg[0:1, off:off + 128], start=True, stop=True), "qT")
                on_T(fxb)
                mxb1 = P.mark("qT")

                def fxc(v, kc=kc):
                    P.wait(v, "qT", mxb1)
                    P.inc(v.tensor_copy(sqd[:, 963:963 + 128][:, 0:128] if False else A_q[:, 832:960],
                                        ps[2][:, 128:256]), "qV")
                on_V(fxc)
                mxc = P.mark("qV")

                def fxt(t, kc=kc):
                    P.wait(t, "qV", mxc)
                    P.inc(t.transpose(ps[3][:, 128:256], A_q[:, 832:960], c_id[:]), "qT")
                on_T(fxt)
                mxt = P.mark("qT")

                def fxe(v, kc=kc):
                    P.wait(v, "qT", mxt)
                    P.inc(v.tensor_copy(mcol[:, 16 + kc:17 + kc], ps[3][:, 128:129]), "qV")
                on_V(fxe)

            for kc in range(TC):
                def ftm(t, kc=kc):
                    P.wait(t, "qV")
                    ins = None
                    for j in range(DC):
                        ins = t.matmul(ps[2][:, 0:64], S_n[j][:, kc * 128:(kc + 1) * 128],
                                       wbufA[:, j * 64:(j + 1) * 64],
                                       start=(j == 0), stop=(j == DC - 1))
                    P.inc(ins, "qT", 1)
                on_T(ftm)
                mtm = P.mark("qT")

                def fti(v, kc=kc):
                    P.wait(v, "qT", mtm)
                    P.inc(v.tensor_scalar(A_q[:, 0:64], ps[2][:, 0:64],
                                          -2.0, mcol[:, 16 + kc:17 + kc],
                                          ALU.mult, ALU.add), "qV")
                    P.wait(v, "qV")
                    v.tensor_tensor(A_q[:, 0:64], A_q[:, 0:64], A_q[:, 576:640], ALU.add)
                    v.tensor_scalar(A_q[:, 0:64], A_q[:, 0:64], 0.0, None, ALU.max)
                    P.inc(v.tensor_tensor(A_q[:, 0:64], A_q[:, 0:64],
                                          A_q[:, 640:704], ALU.mult), "qV")
                on_V(fti)
                mti = P.mark("qV")

                def fte(sc, kc=kc):
                    P.wait(sc, "qV", mti)
                    P.inc(sc.activation(A_q[:, 64:128], A_q[:, 0:64], AF.Exp), "qS")
                on_S(fte)
                mte = P.mark("qS")

                def ftf(v, kc=kc):
                    P.wait(v, "qS", mte)
                    P.inc(v.tensor_tensor(D_tm[:, kc * 64:(kc + 1) * 64],
                                          A_q[:, 64:128], A_q[:, 704:768], ALU.mult), "qV")
                on_V(ftf)

            def ftt(t):
                P.wait(t, "qV")
                ins = None
                for kc in range(TC):
                    ins = t.matmul(ps[3][0:1, 0:64], c_1c[:, 0:1],
                                   D_tm[:, kc * 64:(kc + 1) * 64],
                                   start=(kc == 0), stop=(kc == TC - 1))
                P.inc(ins, "qT", 1)
            on_T(ftt)
            mtt2 = P.mark("qT")

            def ftr(v):
                P.wait(v, "qT", mtt2)
                P.inc(v.tensor_scalar(denb[0:1, 64:128], ps[3][0:1, 0:64],
                                      1e-8, None, ALU.max), "qV")
                P.wait(v, "qV")
                P.inc(v.reciprocal(denb[0:1, 64:128], denb[0:1, 64:128]), "qV")
                P.wait(v, "qV")
            on_V(ftr)
            mrt = P.mark("qV")

            gslot = [ps_big[:, 0:64], ps_big[:, 512:576], ps_big[:, 1024:1088],
                     ps_big[:, 1536:1600], ps[0][:, 0:64], ps[1][:, 0:64],
                     ps[2][:, 0:64], ps[3][:, 0:64]]
            for kc in range(TC):
                def fgl(sd, kc=kc):
                    P.wait(sd, "qT")
                    P.dma(sd.dma_start(out=wbufB[:, :],
                                       in_=xn_tm_d[kc * 128:(kc + 1) * 128, :]))
                on_D(fgl)
                mgl = P.mark("qD")

                def fgm(t, kc=kc):
                    P.wait(t, "qD", mgl)
                    if kc == 0:
                        P.wait(t, "qV")
                    ins = None
                    for ds in range(DC):
                        ins = t.matmul(gslot[ds], wbufB[:, ds * 128:(ds + 1) * 128],
                                       D_tm[:, kc * 64:(kc + 1) * 64],
                                       start=(kc == 0), stop=(kc == TC - 1),
                                       skip_group_check=True)
                    P.inc(ins, "qT", 1)
                on_T(fgm)
            mgm = P.mark("qT")

            def fge(v):
                P.wait(v, "qT", mgm)
                for ds in range(DC):
                    v.tensor_copy(A_std[:, ds * 64:(ds + 1) * 64], gslot[ds])
                P.inc(v.tensor_copy(A_q[:, 0:64], A_q[:, 0:64]), "qV")
            on_V(fge)
            mge = P.mark("qV")

            def fgb(t):
                P.wait(t, "qV", mge)
                P.inc(t.matmul(ps[2][:, 64:128], c_1r[0:1, 0:128], denb[0:1, 64:128],
                               start=True, stop=True), "qT")
            on_T(fgb)
            mgb = P.mark("qT")

            def fgs(v):
                P.wait(v, "qT", mgb)
                P.inc(v.tensor_copy(A_q[:, 768:832], ps[2][:, 64:128]), "qV")
                P.wait(v, "qV")
                rb = A_q[:, 768:832]
                ra = apof(rb)
                rb3 = bass.AP(tensor=ra.tensor, offset=ra.offset,
                              ap=[ra.ap[0], [0, DC], [1, 64]])
                a3 = A_std[:, 0:512].rearrange("p (d s) -> p d s", s=64)
                P.inc(v.tensor_tensor(a3, a3, rb3, ALU.mult), "qV")
                P.wait(v, "qV")
            on_V(fgs)

            for j in range(DC):
                def fel(sd, j=j):
                    P.wait(sd, "qT")
                    P.dma(sd.dma_start(out=wbufB[:, 0:512],
                                       in_=We[j * 128:(j + 1) * 128, :]))
                on_D(fel)
                mel = P.mark("qD")

                def fem(t, j=j):
                    P.wait(t, "qD", mel)
                    if j == 0:
                        P.wait(t, "qV")
                    ins = None
                    for dbs in range(4):
                        ins = t.matmul(ps[dbs][:, 64:128],
                                       wbufB[:, dbs * 128:(dbs + 1) * 128],
                                       A_std[:, j * 64:(j + 1) * 64],
                                       start=(j == 0), stop=(j == DC - 1),
                                       skip_group_check=True)
                    P.inc(ins, "qT", 1)
                on_T(fem)
            mem = P.mark("qT")

            for dbs in range(4):
                def fee(sc, dbs=dbs):
                    P.wait(sc, "qT", mem)
                    P.inc(sc.activation(sqd[:, dbs * 64:(dbs + 1) * 64],
                                        ps[dbs][:, 64:128], AF.Relu,
                                        bias=mcol[:, 8 + dbs:9 + dbs]), "qS")
                on_S(fee)
            menc = P.mark("qS")

            def fbt(sd):
                P.wait(sd, "qV")
                for ec in range(4):
                    P.dma(sd.dma_start(out=A_q[:, 512 + ec * 64:512 + (ec + 1) * 64],
                                       in_=btT[ec * 128:(ec + 1) * 128, :]))
            on_D(fbt)
            mbt = P.mark("qD")

            for sp in range(S0):
                def ftl(sd, sp=sp):
                    P.wait(sd, "qT")
                    for dbs in range(2):
                        P.dma(sd.dma_start(
                            out=wbufB[:, dbs * 512:(dbs + 1) * 512],
                            in_=Wt[sp, dbs * 128:(dbs + 1) * 128, :]))
                on_D(ftl)
                mt1 = P.mark("qD")

                def ftm1(t, sp=sp):
                    P.wait(t, "qD", mt1)
                    P.wait(t, "qS", menc)
                    if sp == 0:
                        P.wait(t, "qD", mbt)
                    P.wait(t, "qV")
                    ins = None
                    for dbs in range(2):
                        for es in range(4):
                            ins = t.matmul(ps[es][:, 192:193],
                                           wbufB[:, dbs * 512 + es * 128:
                                                 dbs * 512 + (es + 1) * 128],
                                           sqd[:, dbs * 64 + sp:dbs * 64 + sp + 1]
                                           if False else
                                           sqd[:, dbs * 64 + sp:dbs * 64 + sp + 1],
                                           start=(dbs == 0), stop=False,
                                           skip_group_check=True)
                    P.inc(ins, "qT", 1)
                on_T(ftm1)
                mt2 = P.mark("qT")

                def ftl2(sd, sp=sp):
                    P.wait(sd, "qT", mt2)
                    for dbs in range(2, 4):
                        P.dma(sd.dma_start(
                            out=wbufB[:, (dbs - 2) * 512:(dbs - 1) * 512],
                            in_=Wt[sp, dbs * 128:(dbs + 1) * 128, :]))
                on_D(ftl2)
                mt3 = P.mark("qD")

                def ftm2(t, sp=sp):
                    P.wait(t, "qD", mt3)
                    ins = None
                    for dbs in range(2, 4):
                        for es in range(4):
                            ins = t.matmul(ps[es][:, 192:193],
                                           wbufB[:, (dbs - 2) * 512 + es * 128:
                                                 (dbs - 2) * 512 + (es + 1) * 128],
                                           sqd[:, dbs * 64 + sp:dbs * 64 + sp + 1],
                                           start=False, stop=(dbs == 3),
                                           skip_group_check=True)
                    P.inc(ins, "qT", 1)
                on_T(ftm2)
                mt4 = P.mark("qT")

                def fte2(v, sp=sp):
                    P.wait(v, "qT", mt4)
                    ins = None
                    for es in range(4):
                        ins = v.tensor_tensor(A_q[:, 768 + es * 64 + sp:769 + es * 64 + sp]
                                              if False else
                                              A_q[:, 768 + es * 64 + sp - (sp % 64) + (sp % 64):769 + es * 64 + sp - (sp % 64) + (sp % 64)],
                                              ps[es][:, 192:193],
                                              A_q[:, 512 + es * 64 + sp:513 + es * 64 + sp]
                                              if False else
                                              A_q[:, 512 + es * 64 + sp - (sp % 64) + (sp % 64):513 + es * 64 + sp - (sp % 64) + (sp % 64)],
                                              ALU.add)
                    P.inc(ins, "qV", 1)
                on_V(fte2)


            def fbd(sd):
                P.dma(sd.dma_start(out=glg[0:1, 0:1024], in_=bd_r))
            on_D(fbd)
            mbd = P.mark("qD")

            for ec in range(4):
                def fdl(sd, ec=ec):
                    P.wait(sd, "qT")
                    P.dma(sd.dma_start(out=wbufB[:, :],
                                       in_=Wd[ec * 128:(ec + 1) * 128, :]))
                on_D(fdl)
                mdl = P.mark("qD")

                def fdm2(t, ec=ec):
                    P.wait(t, "qD", mdl)
                    P.wait(t, "qV")
                    if ec == 0:
                        P.wait(t, "qD", mbd)
                        for half in range(2):
                            t.matmul(ps_big[0:64, half * 512:(half + 1) * 512],
                                     c_1r[0:1, 0:64],
                                     glg[0:1, half * 512:(half + 1) * 512],
                                     start=True, stop=False, skip_group_check=True)
                    ins = None
                    for half in range(2):
                        ins = t.matmul(ps_big[0:64, half * 512:(half + 1) * 512],
                                       A_q[:, 768 + ec * 64:768 + (ec + 1) * 64],
                                       wbufB[:, half * 512:(half + 1) * 512],
                                       start=False, stop=(ec == 3),
                                       skip_group_check=True)
                    P.inc(ins, "qT", 1)
                on_T(fdm2)
            mdm2 = P.mark("qT")

            def fde(v):
                P.wait(v, "qT", mdm2)
                P.inc(v.tensor_copy(A_std[0:64, 0:1024], ps_big[0:64, 0:1024]), "qV")
            on_V(fde)

            for j in range(DC):
                def ffl(sd, j=j):
                    P.wait(sd, "qT")
                    for jp in range(DC):
                        P.dma(sd.dma_start(out=wbufA[:, jp * 128:(jp + 1) * 128],
                                           in_=Wg[jp * 128:(jp + 1) * 128,
                                                  j * 128:(j + 1) * 128]))
                    P.dma(sd.dma_start(out=wbufB[:, :],
                                       in_=Wo[j * 128:(j + 1) * 128, :]))
                on_D(ffl)
                mfl = P.mark("qD")

                for q4 in range(4):
                    def fgm2(t, j=j, q4=q4):
                        P.wait(t, "qD", mfl)
                        P.wait(t, "qS")
                        P.wait(t, "qV")
                        ins = None
                        for jp in range(DC):
                            ins = t.matmul(ps[2][:, 0:512],
                                           wbufA[:, jp * 128:(jp + 1) * 128],
                                           S_n[jp][:, q4 * 512:(q4 + 1) * 512],
                                           start=(jp == 0), stop=(jp == DC - 1))
                        t.matmul(ps[3][:, 0:512],
                                 A_std[0:64, j * 128:(j + 1) * 128],
                                 D_sm[0:64, q4 * 512:(q4 + 1) * 512],
                                 start=True, stop=True)
                        P.inc(ins, "qT", 1)
                    on_T(fgm2)
                    mgm2 = P.mark("qT")

                    def fsg(sc, j=j, q4=q4):
                        P.wait(sc, "qT", mgm2)
                        P.inc(sc.activation(A_v[:, 0:512], ps[2][:, 0:512],
                                            AF.Sigmoid, bias=mcol[:, j:j + 1]), "qS")
                    on_S(fsg)
                    msg = P.mark("qS")

                    def ffm(v, j=j, q4=q4):
                        P.wait(v, "qS", msg)
                        P.inc(v.tensor_tensor(A_v[:, 512:1024], ps[3][:, 0:512],
                                              A_v[:, 0:512], ALU.mult), "qV")
                    on_V(ffm)
                    mfm = P.mark("qV")

                    for do in range(DC):
                        pp = ps[(do % 2)]

                        def fwo2(t, j=j, q4=q4, do=do, pp=pp):
                            P.wait(t, "qV")
                            P.inc(t.matmul(pp[:, 256:256 + 512][:, 0:512]
                                           if False else pp[:, 0:512],
                                           wbufB[:, do * 128:(do + 1) * 128],
                                           A_v[:, 512:1024],
                                           start=True, stop=True), "qT")
                        on_T(fwo2)
                        mwo2 = P.mark("qT")

                        def fwe2(v, j=j, q4=q4, do=do, pp=pp):
                            P.wait(v, "qT", mwo2)
                            P.inc(v.scalar_tensor_tensor(
                                S_x[do][:, q4 * 512:(q4 + 1) * 512], pp[:, 0:512],
                                float(SPLAT_RATIO),
                                S_x[do][:, q4 * 512:(q4 + 1) * 512],
                                ALU.mult, ALU.add), "qV")
                        on_V(fwe2)

        if stage >= 5:
            emit_splat()

        def emit_l0ffn():
            wbufA, wbufB = xt[0], xt[1]
            gslot = [ps_big[:, 0:384], ps_big[:, 512:896], ps_big[:, 1024:1408],
                     ps_big[:, 1536:1920], ps[0][:, 0:384], ps[1][:, 0:384],
                     ps[2][:, 0:384], ps[3][:, 0:384]]

            def xg_ap(j):
                return A_k[:, j * 384:(j + 1) * 384] if j < 5 \
                    else A_v[:, (j - 5) * 384:(j - 4) * 384]

            def h_ap(c):
                return S_n[c // 5][:, (c % 5) * 384:(c % 5 + 1) * 384]

            for kc in range(TC):
                def fsb(v, kc=kc):
                    P.wait(v, "qT")
                    P.inc(v.tensor_copy(A_q[:, 0:128],
                                        bcast_free(own_slot[:, kc:kc + 1], 128)), "qV")
                on_V(fsb)
                msb = P.mark("qV")

                def fst2(t, kc=kc):
                    P.wait(t, "qV", msb)
                    P.inc(t.transpose(ps[2][:, 0:128], A_q[:, 0:128], c_id[:]), "qT")
                on_T(fst2)
                mst = P.mark("qT")

                def fse(v, kc=kc):
                    P.wait(v, "qT", mst)
                    P.inc(v.tensor_copy(A_std[:, kc * 128:(kc + 1) * 128],
                                        ps[2][:, 0:128]), "qV")
                on_V(fse)

            for e in range(E):
                for kc in range(TC):
                    def fgl(sd, kc=kc, e=e):
                        P.wait(sd, "qT")
                        P.dma(sd.dma_start(out=wbufB[:, :],
                                           in_=xn_tm_d[kc * 128:(kc + 1) * 128, :]))
                    on_D(fgl)
                    mgl = P.mark("qD")

                    sl_pts = A_q[:, (kc % 2) * 384:(kc % 2) * 384 + 384]

                    def fpt(v, kc=kc, e=e, sl_pts=sl_pts):
                        P.wait(v, "qT")
                        P.inc(v.tensor_scalar(sl_pts, c_ir[:, 0:384],
                                              own_slot[:, kc:kc + 1],
                                              float(-e * LCAP0), ALU.subtract,
                                              ALU.is_equal), "qV")
                        P.wait(v, "qV")
                        P.inc(v.tensor_scalar(sl_pts, sl_pts,
                                              own_sc[:, kc:kc + 1], None,
                                              ALU.mult), "qV")
                    on_V(fpt)
                    mpt = P.mark("qV")

                    def fgm(t, kc=kc, e=e, sl_pts=sl_pts):
                        P.wait(t, "qV", mpt)
                        P.wait(t, "qD", mgl)
                        ins = None
                        for ds in range(DC):
                            ins = t.matmul(gslot[ds],
                                           wbufB[:, ds * 128:(ds + 1) * 128],
                                           sl_pts,
                                           start=(kc == 0), stop=(kc == TC - 1),
                                           skip_group_check=True)
                        P.inc(ins, "qT", 1)
                    on_T(fgm)
                mgm = P.mark("qT")

                def fge(v, e=e):
                    P.wait(v, "qT", mgm)
                    ins = None
                    for ds in range(DC):
                        ins = v.tensor_copy(xg_ap(ds), gslot[ds])
                    P.inc(ins, "qV", 1)
                on_V(fge)
                mge = P.mark("qV")

                hslot = gslot
                for hg in range(4):
                    for j in range(DC):
                        wb = wbufA if (hg * DC + j) % 2 == 0 else wbufB

                        def fwl(sd, hg=hg, j=j, e=e, wb=wb):
                            P.wait(sd, "qT")
                            P.dma(sd.dma_start(
                                out=wb[:, :],
                                in_=W1[e, j * 128:(j + 1) * 128,
                                       hg * 1024:(hg + 1) * 1024]))
                        on_D(fwl)
                        mwl = P.mark("qD")

                        def fhm(t, hg=hg, j=j, e=e, wb=wb):
                            P.wait(t, "qD", mwl)
                            if hg == 0 and j == 0:
                                P.wait(t, "qV", mge)
                            ins = None
                            for hs in range(DC):
                                ins = t.matmul(hslot[hs],
                                               wb[:, hs * 128:(hs + 1) * 128],
                                               xg_ap(j),
                                               start=(j == 0), stop=(j == DC - 1),
                                               skip_group_check=True)
                            P.inc(ins, "qT", 1)
                        on_T(fhm)
                    mhm = P.mark("qT")

                    for hs in range(DC):
                        def fhe(sc, hg=hg, hs=hs):
                            P.wait(sc, "qT", mhm)
                            P.inc(sc.activation(h_ap(hg * 8 + hs), hslot[hs],
                                                AF.Relu), "qS")
                        on_S(fhe)

                def yg_ap(r3):
                    return (A_v[:, 0:1024], A_v[:, 1024:2048], A_k[:, 0:1024])[r3]

                for c in range(32):
                    wb = wbufA if c % 2 == 0 else wbufB

                    def fyl(sd, c=c, e=e, wb=wb):
                        P.wait(sd, "qT")
                        P.dma(sd.dma_start(out=wb[:, :],
                                           in_=W2[e, c * 128:(c + 1) * 128, :]))
                    on_D(fyl)
                    myl = P.mark("qD")

                    def fym(t, c=c, e=e, wb=wb):
                        P.wait(t, "qD", myl)
                        P.wait(t, "qS")
                        if c == 0:
                            P.wait(t, "qV")
                        ins = None
                        for r3 in range(3):
                            ins = t.matmul(ps_big[:, r3 * 512:(r3 + 1) * 512],
                                           h_ap(c)[:, r3 * 128:(r3 + 1) * 128],
                                           wb[:, 0:512],
                                           start=(c == 0), stop=(c == 31),
                                           skip_group_check=True)
                            ins = t.matmul(ps[r3][:, 0:512],
                                           h_ap(c)[:, r3 * 128:(r3 + 1) * 128],
                                           wb[:, 512:1024],
                                           start=(c == 0), stop=(c == 31),
                                           skip_group_check=True)
                        P.inc(ins, "qT", 1)
                    on_T(fym)
                mym = P.mark("qT")

                def fye(v, e=e):
                    P.wait(v, "qT", mym)
                    ins = None
                    for r3 in range(3):
                        v.tensor_copy(yg_ap(r3)[:, 0:512], ps_big[:, r3 * 512:(r3 + 1) * 512])
                        ins = v.tensor_copy(yg_ap(r3)[:, 512:1024], ps[r3][:, 0:512])
                    P.inc(ins, "qV", 1)
                on_V(fye)
                mye = P.mark("qV")

                pch = (S_n[5], S_n[6], A_q)

                for r3 in range(3):
                    def fpc(v, r3=r3, e=e):
                        P.wait(v, "qT")
                        P.inc(v.tensor_scalar(pch[r3][:, 0:2048], A_std[:, 0:2048],
                                              c_ic[:, 0:1],
                                              float(e * LCAP0 + r3 * 128),
                                              ALU.subtract, ALU.is_equal), "qV")
                    on_V(fpc)
                mpc = P.mark("qV")

                for do in range(DC):
                    for q4 in range(4):
                        pp = ps[2 + (do * 4 + q4) % 2]

                        def fsm(t, do=do, q4=q4, e=e, pp=pp):
                            P.wait(t, "qV")
                            if do == 0 and q4 == 0:
                                P.wait(t, "qV", mpc)
                            ins = None
                            for r3 in range(3):
                                ins = t.matmul(pp[:, 0:512],
                                               yg_ap(r3)[:, do * 128:(do + 1) * 128],
                                               pch[r3][:, q4 * 512:(q4 + 1) * 512],
                                               start=(r3 == 0), stop=(r3 == 2))
                            P.inc(ins, "qT", 1)
                        on_T(fsm)
                        msm = P.mark("qT")

                        def fsa(v, do=do, q4=q4, pp=pp):
                            P.wait(v, "qT", msm)
                            P.inc(v.tensor_tensor(
                                S_x[do][:, q4 * 512:(q4 + 1) * 512],
                                S_x[do][:, q4 * 512:(q4 + 1) * 512],
                                pp[:, 0:512], ALU.add), "qV")
                        on_V(fsa)

        if stage >= 6:
            emit_l0ffn()

        def emit_l1_ln():
            A_a, A_b = S_s[0], S_s[1]
            segs = ((glg, r_t1), (gfo, denb))

            def f1(t):
                P.wait(t, "qV")
                P.wait(t, "qS")
                ins = None
                for j in range(DC):
                    for q4 in range(4):
                        ins = t.matmul(ps_big[0:1, q4 * 512:(q4 + 1) * 512],
                                       c_1c[:, 0:1], S_x[j][:, q4 * 512:(q4 + 1) * 512],
                                       start=(j == 0), stop=(j == DC - 1),
                                       skip_group_check=True)
                P.inc(ins, "qT", 1)
            on_T(f1)
            m1 = P.mark("qT")

            def f1e(v):
                P.wait(v, "qT", m1)
                v.tensor_copy(glg[0:1, 0:1024], ps_big[0:1, 0:1024])
                P.inc(v.tensor_copy(gfo[0:1, 0:1024], ps_big[0:1, 1024:2048]), "qV")
            on_V(f1e)
            m1e = P.mark("qV")

            for j in range(DC):
                def f2s(sc, j=j):
                    P.wait(sc, "qT")
                    if j == 0:
                        P.wait(sc, "qV", m1e)
                    P.inc(sc.activation(A_a[:, :], S_x[j][:, :], AF.Square), "qS")
                on_S(f2s)
                m2s = P.mark("qS")

                def f2m(t, j=j):
                    P.wait(t, "qS", m2s)
                    ins = None
                    for q4 in range(4):
                        ins = t.matmul(ps_big[0:1, q4 * 512:(q4 + 1) * 512],
                                       c_1c[:, 0:1], A_a[:, q4 * 512:(q4 + 1) * 512],
                                       start=(j == 0), stop=(j == DC - 1),
                                       skip_group_check=True)
                    P.inc(ins, "qT", 1)
                on_T(f2m)
            m2 = P.mark("qT")

            def f3(v):
                P.wait(v, "qT", m2)
                v.tensor_copy(r_t1[0:1, 0:1024], ps_big[0:1, 0:1024])
                P.inc(v.tensor_copy(denb[0:1, 0:1024], ps_big[0:1, 1024:2048]), "qV")
                P.wait(v, "qV")
                for sg, qg in segs:
                    v.tensor_scalar(sg[0:1, 0:1024], sg[0:1, 0:1024], 1.0 / D,
                                    None, ALU.mult)
                    P.inc(v.tensor_scalar(qg[0:1, 0:1024], qg[0:1, 0:1024], 1.0 / D,
                                          None, ALU.mult), "qV")
                    P.wait(v, "qV")
                    P.inc(v.tensor_tensor(A_a[0:1, 0:1024], sg[0:1, 0:1024],
                                          sg[0:1, 0:1024], ALU.mult), "qV")
                    P.wait(v, "qV")
                    P.inc(v.tensor_sub(qg[0:1, 0:1024], qg[0:1, 0:1024],
                                       A_a[0:1, 0:1024]), "qV")
                    P.wait(v, "qV")
                P.inc(v.nop(), "qV", 1)
            on_V(f3)
            m3 = P.mark("qV")

            def f4(sc):
                P.wait(sc, "qV", m3)
                sc.activation(r_t1[0:1, 0:1024], r_t1[0:1, 0:1024], AF.Sqrt,
                              bias=float(EPS_LN))
                P.inc(sc.activation(denb[0:1, 0:1024], denb[0:1, 0:1024], AF.Sqrt,
                                    bias=float(EPS_LN)), "qS")
            on_S(f4)
            m4 = P.mark("qS")

            def f5(v):
                P.wait(v, "qS", m4)
                v.reciprocal(r_t1[0:1, 0:1024], r_t1[0:1, 0:1024])
                P.inc(v.reciprocal(denb[0:1, 0:1024], denb[0:1, 0:1024]), "qV")
                P.wait(v, "qV")
                for sg, qg in segs:
                    P.inc(v.tensor_tensor(sg[0:1, 0:1024], sg[0:1, 0:1024],
                                          qg[0:1, 0:1024], ALU.mult), "qV")
                    P.wait(v, "qV")
                    P.inc(v.tensor_scalar(sg[0:1, 0:1024], sg[0:1, 0:1024], -1.0,
                                          None, ALU.mult), "qV")
                    P.wait(v, "qV")
            on_V(f5)
            m5 = P.mark("qV")

            for q4 in range(4):
                qg = r_t1 if q4 < 2 else denb
                sg = glg if q4 < 2 else gfo
                cc = (q4 % 2) * 512

                def fb(t, q4=q4, qg=qg, sg=sg, cc=cc):
                    P.wait(t, "qV")
                    t.matmul(ps[0][:, 0:512], c_1r[0:1, 0:128], qg[0:1, cc:cc + 512],
                             start=True, stop=True)
                    P.inc(t.matmul(ps[1][:, 0:512], c_1r[0:1, 0:128],
                                   sg[0:1, cc:cc + 512], start=True, stop=True), "qT")
                on_T(fb)
                mb = P.mark("qT")

                def fbe(v, q4=q4):
                    P.wait(v, "qT", mb)
                    v.tensor_copy(A_a[:, q4 * 512:(q4 + 1) * 512], ps[0][:, 0:512])
                    P.inc(v.tensor_copy(A_b[:, q4 * 512:(q4 + 1) * 512],
                                        ps[1][:, 0:512]), "qV")
                on_V(fbe)

            for j in range(DC):
                def fx(v, j=j):
                    P.wait(v, "qV")
                    v.tensor_mul(S_n[j][:, :], S_x[j][:, :], A_a[:, :])
                    v.tensor_add(S_n[j][:, :], S_n[j][:, :], A_b[:, :])
                    P.inc(v.tensor_add(S_x[j][:, :], S_x[j][:, :], S_n[j][:, :]), "qV")
                on_V(fx)
            mxn = P.mark("qV")

            for kc in range(TC):
                for j in range(DC):
                    pp = ps[j % 2]

                    def ftp(t, kc=kc, j=j, pp=pp):
                        P.wait(t, "qV")
                        if j == 0:
                            P.wait(t, "qD")
                        if kc == 0 and j == 0:
                            P.wait(t, "qV", mxn)
                        P.inc(t.transpose(pp[:, 0:128],
                                          S_n[j][:, kc * 128:(kc + 1) * 128],
                                          c_id[:]), "qT")
                    on_T(ftp)
                    mtp = P.mark("qT")

                    def fte(v, kc=kc, j=j, pp=pp):
                        P.wait(v, "qT", mtp)
                        P.inc(v.tensor_copy(sqd[:, j * 128:(j + 1) * 128],
                                            pp[:, 0:128]), "qV")
                    on_V(fte)
                mte = P.mark("qV")

                def ftd(sd, kc=kc):
                    P.wait(sd, "qV", mte)
                    P.dma(sd.dma_start(out=xn_tm_d[kc * 128:(kc + 1) * 128, :],
                                       in_=sqd[:, :]))
                on_D(ftd)

            def ftw(v):
                P.wait(v, "qD")
            on_V(ftw)

        if stage >= 7:
            emit_l1_ln()

        LCAP1 = 768

        def emit_l1ffn():
            wbufA, wbufB = xt[0], xt[1]
            U0, U1, U2, U3, U4 = S_s[0], S_s[1], S_s[2], S_s[3], S_s[4]
            slot2 = own_lrk

            def fr0(sd):
                P.wait(sd, "qT")
                P.wait(sd, "qV")
                for j in range(DC):
                    P.dma(sd.dma_start(out=s_wr[:, j * E:(j + 1) * E],
                                       in_=rw[j * 128:(j + 1) * 128, :]))
                P.dma(sd.dma_start(out=colx[:, 0:DC], in_=rsc_c))
                P.dma(sd.dma_start(out=denb[0:1, 512:520], in_=pes_r))
            on_D(fr0)
            mr0 = P.mark("qD")

            def fr1(v):
                P.wait(v, "qD", mr0)
                for j in range(DC):
                    P.inc(v.tensor_scalar(s_wr[:, j * E:(j + 1) * E],
                                          s_wr[:, j * E:(j + 1) * E],
                                          colx[:, j:j + 1], None, ALU.mult), "qV")
                P.wait(v, "qV")
            on_V(fr1)
            mr1 = P.mark("qV")

            for j in range(DC):
                def fq(sc, j=j):
                    P.wait(sc, "qT")
                    P.inc(sc.activation(U0[:, :], S_n[j][:, :], AF.Square), "qS")
                on_S(fq)
                mq = P.mark("qS")

                def fqm(t, j=j):
                    P.wait(t, "qS", mq)
                    if j == 0:
                        P.wait(t, "qV", mr1)
                    ins = None
                    for q4 in range(4):
                        ins = t.matmul(ps_big[0:1, q4 * 512:(q4 + 1) * 512],
                                       c_1c[:, 0:1], U0[:, q4 * 512:(q4 + 1) * 512],
                                       start=(j == 0), stop=(j == DC - 1),
                                       skip_group_check=True)
                    P.inc(ins, "qT", 1)
                on_T(fqm)
            mq2 = P.mark("qT")

            def fq3(v):
                P.wait(v, "qT", mq2)
                v.tensor_copy(glg[0:1, 0:1024], ps_big[0:1, 0:1024])
                P.inc(v.tensor_copy(gfo[0:1, 0:1024], ps_big[0:1, 1024:2048]), "qV")
                P.wait(v, "qV")
            on_V(fq3)
            mq3 = P.mark("qV")

            def fq4(sc):
                P.wait(sc, "qV", mq3)
                sc.activation(glg[0:1, 0:1024], glg[0:1, 0:1024], AF.Sqrt,
                              bias=1e-6, scale=1.0 / D)
                P.inc(sc.activation(gfo[0:1, 0:1024], gfo[0:1, 0:1024], AF.Sqrt,
                                    bias=1e-6, scale=1.0 / D), "qS")
            on_S(fq4)
            mq4 = P.mark("qS")

            def fq5(v):
                P.wait(v, "qS", mq4)
                v.reciprocal(glg[0:1, 0:1024], glg[0:1, 0:1024])
                P.inc(v.reciprocal(gfo[0:1, 0:1024], gfo[0:1, 0:1024]), "qV")
                P.wait(v, "qV")
                v.tensor_scalar(glg[0:1, 0:1024], glg[0:1, 0:1024],
                                float(D ** -0.5), None, ALU.mult)
                P.inc(v.tensor_scalar(gfo[0:1, 0:1024], gfo[0:1, 0:1024],
                                      float(D ** -0.5), None, ALU.mult), "qV")
                P.wait(v, "qV")
            on_V(fq5)
            mq5 = P.mark("qV")

            for kc in range(TC):
                seg = glg if kc < 8 else gfo
                off = (kc % 8) * 128

                def fxb(t, kc=kc, seg=seg, off=off):
                    P.wait(t, "qV")
                    if kc == 0:
                        P.wait(t, "qV", mq5)
                    P.inc(t.matmul(ps[2][:, 128:256], c_1r[0:1, 0:128],
                                   seg[0:1, off:off + 128], start=True, stop=True), "qT")
                on_T(fxb)
                mxb1 = P.mark("qT")

                def fxc(v, kc=kc):
                    P.wait(v, "qT", mxb1)
                    P.inc(v.tensor_copy(U0[:, 832:960], ps[2][:, 128:256]), "qV")
                on_V(fxc)
                mxc = P.mark("qV")

                def fxt(t, kc=kc):
                    P.wait(t, "qV", mxc)
                    P.inc(t.transpose(ps[3][:, 128:256], U0[:, 832:960], c_id[:]), "qT")
                on_T(fxt)
                mxt = P.mark("qT")

                def fxe(v, kc=kc):
                    P.wait(v, "qT", mxt)
                    P.inc(v.tensor_copy(mcol[:, 16 + kc:17 + kc], ps[3][:, 128:129]), "qV")
                on_V(fxe)

            for kc in range(TC):
                def flg(t, kc=kc):
                    P.wait(t, "qV")
                    ins = None
                    for j in range(DC):
                        ins = t.matmul(ps[2][:, 0:8], S_n[j][:, kc * 128:(kc + 1) * 128],
                                       s_wr[:, j * E:(j + 1) * E],
                                       start=(j == 0), stop=(j == DC - 1))
                    P.inc(ins, "qT", 1)
                on_T(flg)
                mlg = P.mark("qT")

                def fle(v, kc=kc):
                    P.wait(v, "qT", mlg)
                    P.inc(v.tensor_scalar(olg[:, kc * 8:(kc + 1) * 8], ps[2][:, 0:8],
                                          mcol[:, 16 + kc:17 + kc], None, ALU.mult), "qV")
                on_V(fle)
            mlgs = P.mark("qV")

            def ft2(v):
                P.wait(v, "qV", mlgs)
                argmax_fo(v, olg, ofo, r_mx, r_t1, r_t1[:, 512:1024], TC)
                lg3 = olg[:, 0:TC * 8].rearrange("p (t e) -> p t e", e=8)
                t13 = r_t1[:, 0:TC * 8].rearrange("p (t e) -> p t e", e=8)
                ra = apof(r_mx)
                mxb = bass.AP(tensor=ra.tensor, offset=ra.offset,
                              ap=[ra.ap[0], [1, TC], [0, 8]])
                P.inc(v.tensor_tensor(t13, lg3, mxb, ALU.subtract), "qV")
            on_V(ft2)
            mt2a = P.mark("qV")

            def ft2e(sc):
                P.wait(sc, "qV", mt2a)
                P.inc(sc.activation(r_t1[:, 0:TC * 8], r_t1[:, 0:TC * 8], AF.Exp), "qS")
            on_S(ft2e)
            mt2e = P.mark("qS")

            def ft3(v):
                P.wait(v, "qS", mt2e)
                ex3 = r_t1[:, 0:TC * 8].rearrange("p (t e) -> p t e", e=8)
                P.inc(v.reduce_sum(own_p[:, 0:TC], ex3, axis=AX.X), "qV")
                P.wait(v, "qV")
                P.inc(v.reciprocal(own_p[:, 0:TC], own_p[:, 0:TC]), "qV")
                P.wait(v, "qV")
                P.inc(v.tensor_mul(r_t1[:, 512:512 + TC * 8], r_t1[:, 0:TC * 8],
                                   ofo[:, 0:TC * 8]), "qV")
                P.wait(v, "qV")
                P.inc(v.reduce_sum(own_sc[:, 0:TC],
                                   r_t1[:, 512:512 + TC * 8].rearrange(
                                       "p (t e) -> p t e", e=8), axis=AX.X), "qV")
                P.wait(v, "qV")
                P.inc(v.tensor_mul(own_sc[:, 0:TC], own_sc[:, 0:TC],
                                   own_p[:, 0:TC]), "qV")
                P.inc(v.scalar_tensor_tensor(glg[:, 0:TC * 8], ofo[:, 0:TC * 8],
                                             -BIGF, olg[:, 0:TC * 8],
                                             ALU.mult, ALU.add), "qV")
                P.wait(v, "qV")
                argmax_fo(v, glg, gfo, r_mx, r_t1[:, 512:1024],
                          r_t1[:, 0:512], TC)
                P.inc(v.nop(), "qV", 1)
            on_V(ft3)
            def ft3b(v):
                lg3 = olg[:, 0:TC * 8].rearrange("p (t e) -> p t e", e=8)
                t13 = r_t1[:, 0:TC * 8].rearrange("p (t e) -> p t e", e=8)
                P.inc(v.reduce_max(r_mx[:, 0:TC], lg3, axis=AX.X), "qV")
                P.wait(v, "qV")
                ra = apof(r_mx)
                mxb = bass.AP(tensor=ra.tensor, offset=ra.offset,
                              ap=[ra.ap[0], [1, TC], [0, 8]])
                P.inc(v.tensor_tensor(t13, lg3, mxb, ALU.subtract), "qV")
            on_V(ft3b)
            mt3b = P.mark("qV")

            def ft3c(sc):
                P.wait(sc, "qV", mt3b)
                P.inc(sc.activation(r_t1[:, 0:TC * 8], r_t1[:, 0:TC * 8], AF.Exp), "qS")
            on_S(ft3c)
            mt3c = P.mark("qS")

            def ft4(v):
                P.wait(v, "qS", mt3c)
                P.inc(v.tensor_mul(glg[:, 0:TC * 8], r_t1[:, 0:TC * 8],
                                   gfo[:, 0:TC * 8]), "qV")
                P.wait(v, "qV")
                P.inc(v.reduce_sum(own_p2[:, 0:TC],
                                   glg[:, 0:TC * 8].rearrange("p (t e) -> p t e", e=8),
                                   axis=AX.X), "qV")
                P.wait(v, "qV")
                P.inc(v.tensor_mul(own_p2[:, 0:TC], own_p2[:, 0:TC],
                                   own_p[:, 0:TC]), "qV")
                P.wait(v, "qV")
                P.inc(v.tensor_tensor(own_p[:, 0:TC], own_sc[:, 0:TC],
                                      own_p2[:, 0:TC], ALU.add), "qV")
                P.wait(v, "qV")
                P.inc(v.tensor_scalar(r_mx[:, 0:TC], own_p[:, 0:TC], 0.0, None,
                                      ALU.is_gt), "qV")
                P.wait(v, "qV")
                P.inc(v.tensor_scalar(r_mx[:, 16:16 + TC], r_mx[:, 0:TC], -1.0, 1.0,
                                      ALU.mult, ALU.add), "qV")
                P.wait(v, "qV")
                P.inc(v.scalar_tensor_tensor(own_p[:, 0:TC], own_p[:, 0:TC], 1.0,
                                             r_mx[:, 16:16 + TC], ALU.mult,
                                             ALU.add), "qV")
                P.wait(v, "qV")
                P.inc(v.reciprocal(own_p[:, 0:TC], own_p[:, 0:TC]), "qV")
                P.wait(v, "qV")
                v.tensor_mul(own_sc[:, 0:TC], own_sc[:, 0:TC], own_p[:, 0:TC])
                P.inc(v.tensor_mul(own_p2[:, 0:TC], own_p2[:, 0:TC],
                                   own_p[:, 0:TC]), "qV")
                P.wait(v, "qV")
            on_V(ft4)
            mt4 = P.mark("qV")

            def fpe(t):
                P.wait(t, "qV", mt4)
                P.inc(t.matmul(ps[2][:, 0:8], c_1r[0:1, 0:128], denb[0:1, 512:520],
                               start=True, stop=True), "qT")
            on_T(fpe)
            mpe = P.mark("qT")

            def fpe2(v):
                P.wait(v, "qT", mpe)
                P.inc(v.tensor_copy(eL_bc[:, 0:8], ps[2][:, 0:8]), "qV")
                P.wait(v, "qV")
                for oh, ccol in ((ofo, own_sc), (gfo, own_p2)):
                    ea = apof(eL_bc)
                    eb = bass.AP(tensor=ea.tensor, offset=ea.offset,
                                 ap=[ea.ap[0], [0, TC], [1, 8]])
                    P.inc(v.tensor_tensor(r_t1[:, 0:TC * 8].rearrange(
                        "p (t e) -> p t e", e=8),
                        oh[:, 0:TC * 8].rearrange("p (t e) -> p t e", e=8),
                        eb, ALU.mult), "qV")
                    P.wait(v, "qV")
                    P.inc(v.reduce_sum(r_mx[:, 0:TC],
                                       r_t1[:, 0:TC * 8].rearrange(
                                           "p (t e) -> p t e", e=8), axis=AX.X), "qV")
                    P.wait(v, "qV")
                    P.inc(v.tensor_mul(ccol[:, 0:TC], ccol[:, 0:TC],
                                       r_mx[:, 0:TC]), "qV")
                    P.wait(v, "qV")
            on_V(fpe2)
            mpe2 = P.mark("qV")

            def zc2(g):
                P.wait(g, "qV", mpe2)
                g.memset(carry[:], 0.0)
                P.inc(g.memset(pfx_bc[:, 0:8], 0.0), "qG")
            on_G(zc2)
            mzc = P.mark("qG")

            for pass_i, (oh, slot_dst) in enumerate(((ofo, own_slot), (gfo, slot2))):
                for k in range(TC):
                    def ftk(t, k=k, oh=oh, pass_i=pass_i):
                        P.wait(t, "qV")
                        if k == 0 and pass_i == 0:
                            P.wait(t, "qG", mzc)
                        t.matmul(ps[0][:, 0:8], c_1r[0:1, 0:128], carry[0:1, 0:8],
                                 start=True, stop=False)
                        t.matmul(ps[0][:, 0:8], c_u[:, :], oh[:, k * 8:(k + 1) * 8],
                                 start=False, stop=True)
                        P.inc(t.matmul(ps[1][:1, 0:8], c_1c[:, 0:1],
                                       oh[:, k * 8:(k + 1) * 8],
                                       start=True, stop=True), "qT")
                    on_T(ftk)
                    mtk = P.mark("qT")

                    def fvk(v, k=k, oh=oh):
                        P.wait(v, "qT", mtk)
                        v.tensor_sub(r_t1[:, k * 8:(k + 1) * 8], ps[0][:, 0:8],
                                     oh[:, k * 8:(k + 1) * 8])
                        P.inc(v.tensor_add(carry[0:1, 0:8], carry[0:1, 0:8],
                                           ps[1][0:1, 0:8]), "qV")
                    on_V(fvk)
                mrank = P.mark("qV")

                def fsl(v, oh=oh, slot_dst=slot_dst, pass_i=pass_i):
                    P.wait(v, "qV", mrank)
                    lrk3 = r_t1[:, 0:TC * 8].rearrange("p (t e) -> p t e", e=8)
                    sv = glg[:, 0:TC * 8]
                    sv3 = sv.rearrange("p (t e) -> p t e", e=8)
                    q = r_t1[:, 512:512 + TC * 8]
                    q3 = q.rearrange("p (t e) -> p t e", e=8)
                    P.inc(v.tensor_scalar(eL_bc[:, 0:8], c_ir[:, 0:8],
                                          float(LCAP1), None, ALU.mult), "qV")
                    P.wait(v, "qV")
                    P.inc(v.tensor_scalar(r_mx[:, 32:40], eL_bc[:, 0:8],
                                          float(LCAP1), None, ALU.add), "qV")
                    if pass_i == 1:
                        P.inc(v.tensor_add(eL_bc[:, 0:8], eL_bc[:, 0:8],
                                           pfx_bc[:, 0:8]), "qV")
                    P.wait(v, "qV")
                    ea = apof(eL_bc)
                    eb = bass.AP(tensor=ea.tensor, offset=ea.offset,
                                 ap=[ea.ap[0], [0, TC], [1, 8]])
                    P.inc(v.tensor_tensor(sv3, lrk3, eb, ALU.add), "qV")
                    P.wait(v, "qV")
                    ca = apof(r_mx)
                    cb = bass.AP(tensor=ca.tensor, offset=ca.offset + 32,
                                 ap=[ca.ap[0], [0, TC], [1, 8]])
                    P.inc(v.tensor_tensor(q3, sv3, cb, ALU.is_lt), "qV")
                    P.wait(v, "qV")
                    P.inc(v.tensor_mul(q, q, oh[:, 0:TC * 8]), "qV")
                    P.wait(v, "qV")
                    P.inc(v.reduce_sum(r_mx[:, 0:TC], q3, axis=AX.X), "qV")
                    P.inc(v.tensor_mul(sv, sv, q), "qV")
                    P.wait(v, "qV")
                    P.inc(v.reduce_sum(slot_dst[:, 0:TC], sv3, axis=AX.X), "qV")
                    P.wait(v, "qV")
                    P.inc(v.tensor_scalar(r_mx[:, 16:16 + TC], r_mx[:, 0:TC],
                                          -BIGF, BIGF, ALU.mult, ALU.add), "qV")
                    P.wait(v, "qV")
                    P.inc(v.tensor_add(slot_dst[:, 0:TC], slot_dst[:, 0:TC],
                                       r_mx[:, 16:16 + TC]), "qV")
                    P.wait(v, "qV")
                on_V(fsl)

                if pass_i == 0:
                    def fcb(t):
                        P.wait(t, "qV")
                        P.inc(t.matmul(ps[2][:, 0:8], c_1r[0:1, 0:128],
                                       carry[0:1, 0:8], start=True, stop=True), "qT")
                    on_T(fcb)
                    mcb = P.mark("qT")

                    def fce(v):
                        P.wait(v, "qT", mcb)
                        P.inc(v.tensor_copy(pfx_bc[:, 0:8], ps[2][:, 0:8]), "qV")
                        P.inc(v.memset(carry[0:1, 0:8], 0.0), "qV")
                        P.wait(v, "qV")
                    on_V(fce)

            gsl = [ps_big[:, 0:384], ps_big[:, 512:896], ps_big[:, 1024:1408],
                   ps_big[:, 1536:1920], ps[0][:, 0:384], ps[1][:, 0:384],
                   ps[2][:, 0:384], ps[3][:, 0:384]]

            def xg_ap1(j):
                return U0[:, j * 384:(j + 1) * 384] if j < 5 \
                    else U1[:, (j - 5) * 384:(j - 4) * 384]

            def h_ap1(hs):
                if hs < 5:
                    return U2[:, hs * 384:(hs + 1) * 384]
                return U3[:, (hs - 5) * 384:(hs - 4) * 384]

            pts_sl = U1[:, 1536:1920]

            def yg_ap1(r3):
                return (U4[:, 0:1024], U4[:, 1024:2048], sqd[:, 0:1024])[r3]

            for e in range(E):
                for hf in range(2):
                    base = e * LCAP1 + hf * 384
                    for kc in range(TC):
                        def fgl(sd, kc=kc):
                            P.wait(sd, "qT")
                            P.dma(sd.dma_start(out=wbufB[:, :],
                                               in_=xn_tm_d[kc * 128:(kc + 1) * 128, :]))
                        on_D(fgl)
                        mgl = P.mark("qD")

                        def fpt(v, kc=kc, base=base):
                            P.wait(v, "qT")
                            P.inc(v.tensor_scalar(pts_sl, c_ir[:, 0:384],
                                                  own_slot[:, kc:kc + 1],
                                                  float(-base), ALU.subtract,
                                                  ALU.is_equal), "qV")
                            P.inc(v.tensor_scalar(U0[:, 832:832 + 384]
                                                  if False else r_t1[:, 0:384],
                                                  c_ir[:, 0:384],
                                                  slot2[:, kc:kc + 1],
                                                  float(-base), ALU.subtract,
                                                  ALU.is_equal), "qV")
                            P.wait(v, "qV")
                            P.inc(v.tensor_add(pts_sl, pts_sl, r_t1[:, 0:384]), "qV")
                            P.wait(v, "qV")
                        on_V(fpt)
                        mpt = P.mark("qV")

                        def fgm(t, kc=kc):
                            P.wait(t, "qV", mpt)
                            P.wait(t, "qD", mgl)
                            ins = None
                            for ds in range(DC):
                                ins = t.matmul(gsl[ds],
                                               wbufB[:, ds * 128:(ds + 1) * 128],
                                               pts_sl,
                                               start=(kc == 0), stop=(kc == TC - 1),
                                               skip_group_check=True)
                            P.inc(ins, "qT", 1)
                        on_T(fgm)
                    mgm = P.mark("qT")

                    def fge(v):
                        P.wait(v, "qT", mgm)
                        ins = None
                        for ds in range(DC):
                            ins = v.tensor_copy(xg_ap1(ds), gsl[ds])
                        P.inc(ins, "qV", 1)
                    on_V(fge)
                    mge = P.mark("qV")

                    if dbg is not None and e == 7 and hf == 1:
                        def fdbg1(sd):
                            P.wait(sd, "qV", mge)
                            P.dma(sd.dma_start(out=dbg[:, 4848:4848 + 384],
                                               in_=U0[:, 0:384]))
                        on_D(fdbg1)

                        def fdbg1w(t):
                            P.wait(t, "qD")
                        on_T(fdbg1w)

                    for kc in range(TC):
                        def fcg(v, kc=kc, base=base):
                            P.wait(v, "qT")
                            P.inc(v.tensor_scalar(pts_sl, c_ir[:, 0:384],
                                                  own_slot[:, kc:kc + 1],
                                                  float(-base), ALU.subtract,
                                                  ALU.is_equal), "qV")
                            P.inc(v.tensor_scalar(r_t1[:, 0:384], c_ir[:, 0:384],
                                                  slot2[:, kc:kc + 1],
                                                  float(-base), ALU.subtract,
                                                  ALU.is_equal), "qV")
                            P.wait(v, "qV")
                        on_V(fcg)
                        mcg = P.mark("qV")

                        def fcgm(t, kc=kc):
                            P.wait(t, "qV", mcg)
                            ins = None
                            for r3 in range(3):
                                ins = t.matmul(ps[1 + r3][:, 400:401],
                                               pts_sl[:, r3 * 128:(r3 + 1) * 128],
                                               own_sc[:, kc:kc + 1],
                                               start=(kc == 0), stop=False,
                                               skip_group_check=True)
                                ins = t.matmul(ps[1 + r3][:, 400:401],
                                               r_t1[:, r3 * 128:(r3 + 1) * 128],
                                               own_p2[:, kc:kc + 1],
                                               start=False, stop=(kc == TC - 1),
                                               skip_group_check=True)
                            P.inc(ins, "qT", 1)
                        on_T(fcgm)
                    mcgm = P.mark("qT")

                    def fcge(v):
                        P.wait(v, "qT", mcgm)
                        for r3 in range(3):
                            v.tensor_copy(mcol[:, 19 + r3:20 + r3],
                                          ps[1 + r3][:, 400:401])
                        P.inc(v.nop(), "qV", 1)
                        P.wait(v, "qV")
                    on_V(fcge)

                    for wv in range(2):
                        for gate in range(2):
                            for j in range(DC):
                                wb = wbufA if j % 2 == 0 else wbufB

                                def fwl(sd, e=e, hf=hf, wv=wv, gate=gate, j=j, wb=wb):
                                    P.wait(sd, "qT")
                                    P.dma(sd.dma_start(
                                        out=wb[:, :],
                                        in_=gwT[e, gate, j * 128:(j + 1) * 128,
                                                hf * 0 + wv * 1024:wv * 1024 + 1024]))
                                on_D(fwl)
                                mwl = P.mark("qD")

                                def fhm(t, j=j, wb=wb, gate=gate):
                                    P.wait(t, "qD", mwl)
                                    P.wait(t, "qV")
                                    ins = None
                                    for hs in range(DC):
                                        ins = t.matmul(gsl[hs],
                                                       wb[:, hs * 128:(hs + 1) * 128],
                                                       xg_ap1(j),
                                                       start=(j == 0),
                                                       stop=(j == DC - 1),
                                                       skip_group_check=True)
                                    P.inc(ins, "qT", 1)
                                on_T(fhm)
                            mhm = P.mark("qT")

                            if gate == 0:
                                for hs in range(DC):
                                    def fhe(sc, hs=hs):
                                        P.wait(sc, "qT", mhm)
                                        P.inc(sc.activation(h_ap1(hs), gsl[hs],
                                                            AF.Gelu_apprx_tanh), "qS")
                                    on_S(fhe)

                                def fhw(v):
                                    P.wait(v, "qS")
                                    P.inc(v.tensor_copy(eL_bc[:, 0:1],
                                                        eL_bc[:, 0:1]), "qV")
                                on_V(fhw)
                            else:
                                for hs in range(DC):
                                    def fhe2(v, hs=hs):
                                        if hs == 0:
                                            P.wait(v, "qT", mhm)
                                        P.inc(v.tensor_tensor(h_ap1(hs), h_ap1(hs),
                                                              gsl[hs], ALU.mult), "qV")
                                    on_V(fhe2)
                        mact = P.mark("qV")

                        if dbg is not None and e == 7 and hf == 1:
                            def fdbg2(sd, wv=wv):
                                P.wait(sd, "qV", mact)
                                P.dma(sd.dma_start(
                                    out=dbg[:, 5232 + wv * 768:5232 + wv * 768 + 384],
                                    in_=h_ap1(0)))
                            on_D(fdbg2)

                            def fdbg2w(t):
                                P.wait(t, "qD")
                            on_T(fdbg2w)

                            def fdbg2v(v):
                                P.wait(v, "qD")
                            on_V(fdbg2v)

                        for hs in range(DC):
                            wb = wbufA if hs % 2 == 0 else wbufB

                            def fyl(sd, e=e, wv=wv, hs=hs, wb=wb):
                                P.wait(sd, "qT")
                                P.dma(sd.dma_start(
                                    out=wb[:, :],
                                    in_=lw[e, (wv * 8 + hs) * 128:
                                           (wv * 8 + hs + 1) * 128, :]))
                            on_D(fyl)
                            myl = P.mark("qD")

                            def fym(t, hs=hs, wv=wv, wb=wb):
                                P.wait(t, "qD", myl)
                                if hs == 0:
                                    P.wait(t, "qV", mact)
                                ins = None
                                for r3 in range(3):
                                    for nh in range(2):
                                        ins = t.matmul(
                                            (ps_big[:, r3 * 512:(r3 + 1) * 512]
                                             if nh == 0 else
                                             ps[r3][:, 0:512]),
                                            h_ap1(hs)[:, r3 * 128:(r3 + 1) * 128],
                                            wb[:, nh * 512:(nh + 1) * 512],
                                            start=(hs == 0),
                                            stop=(hs == DC - 1),
                                            skip_group_check=True)
                                P.inc(ins, "qT", 1)
                            on_T(fym)
                        mymw = P.mark("qT")

                        def fyew(v, wv=wv):
                            P.wait(v, "qT", mymw)
                            ins = None
                            for r3 in range(3):
                                for nh in range(2):
                                    src = ps_big[:, r3 * 512:(r3 + 1) * 512] \
                                        if nh == 0 else ps[r3][:, 0:512]
                                    dst = yg_ap1(r3)[:, nh * 512:(nh + 1) * 512]
                                    if wv == 0:
                                        ins = v.tensor_copy(dst, src)
                                    else:
                                        ins = v.tensor_tensor(dst, dst, src, ALU.add)
                            P.inc(ins, "qV", 1)
                            P.wait(v, "qV")
                            if wv == 1:
                                ins = None
                                for r3 in range(3):
                                    ins = v.tensor_scalar(yg_ap1(r3), yg_ap1(r3),
                                                          mcol[:, 19 + r3:20 + r3],
                                                          None, ALU.mult)
                                P.inc(ins, "qV", 1)
                                P.wait(v, "qV")
                        on_V(fyew)
                    mye = P.mark("qV")

                    if dbg is not None and e == 7 and hf == 1:
                        def fdbgy(sd):
                            P.wait(sd, "qV", mye)
                            P.dma(sd.dma_start(out=dbg[:, 5616:5616 + 384],
                                               in_=yg_ap1(0)[:, 0:384]))
                        on_D(fdbgy)

                        def fdbgyw(t):
                            P.wait(t, "qD")
                        on_T(fdbgyw)

                    for src_col, dstU in ((own_slot, U2), (slot2, U3)):
                        for kc in range(TC):
                            def fsb(v, kc=kc, src_col=src_col):
                                P.wait(v, "qT")
                                P.inc(v.tensor_copy(
                                    r_t1[:, 0:128],
                                    bcast_free(src_col[:, kc:kc + 1], 128)), "qV")
                            on_V(fsb)
                            msb = P.mark("qV")

                            def fst2(t, kc=kc):
                                P.wait(t, "qV", msb)
                                P.inc(t.transpose(ps[2][:, 0:128], r_t1[:, 0:128],
                                                  c_id[:]), "qT")
                            on_T(fst2)
                            mst = P.mark("qT")

                            def fse(v, kc=kc, dstU=dstU):
                                P.wait(v, "qT", mst)
                                P.inc(v.tensor_copy(dstU[:, kc * 128:(kc + 1) * 128],
                                                    ps[2][:, 0:128]), "qV")
                            on_V(fse)

                    for grp in ((0, 1), (2,)):
                        for gi, r3 in enumerate(grp):
                            if len(grp) == 2:
                                dsts = ((U0[:, 0:1024], U0[:, 1024:2048]),
                                        (U1[:, 0:1024], U1[:, 1024:2048]))[gi]
                            else:
                                dsts = (U0[:, 0:1024], U0[:, 1024:2048])

                            def fpc(v, r3=r3, dsts=dsts, base=base):
                                P.wait(v, "qT")
                                for th in range(2):
                                    dst = dsts[th]
                                    P.inc(v.tensor_scalar(
                                        dst, U2[:, th * 1024:(th + 1) * 1024],
                                        c_ic[:, 0:1], float(base + r3 * 128),
                                        ALU.subtract, ALU.is_equal), "qV")
                                    P.inc(v.tensor_scalar(
                                        r_t1[:, 0:1024],
                                        U3[:, th * 1024:(th + 1) * 1024],
                                        c_ic[:, 0:1], float(base + r3 * 128),
                                        ALU.subtract, ALU.is_equal), "qV")
                                    P.wait(v, "qV")
                                    P.inc(v.tensor_add(dst, dst, r_t1[:, 0:1024]),
                                          "qV")
                                    P.wait(v, "qV")
                            on_V(fpc)
                        mpc = P.mark("qV")

                        def pch_ap(gi, q4, grp=grp):
                            if len(grp) == 2:
                                pl = ((U0[:, 0:1024], U0[:, 1024:2048]),
                                      (U1[:, 0:1024], U1[:, 1024:2048]))[gi]
                            else:
                                pl = (U0[:, 0:1024], U0[:, 1024:2048])
                            th, cc = divmod(q4 * 512, 1024)
                            return pl[th][:, cc:cc + 512]

                        for do in range(DC):
                            for q4 in range(4):
                                pp = ps[2 + (do * 4 + q4) % 2]

                                def fsm(t, do=do, q4=q4, grp=grp, pp=pp):
                                    P.wait(t, "qV")
                                    if do == 0 and q4 == 0:
                                        P.wait(t, "qV", mpc)
                                    ins = None
                                    for gi, r3 in enumerate(grp):
                                        ins = t.matmul(
                                            pp[:, 0:512],
                                            yg_ap1(r3)[:, do * 128:(do + 1) * 128],
                                            pch_ap(gi, q4),
                                            start=(gi == 0),
                                            stop=(gi == len(grp) - 1))
                                    P.inc(ins, "qT", 1)
                                on_T(fsm)
                                msm = P.mark("qT")

                                def fsa(v, do=do, q4=q4, pp=pp):
                                    P.wait(v, "qT", msm)
                                    P.inc(v.tensor_tensor(
                                        S_x[do][:, q4 * 512:(q4 + 1) * 512],
                                        S_x[do][:, q4 * 512:(q4 + 1) * 512],
                                        pp[:, 0:512], ALU.add), "qV")
                                on_V(fsa)

        if stage >= 8:
            emit_l1ffn()

        def emit_l1_bk():
            wbufA, wbufB = xt[0], xt[1]
            U0, U1, U2, U3 = S_s[0], S_s[1], S_s[2], S_s[3]
            U4s = S_s[4]

            def f0(sd):
                P.wait(sd, "qT")
                P.wait(sd, "qV")
                for j in range(DC):
                    P.dma(sd.dma_start(out=s_wr[:, j * 2:(j + 1) * 2],
                                       in_=w_pot[j * 128:(j + 1) * 128, :]))
                P.dma(sd.dma_start(out=denb[0:1, 512:514], in_=b_pot_r))
                P.dma(sd.dma_start(out=mcol[:, 0:6 * DC], in_=ln_c)) \
                    if False else None
            on_D(f0)
            def f0b(sd):
                P.dma(sd.dma_start(out=wbufA[:, 0:48], in_=ln_c))
            on_D(f0b)
            ml0 = P.mark("qD")

            def f1(v):
                P.wait(v, "qD", ml0)
                for j in range(DC):
                    P.inc(v.tensor_scalar(s_wr[:, j * 2:(j + 1) * 2],
                                          s_wr[:, j * 2:(j + 1) * 2],
                                          wbufA[:, j:j + 1], None, ALU.mult), "qV")
                P.wait(v, "qV")
            on_V(f1)
            mf1 = P.mark("qV")

            def f1b(sd):
                P.wait(sd, "qV", mf1)
                for j in range(DC):
                    P.dma(sd.dma_start(out=wbufB[:, j * 2:(j + 1) * 2],
                                       in_=w_pot[j * 128:(j + 1) * 128, :]))
            on_D(f1b)
            mf1b = P.mark("qD")

            def f1c(v):
                P.wait(v, "qD", mf1b)
                for j in range(DC):
                    P.inc(v.tensor_scalar(wbufB[:, j * 2:(j + 1) * 2],
                                          wbufB[:, j * 2:(j + 1) * 2],
                                          wbufA[:, 8 + j:9 + j], None, ALU.mult), "qV")
                P.wait(v, "qV")
            on_V(f1c)
            mf1c = P.mark("qV")

            def f1d(t):
                P.wait(t, "qV", mf1c)
                ins = None
                for j in range(DC):
                    ins = t.matmul(ps[0][0:1, 0:2], c_1c[:, 0:1],
                                   wbufB[:, j * 2:(j + 1) * 2],
                                   start=(j == 0), stop=(j == DC - 1))
                P.inc(ins, "qT", 1)
            on_T(f1d)
            mf1d = P.mark("qT")

            def f1e(v):
                P.wait(v, "qT", mf1d)
                P.inc(v.tensor_copy(denb[0:1, 514:516], ps[0][0:1, 0:2]), "qV")
                P.wait(v, "qV")
                P.inc(v.tensor_add(denb[0:1, 514:516], denb[0:1, 514:516],
                                   denb[0:1, 512:514]), "qV")
                P.wait(v, "qV")
            on_V(f1e)
            mf1e = P.mark("qV")

            def f2(t):
                P.wait(t, "qV", mf1e)
                insv = None
                for q4 in range(4):
                    t.matmul(ps_big[0:1, q4 * 512:(q4 + 1) * 512],
                             denb[0:1, 514:515], c_1r[0:1, 0:512],
                             start=True, stop=False, skip_group_check=True)
                    t.matmul(ps[q4][0:1, 0:512], denb[0:1, 515:516],
                             c_1r[0:1, 0:512], start=True, stop=False,
                             skip_group_check=True)
                for j in range(DC):
                    for q4 in range(4):
                        t.matmul(ps_big[0:1, q4 * 512:(q4 + 1) * 512],
                                 s_wr[:, j * 2:j * 2 + 1],
                                 S_n[j][:, q4 * 512:(q4 + 1) * 512],
                                 start=False, stop=(j == DC - 1),
                                 skip_group_check=True)
                        insv = t.matmul(ps[q4][0:1, 0:512],
                                        s_wr[:, j * 2 + 1:j * 2 + 2],
                                        S_n[j][:, q4 * 512:(q4 + 1) * 512],
                                        start=False, stop=(j == DC - 1),
                                        skip_group_check=True)
                P.inc(insv, "qT", 1)
            on_T(f2)
            mf2 = P.mark("qT")

            def f3(v):
                P.wait(v, "qT", mf2)
                v.tensor_copy(glg[0:1, 0:1024], ps_big[0:1, 0:1024])
                v.tensor_copy(gfo[0:1, 0:1024], ps_big[0:1, 1024:2048])
                v.tensor_copy(r_t1[0:1, 0:512], ps[0][0:1, 0:512])
                v.tensor_copy(r_t1[0:1, 512:1024], ps[1][0:1, 0:512])
                v.tensor_copy(denb[0:1, 0:512], ps[2][0:1, 0:512])
                P.inc(v.tensor_copy(denb[0:1, 512:1024], ps[3][0:1, 0:512]), "qV")
                P.wait(v, "qV")
            on_V(f3)
            mf3 = P.mark("qV")

            def f4(sc):
                P.wait(sc, "qV", mf3)
                sc.activation(U0[0:1, 0:1024], r_t1[0:1, 0:1024], AF.Abs)
                P.inc(sc.activation(U0[0:1, 1024:2048], denb[0:1, 0:1024],
                                    AF.Abs), "qS")
            on_S(f4)
            mf4a = P.mark("qS")

            def f4b(sc):
                P.inc(sc.activation(U0[0:1, 0:2048], U0[0:1, 0:2048], AF.Exp,
                                    scale=-1.0), "qS")
            on_S(f4b)
            mf4b = P.mark("qS")

            def f4c(sc):
                P.inc(sc.activation(U0[0:1, 0:2048], U0[0:1, 0:2048], AF.Ln,
                                    bias=1.0), "qS")
            on_S(f4c)
            mf4c = P.mark("qS")

            def f5(v):
                P.wait(v, "qS", mf4c)
                v.tensor_scalar(r_t1[0:1, 0:1024], r_t1[0:1, 0:1024], 0.0,
                                None, ALU.max)
                P.inc(v.tensor_scalar(denb[0:1, 0:1024], denb[0:1, 0:1024], 0.0,
                                      None, ALU.max), "qV")
                P.wait(v, "qV")
                v.tensor_add(r_t1[0:1, 0:1024], r_t1[0:1, 0:1024],
                             U0[0:1, 0:1024])
                P.inc(v.tensor_add(denb[0:1, 0:1024], denb[0:1, 0:1024],
                                   U0[0:1, 1024:2048]), "qV")
                P.wait(v, "qV")
                v.tensor_scalar(r_t1[0:1, 0:1024], r_t1[0:1, 0:1024], 0.1,
                                None, ALU.add)
                P.inc(v.tensor_scalar(denb[0:1, 0:1024], denb[0:1, 0:1024], 0.1,
                                      None, ALU.add), "qV")
                P.wait(v, "qV")
            on_V(f5)
            mf5 = P.mark("qV")

            def f6(v):
                P.wait(v, "qV", mf5)
                v.tensor_scalar(glg[0:1, 0:1024], glg[0:1, 0:1024], 2.0, None, ALU.add)
                P.inc(v.tensor_scalar(gfo[0:1, 0:1024], gfo[0:1, 0:1024], 2.0,
                                      None, ALU.add), "qV")
                P.wait(v, "qV")
                v.tensor_scalar(U0[0:1, 0:1024], r_t1[0:1, 0:1024], -1.0, None,
                                ALU.mult)
                P.inc(v.tensor_scalar(U0[0:1, 1024:2048], denb[0:1, 0:1024], -1.0,
                                      None, ALU.mult), "qV")
                P.wait(v, "qV")
                for (srcA, srcB, dst) in (
                        ((glg, 0), (gfo, 0), (U1, 0)),):
                    pass
                def revap(row, off):
                    a = apof(row)
                    return bass.AP(tensor=a.tensor, offset=a.offset + off,
                                   ap=[[a.ap[0][0], 1], [-1, 1024]])
                v.tensor_copy(U1[0:1, 0:1024], revap(gfo[0:1, :], 1023))
                P.inc(v.tensor_copy(U1[0:1, 1024:2048],
                                    revap(glg[0:1, :], 1023)), "qV")
                v.tensor_copy(U1[32:33, 0:1024], revap(U0[0:1, :], 2047))
                P.inc(v.tensor_copy(U1[32:33, 1024:2048],
                                    revap(U0[0:1, :], 1023)), "qV")
                P.wait(v, "qV")
            on_V(f6)
            mf6 = P.mark("qV")

            def f7(sd):
                P.wait(sd, "qV", mf6)
                P.dma(sd.dma_start(out=scr[0:1, 0:1024], in_=glg[0:1, 0:1024]))
                P.dma(sd.dma_start(out=scr[1:2, 0:1024], in_=gfo[0:1, 0:1024]))
                P.dma(sd.dma_start(out=scr[2:3, 0:1024], in_=U0[0:1, 0:1024]))
                P.dma(sd.dma_start(out=scr[3:4, 0:1024], in_=U0[0:1, 1024:2048]))
                P.dma(sd.dma_start(out=scr[4:5, 0:1024], in_=U1[0:1, 0:1024]))
                P.dma(sd.dma_start(out=scr[5:6, 0:1024], in_=U1[0:1, 1024:2048]))
                P.dma(sd.dma_start(out=scr[6:7, 0:1024], in_=U1[32:33, 0:1024]))
                P.dma(sd.dma_start(out=scr[7:8, 0:1024], in_=U1[32:33, 1024:2048]))
            on_D(f7)
            mf7 = P.mark("qD")

            def scan_cf(dir_i):
                C = lambda a, b=None: U4s[:, a:(b if b else a + 16)]

                def g1(sd, dir_i=dir_i):
                    P.wait(sd, "qV")
                    P.wait(sd, "qD", mf7)
                    r0 = 4 * dir_i
                    for hh2 in range(2):
                        P.dma(sd.dma_start(
                            out=U4s[hh2 * 64:(hh2 + 1) * 64, 0:16],
                            in_=scr[r0 + hh2:r0 + hh2 + 1, 0:1024].rearrange(
                                "a (p i) -> (a p) i", i=16)))
                        P.dma(sd.dma_start(
                            out=U4s[hh2 * 64:(hh2 + 1) * 64, 16:32],
                            in_=scr[r0 + 2 + hh2:r0 + 3 + hh2, 0:1024].rearrange(
                                "a (p i) -> (a p) i", i=16)))
                on_D(g1)
                mg1 = P.mark("qD")

                def g2(v):
                    P.wait(v, "qD", mg1)
                    v.tensor_copy(C(32)[:, 0:1], C(0)[:, 0:1])
                    v.tensor_copy(C(48)[:, 0:1], C(16)[:, 0:1])
                    v.memset(C(64)[:, 0:1], -1.0)
                    v.memset(C(80)[:, 0:1], 0.0)
                    P.inc(v.nop(), "qV", 1)
                    P.wait(v, "qV")
                    for i in range(1, 16):
                        dre_i = C(0)[:, i:i + 1]
                        dim_i = C(16)[:, i:i + 1]
                        for (pr, pi) in ((32, 48), (64, 80)):
                            prev_r = C(pr)[:, i - 1:i]
                            prev_i = C(pi)[:, i - 1:i]
                            v.tensor_tensor(C(160)[:, 0:1], dre_i, prev_r, ALU.mult)
                            v.tensor_tensor(C(160)[:, 1:2], dim_i, prev_i, ALU.mult)
                            v.tensor_tensor(C(160)[:, 2:3], dre_i, prev_i, ALU.mult)
                            P.inc(v.tensor_tensor(C(160)[:, 3:4], dim_i, prev_r,
                                                  ALU.mult), "qV")
                            P.wait(v, "qV")
                            v.tensor_sub(C(pr)[:, i:i + 1], C(160)[:, 0:1],
                                         C(160)[:, 1:2])
                            P.inc(v.tensor_add(C(pi)[:, i:i + 1], C(160)[:, 2:3],
                                               C(160)[:, 3:4]), "qV")
                            P.wait(v, "qV")
                            if i >= 2:
                                v.tensor_sub(C(pr)[:, i:i + 1], C(pr)[:, i:i + 1],
                                             C(pr)[:, i - 2:i - 1])
                                P.inc(v.tensor_sub(C(pi)[:, i:i + 1],
                                                   C(pi)[:, i:i + 1],
                                                   C(pi)[:, i - 2:i - 1]), "qV")
                                P.wait(v, "qV")
                            elif i == 1 and pr == 32:
                                P.inc(v.tensor_scalar(C(pr)[:, 1:2], C(pr)[:, 1:2],
                                                      -1.0, None, ALU.add), "qV")
                                P.wait(v, "qV")
                    P.inc(v.nop(), "qV", 1)
                on_V(g2)
                mg2 = P.mark("qV")

                def g3(v):
                    P.wait(v, "qV", mg2)
                    v.tensor_copy(C(160)[:, 0:1], C(32)[:, 15:16])
                    v.tensor_copy(C(160)[:, 1:2], C(48)[:, 15:16])
                    v.tensor_copy(C(160)[:, 2:3], C(64)[:, 15:16])
                    v.tensor_copy(C(160)[:, 3:4], C(80)[:, 15:16])
                    v.tensor_copy(C(160)[:, 4:5], C(32)[:, 14:15])
                    v.tensor_copy(C(160)[:, 5:6], C(48)[:, 14:15])
                    v.tensor_copy(C(160)[:, 6:7], C(64)[:, 14:15])
                    P.inc(v.tensor_copy(C(160)[:, 7:8], C(80)[:, 14:15]), "qV")
                    P.wait(v, "qV")
                    P.inc(v.tensor_tensor(C(168)[:, 0:8], C(160)[:, 0:8],
                                          C(160)[:, 0:8], ALU.mult), "qV")
                    P.wait(v, "qV")
                    P.inc(v.reduce_sum(C(176)[:, 0:1], C(168)[:, 0:8], axis=AX.X),
                          "qV")
                    P.wait(v, "qV")
                on_V(g3)
                mg3 = P.mark("qV")

                def g3b(sc):
                    P.wait(sc, "qV", mg3)
                    P.inc(sc.activation(C(176)[:, 0:1], C(176)[:, 0:1], AF.Sqrt,
                                        bias=1e-30), "qS")
                on_S(g3b)
                mg3b = P.mark("qS")

                def g3c(v):
                    P.wait(v, "qS", mg3b)
                    P.inc(v.reciprocal(C(176)[:, 0:1], C(176)[:, 0:1]), "qV")
                    P.wait(v, "qV")
                    P.inc(v.tensor_scalar(C(160)[:, 0:8], C(160)[:, 0:8],
                                          C(176)[:, 0:1], None, ALU.mult), "qV")
                    P.wait(v, "qV")
                on_V(g3c)

                for lvl in range(7):
                    sft = 1 << lvl
                    mpre = P.mark("qV")

                    def g4a(g, sft=sft, mpre=mpre):
                        P.wait(g, "qV", mpre)
                        g.memset(U4s[:, 168:176], 0.0)
                        g.memset(U4s[:, 168:169], 1.0)
                        P.inc(g.memset(U4s[:, 174:175], 1.0), "qG")
                    on_G(g4a)
                    mga = P.mark("qG")

                    def g4b(g, sft=sft, mga=mga):
                        P.wait(g, "qG", mga)
                        P.dma(g.dma_start(out=U4s[sft:128, 168:176],
                                          in_=U4s[0:128 - sft, 160:168]), "qG")
                    on_G(g4b)
                    mgb = P.mark("qG")

                    def g4c(v, sft=sft, mgb=mgb):
                        P.wait(v, "qG", mgb)
                        def cm(dst_r, dst_i, x, y):
                            xr, xi = C(160)[:, x:x + 1], C(160)[:, x + 1:x + 2]
                            yr, yi = C(168)[:, y:y + 1], C(168)[:, y + 1:y + 2]
                            v.tensor_tensor(C(176)[:, 8:9], xr, yr, ALU.mult)
                            v.tensor_tensor(C(176)[:, 9:10], xi, yi, ALU.mult)
                            v.tensor_tensor(C(176)[:, 10:11], xr, yi, ALU.mult)
                            P.inc(v.tensor_tensor(C(176)[:, 11:12], xi, yr,
                                                  ALU.mult), "qV")
                            P.wait(v, "qV")
                            v.tensor_sub(C(176)[:, 12:13], C(176)[:, 8:9],
                                         C(176)[:, 9:10])
                            P.inc(v.tensor_add(C(176)[:, 13:14], C(176)[:, 10:11],
                                               C(176)[:, 11:12]), "qV")
                            P.wait(v, "qV")
                            if dst_r is not None:
                                v.tensor_add(dst_r, dst_r, C(176)[:, 12:13])
                                P.inc(v.tensor_add(dst_i, dst_i,
                                                   C(176)[:, 13:14]), "qV")
                            else:
                                v.tensor_copy(C(176)[:, 14:15], C(176)[:, 12:13])
                                P.inc(v.tensor_copy(C(176)[:, 15:16],
                                                    C(176)[:, 13:14]), "qV")
                            P.wait(v, "qV")

                        for oi, (x1, y1, x2, y2) in enumerate(
                                ((0, 0, 2, 4), (0, 2, 2, 6),
                                 (4, 0, 6, 4), (4, 2, 6, 6))):
                            cm(None, None, x1, y1)
                            v.tensor_copy(C(200)[:, 2 * oi:2 * oi + 1],
                                          C(176)[:, 14:15])
                            P.inc(v.tensor_copy(C(200)[:, 2 * oi + 1:2 * oi + 2],
                                                C(176)[:, 15:16]), "qV")
                            P.wait(v, "qV")
                            cm(C(200)[:, 2 * oi:2 * oi + 1],
                               C(200)[:, 2 * oi + 1:2 * oi + 2], x2, y2)
                        P.inc(v.tensor_copy(C(160)[:, 0:8], C(200)[:, 0:8]), "qV")
                        P.wait(v, "qV")
                        P.inc(v.tensor_tensor(C(168)[:, 0:8], C(160)[:, 0:8],
                                              C(160)[:, 0:8], ALU.mult), "qV")
                        P.wait(v, "qV")
                        P.inc(v.reduce_sum(C(176)[:, 0:1], C(168)[:, 0:8],
                                           axis=AX.X), "qV")
                        P.wait(v, "qV")
                    on_V(g4c)
                    mgc = P.mark("qV")

                    def g4d(sc, mgc=mgc):
                        P.wait(sc, "qV", mgc)
                        P.inc(sc.activation(C(176)[:, 0:1], C(176)[:, 0:1],
                                            AF.Sqrt, bias=1e-30), "qS")
                    on_S(g4d)
                    mgd = P.mark("qS")

                    def g4e(v, mgd=mgd):
                        P.wait(v, "qS", mgd)
                        P.inc(v.reciprocal(C(176)[:, 0:1], C(176)[:, 0:1]), "qV")
                        P.wait(v, "qV")
                        P.inc(v.tensor_scalar(C(160)[:, 0:8], C(160)[:, 0:8],
                                              C(176)[:, 0:1], None, ALU.mult), "qV")
                        P.wait(v, "qV")
                    on_V(g4e)

                mfin = P.mark("qV")

                def g5(g, mfin=mfin):
                    P.wait(g, "qV", mfin)
                    g.memset(U4s[:, 168:176], 0.0)
                    g.memset(U4s[:, 168:169], 1.0)
                    P.inc(g.memset(U4s[:, 174:175], 1.0), "qG")
                on_G(g5)
                mg5 = P.mark("qG")

                def g5b(g, mg5=mg5):
                    P.wait(g, "qG", mg5)
                    P.dma(g.dma_start(out=U4s[1:128, 168:176],
                                      in_=U4s[0:127, 160:168]), "qG")
                on_G(g5b)
                mg5b = P.mark("qG")

                def g6(v, mg5b=mg5b):
                    P.wait(v, "qG", mg5b)
                    v.tensor_copy(C(96)[:, 1:16], C(32)[:, 0:15])
                    v.tensor_copy(C(112)[:, 1:16], C(48)[:, 0:15])
                    v.tensor_copy(C(128)[:, 1:16], C(64)[:, 0:15])
                    v.tensor_copy(C(144)[:, 1:16], C(80)[:, 0:15])
                    v.memset(C(96)[:, 0:1], 1.0)
                    v.memset(C(112)[:, 0:1], 0.0)
                    v.memset(C(128)[:, 0:1], 0.0)
                    P.inc(v.memset(C(144)[:, 0:1], 0.0), "qV")
                    P.wait(v, "qV")
                    def lin(outr, outi, P_r, P_i, Q_r, Q_i):
                        ar, ai = C(168)[:, 0:1], C(168)[:, 1:2]
                        cr, ci = C(168)[:, 4:5], C(168)[:, 5:6]
                        v.tensor_scalar(C(208), C(P_r), ar, None, ALU.mult)
                        v.tensor_scalar(C(224), C(P_i), ai, None, ALU.mult)
                        v.tensor_scalar(C(240), C(P_r), ai, None, ALU.mult)
                        P.inc(v.tensor_scalar(C(256), C(P_i), ar, None,
                                              ALU.mult), "qV")
                        P.wait(v, "qV")
                        v.tensor_sub(C(outr), C(208), C(224))
                        P.inc(v.tensor_add(C(outi), C(240),
                                           C(256)), "qV")
                        P.wait(v, "qV")
                        v.tensor_scalar(C(208), C(Q_r), cr, None, ALU.mult)
                        v.tensor_scalar(C(224), C(Q_i), ci, None, ALU.mult)
                        v.tensor_scalar(C(240), C(Q_r), ci, None, ALU.mult)
                        P.inc(v.tensor_scalar(C(256), C(Q_i), cr, None,
                                              ALU.mult), "qV")
                        P.wait(v, "qV")
                        v.tensor_sub(C(208), C(208),
                                     C(224))
                        P.inc(v.tensor_add(C(240), C(240),
                                           C(256)), "qV")
                        P.wait(v, "qV")
                        v.tensor_add(C(outr), C(outr), C(208))
                        P.inc(v.tensor_add(C(outi), C(outi), C(240)), "qV")
                        P.wait(v, "qV")

                    lin(272, 288, 32, 48, 64, 80)
                    lin(304, 320, 96, 112, 128, 144)
                    v.tensor_tensor(C(208), C(304), C(304), ALU.mult)
                    P.inc(v.tensor_tensor(C(224), C(320), C(320),
                                          ALU.mult), "qV")
                    P.wait(v, "qV")
                    P.inc(v.tensor_add(C(208), C(208),
                                       C(224)), "qV")
                    P.wait(v, "qV")
                    P.inc(v.reciprocal(C(208), C(208)), "qV")
                    P.wait(v, "qV")
                    v.tensor_tensor(C(240), C(272), C(304), ALU.mult)
                    P.inc(v.tensor_tensor(C(256), C(288), C(320),
                                          ALU.mult), "qV")
                    P.wait(v, "qV")
                    P.inc(v.tensor_add(C(176), C(240),
                                       C(256)), "qV")
                    P.wait(v, "qV")
                    v.tensor_tensor(C(240), C(288), C(304), ALU.mult)
                    P.inc(v.tensor_tensor(C(256), C(272), C(320),
                                          ALU.mult), "qV")
                    P.wait(v, "qV")
                    P.inc(v.tensor_sub(C(192), C(240),
                                       C(256)), "qV")
                    P.wait(v, "qV")
                    v.tensor_scalar(C(176), C(176), C(208)[:, 0:1] if False else C(208), None, ALU.mult) if False else None
                    P.inc(v.tensor_tensor(C(176), C(176), C(208),
                                          ALU.mult), "qV")
                    P.wait(v, "qV")
                    P.inc(v.tensor_tensor(C(192), C(192), C(208),
                                          ALU.mult), "qV")
                    P.wait(v, "qV")
                on_V(g6)
                mg6 = P.mark("qV")

                def g7(sd, dir_i=dir_i, mg6=mg6):
                    P.wait(sd, "qV", mg6)
                    rr = 8 + 2 * dir_i
                    P.dma(sd.dma_start(
                        out=scr[rr:rr + 1, 0:2048].rearrange("a (p i) -> (a p) i",
                                                        i=16),
                        in_=U4s[:, 176:192]))
                    P.dma(sd.dma_start(
                        out=scr[rr + 1:rr + 2, 0:2048].rearrange("a (p i) -> (a p) i",
                                                            i=16),
                        in_=U4s[:, 192:208]))
                on_D(g7)
                mg7 = P.mark("qD")

                def g8(sd, dir_i=dir_i, mg7=mg7):
                    P.wait(sd, "qD", mg7)
                    rr = 8 + 2 * dir_i
                    if dir_i == 0:
                        P.dma(sd.dma_start(out=U1[0:1, :], in_=scr[rr:rr + 1, 0:2048]))
                        P.dma(sd.dma_start(out=U2[0:1, :],
                                           in_=scr[rr + 1:rr + 2, 0:2048]))
                    else:
                        P.dma(sd.dma_start(out=U0[32:33, :], in_=scr[rr:rr + 1, 0:2048]))
                        P.dma(sd.dma_start(out=U4s[32:33, 0:2048],
                                           in_=scr[rr + 1:rr + 2, 0:2048]))
                on_D(g8)
                if dir_i == 1:
                    mg8 = P.mark("qD")

                    def g9(v, mg8=mg8):
                        P.wait(v, "qD", mg8)
                        def revap(row, off):
                            a = apof(row)
                            return bass.AP(tensor=a.tensor, offset=a.offset + off,
                                           ap=[[a.ap[0][0], 1], [-1, 1024]])
                        v.tensor_copy(U3[0:1, 0:1024],
                                      revap(U0[32:33, :], 2047))
                        P.inc(v.tensor_copy(U3[0:1, 1024:2048],
                                            revap(U0[32:33, :], 1023)), "qV")
                        v.tensor_copy(wbufA[0:1, 0:1024],
                                      revap(U4s[32:33, 0:2048], 2047))
                        P.inc(v.tensor_copy(wbufB[0:1, 0:1024],
                                            revap(U4s[32:33, 0:2048], 1023)), "qV")
                        P.wait(v, "qV")
                    on_V(g9)

            scan_cf(0)
            scan_cf(1)

            def h1(v):
                P.wait(v, "qD")
                P.wait(v, "qV")
                P.inc(v.tensor_add(U4s[0:1, 0:2048], U1[0:1, :], U3[0:1, :]), "qV")
                v.tensor_add(U1[0:1, 0:1024], U2[0:1, 0:1024], wbufA[0:1, 0:1024])
                P.inc(v.tensor_add(U1[0:1, 1024:2048], U2[0:1, 1024:2048],
                                   wbufB[0:1, 0:1024]), "qV")
                P.wait(v, "qV")
                v.tensor_sub(U4s[0:1, 0:1024], U4s[0:1, 0:1024], glg[0:1, 0:1024])
                v.tensor_sub(U4s[0:1, 1024:2048], U4s[0:1, 1024:2048],
                             gfo[0:1, 0:1024])
                P.inc(v.tensor_sub(U1[0:1, :], U1[0:1, :], U0[0:1, :]), "qV")
                P.wait(v, "qV")
                v.tensor_tensor(U2[0:1, :], U4s[0:1, 0:2048], U4s[0:1, 0:2048],
                                ALU.mult)
                P.inc(v.tensor_tensor(U3[0:1, :], U1[0:1, :], U1[0:1, :],
                                      ALU.mult), "qV")
                P.wait(v, "qV")
                P.inc(v.tensor_add(U2[0:1, :], U2[0:1, :], U3[0:1, :]), "qV")
                P.wait(v, "qV")
                P.inc(v.reciprocal(U2[0:1, :], U2[0:1, :]), "qV")
                P.wait(v, "qV")
                P.inc(v.tensor_tensor(U3[0:1, :], U4s[0:1, 0:2048], U2[0:1, :],
                                      ALU.mult), "qV")
                v.tensor_tensor(U0[0:1, :], U1[0:1, :], U2[0:1, :], ALU.mult)
                P.inc(v.tensor_scalar(U0[0:1, :], U0[0:1, :], -1.0, None,
                                      ALU.mult), "qV")
                P.wait(v, "qV")
            on_V(h1)
            mh1 = P.mark("qV")

            def h2(sd):
                P.wait(sd, "qV", mh1)
                P.dma(sd.dma_start(out=scr[12:13, 0:2048], in_=U3[0:1, :]))
                P.dma(sd.dma_start(out=scr[13:14, 0:2048], in_=U0[0:1, :]))
            on_D(h2)
            mh2 = P.mark("qD")

            def h3(sd):
                P.wait(sd, "qD", mh2)
                P.dma(sd.dma_start(out=U4s[0:1, 0:2048], in_=scr[12:13, 0:2048]))
                P.dma(sd.dma_start(out=U4s[1:2, 0:2048], in_=scr[13:14, 0:2048]))
                P.dma(sd.dma_start(out=wbufA[0:2, 0:1024], in_=w_bk))
                P.dma(sd.dma_start(out=colx[:, 0:DC], in_=b_bk_c))
            on_D(h3)
            mh3 = P.mark("qD")

            for do in range(DC):
                for q4 in range(4):
                    pp = ps[(do * 4 + q4) % 2]

                    def hm(t, do=do, q4=q4, pp=pp):
                        P.wait(t, "qV")
                        P.wait(t, "qD", mh3)
                        P.inc(t.matmul(pp[:, 0:512],
                                       wbufA[0:2, do * 128:(do + 1) * 128],
                                       U4s[0:2, q4 * 512:(q4 + 1) * 512],
                                       start=True, stop=True), "qT")
                    on_T(hm)
                    mhm = P.mark("qT")

                    def he(v, do=do, q4=q4, pp=pp):
                        P.wait(v, "qT", mhm)
                        P.inc(v.tensor_scalar(U0[:, 0:512], pp[:, 0:512],
                                              colx[:, do:do + 1], None, ALU.add),
                              "qV")
                        P.wait(v, "qV")
                        v.tensor_add(S_n[do][:, q4 * 512:(q4 + 1) * 512],
                                     S_n[do][:, q4 * 512:(q4 + 1) * 512],
                                     U0[:, 0:512])
                        P.inc(v.tensor_add(S_x[do][:, q4 * 512:(q4 + 1) * 512],
                                           S_x[do][:, q4 * 512:(q4 + 1) * 512],
                                           U0[:, 0:512]), "qV")
                    on_V(he)

        if stage >= 9:
            emit_l1_bk()
        def emit_l1_heb():
            wbufA, wbufB = xt[0], xt[1]
            U0, U1, U2, U3, U4s = S_s[0], S_s[1], S_s[2], S_s[3], S_s[4]

            def xsp(sd):
                P.wait(sd, "qV")
                for j in range(DC):
                    P.dma(sd.dma_start(out=xls[j * 128:(j + 1) * 128, :],
                                       in_=S_n[j][:, :]))
            on_D(xsp)
            mxsp = P.mark("qD")

            def e0(t):
                P.wait(t, "qV")
                P.wait(t, "qS")
                ins = None
                for j in range(DC):
                    for q4 in range(4):
                        ins = t.matmul(ps_big[0:1, q4 * 512:(q4 + 1) * 512],
                                       c_1c[:, 0:1],
                                       S_n[j][:, q4 * 512:(q4 + 1) * 512],
                                       start=(j == 0), stop=(j == DC - 1),
                                       skip_group_check=True)
                P.inc(ins, "qT", 1)
            on_T(e0)
            me0 = P.mark("qT")

            def e0b(v):
                P.wait(v, "qT", me0)
                v.tensor_copy(glg[0:1, 0:1024], ps_big[0:1, 0:1024])
                P.inc(v.tensor_copy(gfo[0:1, 0:1024], ps_big[0:1, 1024:2048]), "qV")
                P.wait(v, "qV")
            on_V(e0b)
            me0b = P.mark("qV")

            for j in range(DC):
                def e1s(sc, j=j):
                    P.wait(sc, "qT")
                    if j == 0:
                        P.wait(sc, "qV", me0b)
                    P.inc(sc.activation(U0[:, :], S_n[j][:, :], AF.Square), "qS")
                on_S(e1s)
                me1s = P.mark("qS")

                def e1m(t, j=j):
                    P.wait(t, "qS", me1s)
                    ins = None
                    for q4 in range(4):
                        ins = t.matmul(ps_big[0:1, q4 * 512:(q4 + 1) * 512],
                                       c_1c[:, 0:1], U0[:, q4 * 512:(q4 + 1) * 512],
                                       start=(j == 0), stop=(j == DC - 1),
                                       skip_group_check=True)
                    P.inc(ins, "qT", 1)
                on_T(e1m)
            me1 = P.mark("qT")

            def e2(v):
                P.wait(v, "qT", me1)
                v.tensor_copy(U0[0:1, 0:1024], ps_big[0:1, 0:1024])
                P.inc(v.tensor_copy(U0[0:1, 1024:2048], ps_big[0:1, 1024:2048]),
                      "qV")
                P.wait(v, "qV")
                for sgt, qo in ((glg, 0), (gfo, 1024)):
                    P.inc(v.tensor_scalar(sgt[0:1, 0:1024], sgt[0:1, 0:1024],
                                          1.0 / D, None, ALU.mult), "qV")
                    P.wait(v, "qV")
                    P.inc(v.tensor_scalar(U0[0:1, qo:qo + 1024],
                                          U0[0:1, qo:qo + 1024],
                                          1.0 / D, None, ALU.mult), "qV")
                    P.wait(v, "qV")
                    P.inc(v.tensor_tensor(U1[0:1, 0:1024], sgt[0:1, 0:1024],
                                          sgt[0:1, 0:1024], ALU.mult), "qV")
                    P.wait(v, "qV")
                    P.inc(v.tensor_sub(U0[0:1, qo:qo + 1024],
                                       U0[0:1, qo:qo + 1024],
                                       U1[0:1, 0:1024]), "qV")
                    P.wait(v, "qV")
            on_V(e2)
            me2 = P.mark("qV")

            def e3(sc):
                P.wait(sc, "qV", me2)
                P.inc(sc.activation(U0[0:1, 0:2048], U0[0:1, 0:2048], AF.Sqrt,
                                    bias=float(EPS_LN)), "qS")
            on_S(e3)
            me3 = P.mark("qS")

            def e4(v):
                P.wait(v, "qS", me3)
                P.inc(v.reciprocal(U0[0:1, 0:2048], U0[0:1, 0:2048]), "qV")
                P.wait(v, "qV")
                for sgt, qo in ((glg, 0), (gfo, 1024)):
                    P.inc(v.tensor_tensor(sgt[0:1, 0:1024], sgt[0:1, 0:1024],
                                          U0[0:1, qo:qo + 1024], ALU.mult), "qV")
                    P.wait(v, "qV")
                    P.inc(v.tensor_scalar(sgt[0:1, 0:1024], sgt[0:1, 0:1024],
                                          -1.0, None, ALU.mult), "qV")
                    P.wait(v, "qV")
            on_V(e4)
            me4 = P.mark("qV")

            for q4 in range(4):
                def rb(t, q4=q4):
                    P.wait(t, "qV")
                    if q4 == 0:
                        P.wait(t, "qV", me4)
                    P.inc(t.matmul(ps[2][:, 0:512], c_1r[0:1, 0:128],
                                   U0[0:1, q4 * 512:(q4 + 1) * 512],
                                   start=True, stop=True), "qT")
                on_T(rb)
                mrb = P.mark("qT")

                def rbe(v, q4=q4):
                    P.wait(v, "qT", mrb)
                    P.inc(v.tensor_copy(U4s[:, q4 * 512:(q4 + 1) * 512],
                                        ps[2][:, 0:512]), "qV")
                on_V(rbe)
            mrsd = P.mark("qV")
            if sub < 2:
                return

            def e5z(g):
                P.wait(g, "qV", mrsd)
                P.inc(g.memset(own_p2[:, 0:1], 0.0), "qG")
            on_G(e5z)
            me5z = P.mark("qG")

            for kc in range(TC):
                seg = r_t1 if kc < 8 else denb
                off = (kc % 8) * 128

                def p1(t, kc=kc, seg=seg, off=off):
                    P.wait(t, "qV")
                    if kc == 0:
                        P.wait(t, "qG", me5z)
                    P.inc(t.matmul(ps[2][:, 0:128], c_1r[0:1, 0:128],
                                   seg[0:1, off:off + 128], start=True, stop=True),
                          "qT")
                on_T(p1)
                mp1 = P.mark("qT")

                def p2(v, kc=kc):
                    P.wait(v, "qT", mp1)
                    P.inc(v.tensor_scalar(U1[:, 0:128], ps[2][:, 0:128],
                                          float(DT_HEB), None, ALU.mult), "qV")
                    P.wait(v, "qV")
                on_V(p2)
                mp2 = P.mark("qV")

                def p3(t, kc=kc):
                    P.wait(t, "qV", mp2)
                    P.inc(t.transpose(ps[3][:, 0:128], U1[:, 0:128], c_id[:]), "qT")
                on_T(p3)
                mp3 = P.mark("qT")

                def p4(v, kc=kc):
                    P.wait(v, "qT", mp3)
                    P.inc(v.tensor_copy(mcol[:, kc:kc + 1], ps[3][:, 0:1]), "qV")
                    P.wait(v, "qV")
                on_V(p4)
                mp4 = P.mark("qV")

                def p5(t, kc=kc):
                    P.wait(t, "qV", mp4)
                    t.matmul(ps[2][:, 128:129], c_1r[0:1, 0:128],
                             own_p2[0:1, 0:1], start=True, stop=False)
                    t.matmul(ps[2][:, 128:129], c_u[:, :], mcol[:, kc:kc + 1],
                             start=False, stop=True)
                    P.inc(t.matmul(ps[3][0:1, 128:129], c_1c[:, 0:1],
                                   mcol[:, kc:kc + 1], start=True, stop=True), "qT")
                on_T(p5)
                mp5 = P.mark("qT")

                def p6(v, kc=kc):
                    P.wait(v, "qT", mp5)
                    v.tensor_copy(mcol[:, 16 + kc:17 + kc], ps[2][:, 128:129])
                    P.inc(v.tensor_add(own_p2[0:1, 0:1], own_p2[0:1, 0:1],
                                       ps[3][0:1, 128:129]), "qV")
                    P.wait(v, "qV")
                on_V(p6)
            mpc = P.mark("qV")

            def p7(sc):
                P.wait(sc, "qV", mpc)
                P.inc(sc.activation(own_p[:, 0:16], mcol[:, 16:32], AF.Exp), "qS")
            on_S(p7)
            mp7 = P.mark("qS")

            def p8(v):
                P.wait(v, "qS", mp7)
                P.inc(v.tensor_scalar(own_p[:, 0:16], own_p[:, 0:16], float(ETA),
                                      None, ALU.mult), "qV")
                P.wait(v, "qV")
            on_V(p8)

            for kc in range(TC):
                def q1(v, kc=kc):
                    P.wait(v, "qT")
                    P.inc(v.tensor_copy(U1[:, 0:128],
                                        bcast_free(mcol[:, 16 + kc:17 + kc], 128)),
                          "qV")
                on_V(q1)
                mq1 = P.mark("qV")

                def q2m(t, kc=kc):
                    P.wait(t, "qV", mq1)
                    P.wait(t, "qS")
                    P.inc(t.transpose(ps[2][:, 0:128], U1[:, 0:128], c_id[:]), "qT")
                on_T(q2m)
                mq2 = P.mark("qT")

                def q3(sc, kc=kc):
                    P.wait(sc, "qT", mq2)
                    P.inc(sc.activation(U0[:, kc * 128:(kc + 1) * 128],
                                        ps[2][:, 0:128], AF.Exp, scale=-1.0), "qS")
                on_S(q3)
            mexp = P.mark("qS")

            def q4wv(v):
                P.wait(v, "qS", mexp)
            on_V(q4wv)
            if sub < 3:
                return

            def e6(sd):
                P.wait(sd, "qV")
                P.dma(sd.dma_start(out=r_t1[0:1, 0:1024],
                                   in_=b_qkv_r[0:1, 0:1024]))
                P.dma(sd.dma_start(out=denb[0:1, 0:1024], in_=b_out_r))
                P.dma(sd.dma_start(out=own_sc[:, 0:8], in_=ln_c[:, 2 * DC:3 * DC]))
            on_D(e6)
            me6 = P.mark("qD")

            def btok_ap(q4):
                t0 = glg if q4 < 2 else gfo
                return t0[0:1, (q4 % 2) * 512:(q4 % 2) * 512 + 512]

            for pr in range(4):
                def cg0(sd, pr=pr):
                    P.wait(sd, "qT")
                    P.dma(sd.dma_start(out=own_p2[:, 8:16],
                                       in_=ln_c[:, 3 * DC:4 * DC]))
                on_D(cg0)
                mcg0 = P.mark("qD")

                for j in range(DC):
                    wb = wbufA if j % 2 == 0 else wbufB

                    def cg1(sd, pr=pr, j=j, wb=wb):
                        P.wait(sd, "qT")
                        for kk, base_c in enumerate((pr * 128, 512 + pr * 128,
                                                     1024 + pr * 128)):
                            P.dma(sd.dma_start(
                                out=wb[:, kk * 128:(kk + 1) * 128],
                                in_=w_qkv[j * 128:(j + 1) * 128,
                                          base_c:base_c + 128]))
                    on_D(cg1)
                    mcg1 = P.mark("qD")

                    def cg2(v, j=j, wb=wb):
                        P.wait(v, "qD", mcg1)
                        P.wait(v, "qD", mcg0)
                        P.inc(v.tensor_scalar(wb[:, 0:384], wb[:, 0:384],
                                              own_sc[:, j:j + 1], None, ALU.mult),
                              "qV")
                        P.wait(v, "qV")
                    on_V(cg2)
                    mcg2 = P.mark("qV")

                    def cg3(t, j=j, wb=wb):
                        P.wait(t, "qV", mcg2)
                        P.inc(t.matmul(ps[2][0:1, 0:384], c_1c[:, 0:1],
                                       wb[:, 0:384], start=(j == 0),
                                       stop=(j == DC - 1), skip_group_check=True),
                              "qT")
                    on_T(cg3)
                mcg3 = P.mark("qT")

                def cg4(v):
                    P.wait(v, "qT", mcg3)
                    P.inc(v.tensor_copy(wbufA[0:1, 384:768], ps[2][0:1, 0:384]),
                          "qV")
                    P.wait(v, "qV")
                on_V(cg4)

                for j in range(DC):
                    wb = wbufB

                    def cb1(sd, pr=pr, j=j):
                        P.wait(sd, "qT")
                        for kk, base_c in enumerate((pr * 128, 512 + pr * 128,
                                                     1024 + pr * 128)):
                            P.dma(sd.dma_start(
                                out=wbufB[:, kk * 128:(kk + 1) * 128],
                                in_=w_qkv[j * 128:(j + 1) * 128,
                                          base_c:base_c + 128]))
                    on_D(cb1)
                    mcb1 = P.mark("qD")

                    def cb2(v, j=j):
                        P.wait(v, "qD", mcb1)
                        P.inc(v.tensor_scalar(wbufB[:, 0:384], wbufB[:, 0:384],
                                              own_p2[:, 8 + j:9 + j], None,
                                              ALU.mult), "qV")
                        P.wait(v, "qV")
                    on_V(cb2)
                    mcb2 = P.mark("qV")

                    def cb3(t, j=j):
                        P.wait(t, "qV", mcb2)
                        P.inc(t.matmul(ps[3][0:1, 0:384], c_1c[:, 0:1],
                                       wbufB[:, 0:384], start=(j == 0),
                                       stop=(j == DC - 1), skip_group_check=True),
                              "qT")
                    on_T(cb3)
                mcb3 = P.mark("qT")

                def cb4(v):
                    P.wait(v, "qT", mcb3)
                    P.inc(v.tensor_copy(sqd[0:1, 576:960],
                                        ps[3][0:1, 0:384]), "qV")
                    P.wait(v, "qV")
                on_V(cb4)
                mcb4 = P.mark("qV")

                def cb5(sd, pr=pr):
                    P.wait(sd, "qV", mcb4)
                    P.dma(sd.dma_start(
                        out=wbufB[0:1, 0:128],
                        in_=b_qkv_r[0:1, 1024 + pr * 128:1024 + (pr + 1) * 128]))
                on_D(cb5)
                mcb5 = P.mark("qD")

                def cb6(v, pr=pr):
                    P.wait(v, "qD", mcb5)
                    v.tensor_add(sqd[0:1, 576:704], sqd[0:1, 576:704],
                                 r_t1[0:1, pr * 128:(pr + 1) * 128])
                    v.tensor_add(sqd[0:1, 704:832], sqd[0:1, 704:832],
                                 r_t1[0:1, 512 + pr * 128:512 + (pr + 1) * 128])
                    P.inc(v.tensor_add(sqd[0:1, 832:960], sqd[0:1, 832:960],
                                       wbufB[0:1, 0:128]), "qV")
                    P.wait(v, "qV")
                on_V(cb6)

                def cgq(part):
                    return wbufA[0:1, 384 + part * 128:384 + (part + 1) * 128]

                def cbq(part):
                    return sqd[0:1, 576 + part * 128:576 + (part + 1) * 128]

                for pas in range(2):
                    for q4 in range(4):
                        for j in range(DC):
                            wb = wbufA if j % 2 == 0 else wbufB

                            def w1(sd, pr=pr, j=j, wb=wb, pas=pas, q4=q4):
                                P.wait(sd, "qT")
                                if pr == 0 and pas == 0 and q4 == 0 and j == 0:
                                    P.wait(sd, "qD", mxsp)
                                P.dma(sd.dma_start(
                                    out=sqd[:, 0:512],
                                    in_=xls[j * 128:(j + 1) * 128,
                                            q4 * 512:(q4 + 1) * 512]))
                                if pas == 0:
                                    P.dma(sd.dma_start(
                                        out=wb[:, 0:128],
                                        in_=w_qkv[j * 128:(j + 1) * 128,
                                                  pr * 128:(pr + 1) * 128]))
                                    P.dma(sd.dma_start(
                                        out=wb[:, 128:256],
                                        in_=w_qkv[j * 128:(j + 1) * 128,
                                                  512 + pr * 128:
                                                  512 + (pr + 1) * 128]))
                                else:
                                    P.dma(sd.dma_start(
                                        out=wb[:, 128:256],
                                        in_=w_qkv[j * 128:(j + 1) * 128,
                                                  1024 + pr * 128:
                                                  1024 + (pr + 1) * 128]))
                            on_D(w1)
                            mw1 = P.mark("qD")

                            def sc1(v, j=j, wb=wb, pas=pas, q4=q4):
                                P.wait(v, "qD", mw1)
                                P.wait(v, "qT")
                                if pas == 0:
                                    P.inc(v.tensor_scalar(wb[:, 0:256],
                                                          wb[:, 0:256],
                                                          own_sc[:, j:j + 1],
                                                          None, ALU.mult), "qV")
                                else:
                                    P.inc(v.tensor_scalar(wb[:, 128:256],
                                                          wb[:, 128:256],
                                                          own_sc[:, j:j + 1],
                                                          None, ALU.mult), "qV")
                                P.inc(v.tensor_tensor(
                                    sqd[:, 0:512],
                                    sqd[:, 0:512],
                                    U4s[:, q4 * 512:(q4 + 1) * 512],
                                    ALU.mult), "qV")
                                P.wait(v, "qV")
                            on_V(sc1)
                            msc1 = P.mark("qV")

                            def m1(t, j=j, q4=q4, wb=wb, pas=pas, pr=pr):
                                P.wait(t, "qV", msc1)
                                ins = None
                                if pas == 0:
                                    if j == 0:
                                        t.matmul(ps_big[:, q4 * 512:(q4 + 1) * 512],
                                                 cbq(0), c_1r[0:1, 0:512],
                                                 start=True, stop=False,
                                                 skip_group_check=True)
                                        t.matmul(ps_big[:, q4 * 512:(q4 + 1) * 512],
                                                 cgq(0), btok_ap(q4),
                                                 start=False, stop=False,
                                                 skip_group_check=True)
                                    ins = t.matmul(
                                        ps_big[:, q4 * 512:(q4 + 1) * 512],
                                        wb[:, 0:128], sqd[:, 0:512],
                                        start=False, stop=(j == DC - 1),
                                        skip_group_check=True)
                                part = 1 if pas == 0 else 2
                                for ts4 in range(4):
                                    bt = btok_ap(q4)
                                    ba = apof(bt)
                                    btsub = bass.AP(tensor=ba.tensor,
                                                    offset=ba.offset + ts4 * 128,
                                                    ap=[ba.ap[0], [1, 128]])
                                    if j == 0:
                                        t.matmul(ps[ts4][:, 0:128],
                                                 c_1r[0:1, 0:128], cbq(part),
                                                 start=True, stop=False,
                                                 skip_group_check=True)
                                        t.matmul(ps[ts4][:, 0:128],
                                                 btsub, cgq(part),
                                                 start=False, stop=False,
                                                 skip_group_check=True)
                                    ins = t.matmul(
                                        ps[ts4][:, 0:128],
                                        sqd[:, ts4 * 128:(ts4 + 1) * 128],
                                        wb[:, 128:256],
                                        start=False, stop=(j == DC - 1),
                                        skip_group_check=True)
                                P.inc(ins, "qT", 1)
                            on_T(m1)
                        mj = P.mark("qT")

                        def ev1(v, q4=q4, pas=pas):
                            P.wait(v, "qT", mj)
                            ins = None
                            if pas == 0:
                                v.tensor_copy(U1[:, q4 * 512:(q4 + 1) * 512],
                                              ps_big[:, q4 * 512:(q4 + 1) * 512])
                                for ts4 in range(4):
                                    ins = v.tensor_copy(
                                        U2[:, q4 * 512 + ts4 * 128:
                                           q4 * 512 + (ts4 + 1) * 128],
                                        ps[ts4][:, 0:128])
                            else:
                                for ts4 in range(4):
                                    ins = v.tensor_copy(
                                        U3[:, q4 * 512 + ts4 * 128:
                                           q4 * 512 + (ts4 + 1) * 128],
                                        ps[ts4][:, 0:128])
                            P.inc(ins, "qV", 1)
                        on_V(ev1)

                def wo1(sd, pr=pr):
                    P.wait(sd, "qT")
                    P.dma(sd.dma_start(out=wbufB[:, :],
                                       in_=w_out[pr * 128:(pr + 1) * 128, :]))
                on_D(wo1)
                mwo1 = P.mark("qD")

                def hz(g, pr=pr):
                    P.wait(g, "qV")
                    P.inc(g.memset(sqd[:, 384:448], 0.0), "qG")
                on_G(hz)
                mhz = P.mark("qG")

                for c in range(TC):
                    def h1t(t, c=c):
                        P.wait(t, "qV")
                        if c == 0:
                            P.wait(t, "qG", mhz)
                            P.wait(t, "qD", mwo1)
                        P.inc(t.transpose(ps[3][:, 0:128],
                                          U2[:, c * 128:(c + 1) * 128], c_id[:]),
                              "qT")
                    on_T(h1t)
                    mh1t = P.mark("qT")

                    def h2v(v, c=c):
                        P.wait(v, "qT", mh1t)
                        P.inc(v.tensor_copy(sqd[:, 0:128], ps[3][:, 0:128]), "qV")
                        P.wait(v, "qV")
                    on_V(h2v)
                    mh2v = P.mark("qV")

                    def h3t(t, c=c):
                        P.wait(t, "qV", mh2v)
                        t.matmul(ps[0][:, 0:128], sqd[0:64, 0:128],
                                 U1[0:64, c * 128:(c + 1) * 128],
                                 start=True, stop=True)
                        P.inc(t.matmul(ps[1][:, 0:128], sqd[64:128, 0:128],
                                       U1[64:128, c * 128:(c + 1) * 128],
                                       start=True, stop=True), "qT")
                    on_T(h3t)
                    mh3t = P.mark("qT")

                    def h4v(v, c=c):
                        P.wait(v, "qT", mh3t)
                        for hh in range(2):
                            P.inc(v.tensor_scalar(
                                sqd[:, 128 + hh * 128:256 + hh * 128],
                                ps[hh][:, 0:128], own_p[:, c:c + 1], None,
                                ALU.mult), "qV")
                        P.wait(v, "qV")
                        ins = None
                        for hh in range(2):
                            ins = v.tensor_tensor(
                                sqd[:, 128 + hh * 128:256 + hh * 128],
                                sqd[:, 128 + hh * 128:256 + hh * 128],
                                c_u[:, :], ALU.mult)
                        P.inc(ins, "qV", 1)
                        P.inc(v.tensor_scalar(sqd[:, 448:576],
                                              U2[:, c * 128:(c + 1) * 128],
                                              own_p[:, c:c + 1], None, ALU.mult),
                              "qV")
                        P.wait(v, "qV")
                    on_V(h4v)
                    mh4v = P.mark("qV")

                    def h5t(t, c=c):
                        P.wait(t, "qV", mh4v)
                        for hh in range(2):
                            pp = ps[hh]
                            t.matmul(pp[0:64, 0:128],
                                     U3[:, c * 128 + hh * 64:
                                        c * 128 + (hh + 1) * 64],
                                     sqd[:, 128 + hh * 128:256 + hh * 128],
                                     start=True, stop=False,
                                     skip_group_check=True)
                            t.matmul(pp[0:64, 0:128],
                                     sqd[hh * 64:(hh + 1) * 64, 384:448],
                                     U1[hh * 64:(hh + 1) * 64,
                                        c * 128:(c + 1) * 128],
                                     start=False, stop=True,
                                     skip_group_check=True)
                        t.matmul(ps[2][0:64, 0:64], sqd[:, 448:512],
                                 U3[:, c * 128:c * 128 + 64],
                                 start=(c == 0), stop=(c == TC - 1),
                                 skip_group_check=True)
                        P.inc(t.matmul(ps_big[0:64, 0:64], sqd[:, 512:576],
                                       U3[:, c * 128 + 64:c * 128 + 128],
                                       start=(c == 0), stop=(c == TC - 1),
                                       skip_group_check=True), "qT")
                    on_T(h5t)
                    mh5t = P.mark("qT")

                    def h6v(v, c=c):
                        P.wait(v, "qT", mh5t)
                        v.tensor_copy(sqd[0:64, 576:704], ps[0][0:64, 0:128])
                        P.inc(v.tensor_copy(sqd[64:128, 576:704],
                                            ps[1][0:64, 0:128]), "qV")
                        v.tensor_copy(sqd[0:64, 384:448], ps[2][0:64, 0:64])
                        P.inc(v.tensor_copy(sqd[64:128, 384:448],
                                            ps_big[0:64, 0:64]), "qV")
                        P.wait(v, "qV")
                        P.inc(v.tensor_tensor(sqd[:, 704:832], sqd[:, 576:704],
                                              U0[:, c * 128:(c + 1) * 128],
                                              ALU.mult), "qV")
                        P.wait(v, "qV")
                    on_V(h6v)
                    mh6v = P.mark("qV")

                    for do in range(DC):
                        def d2m(t, c=c, do=do, pr=pr):
                            P.wait(t, "qV")
                            if do == 0:
                                P.wait(t, "qV", mh6v)
                            if pr == 0:
                                t.matmul(ps[3][:, 128:256],
                                         denb[0:1, do * 128:(do + 1) * 128],
                                         c_1r[0:1, 0:128],
                                         start=True, stop=False,
                                         skip_group_check=True)
                                P.inc(t.matmul(ps[3][:, 128:256],
                                               wbufB[:, do * 128:(do + 1) * 128],
                                               sqd[:, 704:832],
                                               start=False, stop=True,
                                               skip_group_check=True), "qT")
                            else:
                                P.inc(t.matmul(ps[3][:, 128:256],
                                               wbufB[:, do * 128:(do + 1) * 128],
                                               sqd[:, 704:832],
                                               start=True, stop=True), "qT")
                        on_T(d2m)
                        md2 = P.mark("qT")

                        def d2e(v, c=c, do=do):
                            P.wait(v, "qT", md2)
                            P.inc(v.tensor_copy(sqd[:, 832:960],
                                                ps[3][:, 128:256]), "qV")
                            P.wait(v, "qV")
                            v.tensor_add(S_n[do][:, c * 128:(c + 1) * 128],
                                         S_n[do][:, c * 128:(c + 1) * 128],
                                         sqd[:, 832:960])
                            P.inc(v.tensor_add(S_x[do][:, c * 128:(c + 1) * 128],
                                               S_x[do][:, c * 128:(c + 1) * 128],
                                               sqd[:, 832:960]), "qV")
                        on_V(d2e)


            if sub < 4:
                return
            def f6a(sd):
                P.wait(sd, "qV")
                P.dma(sd.dma_start(out=own_sc[:, 0:8], in_=ln_c[:, 4 * DC:5 * DC]))
                P.dma(sd.dma_start(out=own_p2[:, 8:16],
                                   in_=ln_c[:, 5 * DC:6 * DC]))
                P.dma(sd.dma_start(out=colx[:, 0:DC], in_=b_ff2_c))
            on_D(f6a)
            mf6a = P.mark("qD")

            def e0c(t):
                P.wait(t, "qV")
                P.wait(t, "qS")
                ins = None
                for j in range(DC):
                    for q4 in range(4):
                        ins = t.matmul(ps_big[0:1, q4 * 512:(q4 + 1) * 512],
                                       c_1c[:, 0:1],
                                       S_n[j][:, q4 * 512:(q4 + 1) * 512],
                                       start=(j == 0), stop=(j == DC - 1),
                                       skip_group_check=True)
                P.inc(ins, "qT", 1)
            on_T(e0c)
            me0c = P.mark("qT")

            def e0d(v):
                P.wait(v, "qT", me0c)
                v.tensor_copy(glg[0:1, 0:1024], ps_big[0:1, 0:1024])
                P.inc(v.tensor_copy(gfo[0:1, 0:1024], ps_big[0:1, 1024:2048]),
                      "qV")
                P.wait(v, "qV")
            on_V(e0d)
            me0d = P.mark("qV")

            for j in range(DC):
                def e1c(sc, j=j):
                    P.wait(sc, "qT")
                    if j == 0:
                        P.wait(sc, "qV", me0d)
                    P.inc(sc.activation(U0[:, :], S_n[j][:, :], AF.Square), "qS")
                on_S(e1c)
                me1c = P.mark("qS")

                def e1d(t, j=j):
                    P.wait(t, "qS", me1c)
                    ins = None
                    for q4 in range(4):
                        ins = t.matmul(ps_big[0:1, q4 * 512:(q4 + 1) * 512],
                                       c_1c[:, 0:1],
                                       U0[:, q4 * 512:(q4 + 1) * 512],
                                       start=(j == 0), stop=(j == DC - 1),
                                       skip_group_check=True)
                    P.inc(ins, "qT", 1)
                on_T(e1d)
            me1b = P.mark("qT")

            def e2b(v):
                P.wait(v, "qT", me1b)
                v.tensor_copy(U0[0:1, 0:1024], ps_big[0:1, 0:1024])
                P.inc(v.tensor_copy(U0[0:1, 1024:2048], ps_big[0:1, 1024:2048]),
                      "qV")
                P.wait(v, "qV")
                for sgt, qo in ((glg, 0), (gfo, 1024)):
                    P.inc(v.tensor_scalar(sgt[0:1, 0:1024], sgt[0:1, 0:1024],
                                          1.0 / D, None, ALU.mult), "qV")
                    P.wait(v, "qV")
                    P.inc(v.tensor_scalar(U0[0:1, qo:qo + 1024],
                                          U0[0:1, qo:qo + 1024],
                                          1.0 / D, None, ALU.mult), "qV")
                    P.wait(v, "qV")
                    P.inc(v.tensor_tensor(U1[0:1, 0:1024], sgt[0:1, 0:1024],
                                          sgt[0:1, 0:1024], ALU.mult), "qV")
                    P.wait(v, "qV")
                    P.inc(v.tensor_sub(U0[0:1, qo:qo + 1024],
                                       U0[0:1, qo:qo + 1024],
                                       U1[0:1, 0:1024]), "qV")
                    P.wait(v, "qV")
            on_V(e2b)
            me2b = P.mark("qV")

            def e3b(sc):
                P.wait(sc, "qV", me2b)
                P.inc(sc.activation(U0[0:1, 0:2048], U0[0:1, 0:2048], AF.Sqrt,
                                    bias=float(EPS_LN)), "qS")
            on_S(e3b)
            me3b = P.mark("qS")

            def e4b(v):
                P.wait(v, "qS", me3b)
                P.inc(v.reciprocal(U0[0:1, 0:2048], U0[0:1, 0:2048]), "qV")
                P.wait(v, "qV")
                for sgt, qo in ((glg, 0), (gfo, 1024)):
                    P.inc(v.tensor_tensor(sgt[0:1, 0:1024], sgt[0:1, 0:1024],
                                          U0[0:1, qo:qo + 1024], ALU.mult), "qV")
                    P.wait(v, "qV")
                    P.inc(v.tensor_scalar(sgt[0:1, 0:1024], sgt[0:1, 0:1024],
                                          -1.0, None, ALU.mult), "qV")
                    P.wait(v, "qV")
            on_V(e4b)
            me4b = P.mark("qV")

            for q4 in range(4):
                def rb2(t, q4=q4):
                    P.wait(t, "qV")
                    if q4 == 0:
                        P.wait(t, "qV", me4b)
                    P.inc(t.matmul(ps[2][:, 0:512], c_1r[0:1, 0:128],
                                   U0[0:1, q4 * 512:(q4 + 1) * 512],
                                   start=True, stop=True), "qT")
                on_T(rb2)
                mrb2 = P.mark("qT")

                def rbe2(v, q4=q4):
                    P.wait(v, "qT", mrb2)
                    P.inc(v.tensor_copy(U4s[:, q4 * 512:(q4 + 1) * 512],
                                        ps[2][:, 0:512]), "qV")
                on_V(rbe2)
            mrsd3 = P.mark("qV")

            for hc in range(32):
                for j in range(DC):
                    wb = wbufA if j % 2 == 0 else wbufB

                    def cf1(sd, hc=hc, j=j, wb=wb):
                        P.wait(sd, "qT")
                        P.wait(sd, "qV")
                        P.dma(sd.dma_start(out=wb[:, 0:128],
                                           in_=wff1t[hc, j, :, :]))
                    on_D(cf1)
                    mcf1 = P.mark("qD")

                    def cf2(v, j=j, wb=wb):
                        P.wait(v, "qD", mcf1)
                        if j < 2:
                            P.wait(v, "qV", mrsd3)
                        v.tensor_scalar(wb[:, 128:256], wb[:, 0:128],
                                        own_sc[:, j:j + 1], None, ALU.mult)
                        P.inc(v.tensor_scalar(wb[:, 256:384], wb[:, 0:128],
                                              own_p2[:, 8 + j:9 + j], None,
                                              ALU.mult), "qV")
                        P.wait(v, "qV")
                    on_V(cf2)
                    mcf2 = P.mark("qV")

                    def cf3(t, hc=hc, j=j, wb=wb):
                        P.wait(t, "qV", mcf2)
                        t.matmul(ps[2][0:1, 0:128], c_1c[:, 0:1], wb[:, 128:256],
                                 start=(j == 0), stop=(j == DC - 1),
                                 skip_group_check=True)
                        P.inc(t.matmul(ps[3][0:1, 0:128], c_1c[:, 0:1],
                                       wb[:, 256:384],
                                       start=(j == 0), stop=(j == DC - 1),
                                       skip_group_check=True), "qT")
                    on_T(cf3)
                mcf3 = P.mark("qT")

                def cf4(v, hc=hc):
                    P.wait(v, "qT", mcf3)
                    v.tensor_copy(sqd[0:1, 768:896], ps[2][0:1, 0:128])
                    P.inc(v.tensor_copy(sqd[0:1, 896:1024], ps[3][0:1, 0:128]),
                          "qV")
                    P.wait(v, "qV")
                on_V(cf4)
                mcf4 = P.mark("qV")

                def cf5(sd, hc=hc):
                    P.wait(sd, "qV", mcf4)
                    P.dma(sd.dma_start(out=wbufA[0:1, 0:128],
                                       in_=bff1_r[0:1, hc * 128:(hc + 1) * 128]))
                on_D(cf5)
                mcf5 = P.mark("qD")

                def cf6(v, hc=hc):
                    P.wait(v, "qD", mcf5)
                    P.inc(v.tensor_add(sqd[0:1, 896:1024], sqd[0:1, 896:1024],
                                       wbufA[0:1, 0:128]), "qV")
                    P.wait(v, "qV")
                on_V(cf6)
                mcf6 = P.mark("qV")

                def cf7(sd, hc=hc):
                    P.wait(sd, "qV", mcf6)
                    P.dma(sd.dma_start(out=scr[14:15, hc * 128:(hc + 1) * 128],
                                       in_=sqd[0:1, 768:896]))
                    P.dma(sd.dma_start(out=scr[15:16, hc * 128:(hc + 1) * 128],
                                       in_=sqd[0:1, 896:1024]))
                on_D(cf7)

            def cgw(v):
                P.wait(v, "qD")
            on_V(cgw)

            def cg1q(hc):
                return sqd[0:1, 768:896]

            def cb1q(hc):
                return sqd[0:1, 896:1024]

            for tt in range(8):
                for j in range(DC):
                    def xa1(v, tt=tt, j=j):
                        P.wait(v, "qT")
                        P.inc(v.tensor_tensor(U0[:, j * 256:(j + 1) * 256],
                                              S_n[j][:, tt * 256:(tt + 1) * 256],
                                              U4s[:, tt * 256:(tt + 1) * 256],
                                              ALU.mult), "qV")
                    on_V(xa1)
                mxa = P.mark("qV")

                def bt3(tt=tt):
                    t0 = glg if tt < 4 else gfo
                    return t0[0:1, (tt % 4) * 256:(tt % 4) * 256 + 256]

                for hc in range(32):
                    hps = ps[hc % 2]

                    def fl1(sd, tt=tt, hc=hc):
                        P.wait(sd, "qT")
                        P.wait(sd, "qS")
                        wbd = (wbufA if hc % 2 == 0 else wbufB)
                        for j in range(DC):
                            P.dma(sd.dma_start(out=wbd[:, j * 128:(j + 1) * 128],
                                               in_=wff1t[hc, j, :, :]))
                        P.dma(sd.dma_start(out=sqd[0:1, 768:896],
                                           in_=scr[14:15, hc * 128:(hc + 1) * 128]))
                        P.dma(sd.dma_start(out=sqd[0:1, 896:1024],
                                           in_=scr[15:16, hc * 128:(hc + 1) * 128]))
                    on_D(fl1)
                    mfl1 = P.mark("qD")

                    def fm1(t, tt=tt, hc=hc, hps=hps):
                        P.wait(t, "qD", mfl1)
                        P.wait(t, "qS")
                        if hc < 2:
                            P.wait(t, "qV", mxa)
                        wbd = (wbufA if hc % 2 == 0 else wbufB)
                        t.matmul(hps[:, 0:256], cb1q(hc), c_1r[0:1, 0:256],
                                 start=True, stop=False, skip_group_check=True)
                        t.matmul(hps[:, 0:256], cg1q(hc), bt3(),
                                 start=False, stop=False, skip_group_check=True)
                        ins = None
                        for j in range(DC):
                            ins = t.matmul(hps[:, 0:256],
                                           wbd[:, j * 128:(j + 1) * 128],
                                           U0[:, j * 256:(j + 1) * 256],
                                           start=False, stop=(j == DC - 1),
                                           skip_group_check=True)
                        P.inc(ins, "qT", 1)
                    on_T(fm1)
                    mfm1 = P.mark("qT")

                    def fg1(sc, hc=hc, hps=hps):
                        P.wait(sc, "qT", mfm1)
                        P.inc(sc.activation(sqd[:, 0:256], hps[:, 0:256],
                                            AF.Gelu_apprx_tanh), "qS")
                    on_S(fg1)
                    mfg1 = P.mark("qS")

                    def fl2(sd, tt=tt, hc=hc):
                        P.wait(sd, "qT")
                        wbd = (wbufB if hc % 2 == 0 else wbufA)
                        for do in range(DC):
                            P.dma(sd.dma_start(
                                out=wbd[:, do * 128:(do + 1) * 128],
                                in_=wff2t[hc, do, :, :]))
                    on_D(fl2)
                    mfl2 = P.mark("qD")

                    for do in range(DC):
                        ops = ps[2 + do % 2]

                        def fm2(t, hc=hc, do=do, ops=ops):
                            P.wait(t, "qS", mfg1)
                            P.wait(t, "qD", mfl2)
                            P.wait(t, "qV")
                            wbd = (wbufB if hc % 2 == 0 else wbufA)
                            P.inc(t.matmul(ops[:, 256:512],
                                           wbd[:, do * 128:(do + 1) * 128],
                                           sqd[:, 0:256],
                                           start=True, stop=True), "qT")
                        on_T(fm2)
                        mfm2 = P.mark("qT")

                        def fa2(v, hc=hc, do=do, ops=ops, tt=tt):
                            P.wait(v, "qT", mfm2)
                            dst = U2[:, do * 256:(do + 1) * 256]
                            if hc == 0:
                                P.inc(v.tensor_copy(dst, ops[:, 256:512]), "qV")
                            else:
                                P.inc(v.tensor_tensor(dst, dst, ops[:, 256:512],
                                                      ALU.add), "qV")
                        on_V(fa2)

                for do in range(DC):
                    def fe1(v, tt=tt, do=do):
                        P.wait(v, "qV")
                        P.inc(v.tensor_scalar(sqd[:, 256:512],
                                              U2[:, do * 256:(do + 1) * 256],
                                              colx[:, do:do + 1], None, ALU.add),
                              "qV")
                        P.wait(v, "qV")
                        P.inc(v.tensor_add(S_x[do][:, tt * 256:(tt + 1) * 256],
                                           S_x[do][:, tt * 256:(tt + 1) * 256],
                                           sqd[:, 256:512]), "qV")
                    on_V(fe1)

        if stage >= 10:
            emit_l1_heb()

        def emit_out():
            for kc in range(TC):
                for j in range(DC):
                    pp = ps[j % 2]

                    def g1t(t, kc=kc, j=j, pp=pp):
                        P.wait(t, "qV")
                        if j == 0:
                            P.wait(t, "qD")
                        P.inc(t.transpose(pp[:, 0:128],
                                          S_x[j][:, kc * 128:(kc + 1) * 128],
                                          c_id[:]), "qT")
                    on_T(g1t)
                    mg1t = P.mark("qT")

                    def g2v(v, kc=kc, j=j, pp=pp):
                        P.wait(v, "qT", mg1t)
                        P.inc(v.tensor_copy(sqd[:, j * 128:(j + 1) * 128],
                                            pp[:, 0:128]), "qV")
                    on_V(g2v)
                mg2v = P.mark("qV")

                def g3d(sd, kc=kc):
                    P.wait(sd, "qV", mg2v)
                    P.dma(sd.dma_start(out=out[kc * 128:(kc + 1) * 128, :],
                                       in_=sqd[:, :]))
                on_D(g3d)

        if stage >= 11:
            emit_out()


        def fin(s):
            P.wait(s, "qV")
            P.wait(s, "qT")
            P.wait(s, "qS")
            if dbg is not None and stage >= 1:
                P.dma(s.dma_start(out=dbg[:, 0:2048], in_=S_n[1][:, :]))
                P.dma(s.dma_start(out=dbg[:, 2048:4096], in_=S_x[1][:, :]))
                P.dma(s.dma_start(out=dbg[:, 4096:4096 + TC * 8], in_=olg[:, :]))
                P.dma(s.dma_start(out=dbg[0:1, 4224:4224 + 64], in_=r_cnt[0:1, 0:64]))
                P.dma(s.dma_start(out=dbg[0:1, 4288:4288 + 8], in_=r_pfx[0:1, 0:8]))
                P.dma(s.dma_start(out=dbg[:, 4352:4352 + TC], in_=own_slot[:, :]))
                P.dma(s.dma_start(out=dbg[:, 4368:4368 + TC], in_=own_sc[:, :]))
                P.dma(s.dma_start(out=dbg[:, 4384:4384 + TC], in_=own_p[:, :]))
                P.dma(s.dma_start(out=dbg[:, 4400:4400 + TC * 8], in_=ofo[:, :]))
                P.dma(s.dma_start(out=dbg[:, 4528:4528 + TC * 8], in_=own_lrk[:, :]))
                P.dma(s.dma_start(out=dbg[:, 7168:7168 + 1024], in_=gfo[:, 0:1024]))
                if stage >= 8:
                    P.dma(s.dma_start(out=dbg[:, 4656:4656 + 16], in_=own_slot[:, 0:16]))
                    P.dma(s.dma_start(out=dbg[:, 4672:4672 + 16], in_=own_lrk[:, 0:16]))
                    P.dma(s.dma_start(out=dbg[:, 4688:4688 + 16], in_=own_sc[:, 0:16]))
                    P.dma(s.dma_start(out=dbg[:, 4704:4704 + 16], in_=own_p2[:, 0:16]))
                    P.dma(s.dma_start(out=dbg[:, 4720:4720 + 128], in_=olg[:, 0:128]))
            if stage < 11:
                P.dma(s.dma_start(out=out[0:128, :], in_=xt[0][:, :]))
            P.wait(s, "qD")
        on_D(fin)

    return nc


def make_consts(core):
    mask = (np.arange(8) < core).astype(np.float32)
    return {
        "coremask": np.repeat(mask, 8)[None, :].copy(),
        "identc": np.eye(128, dtype=np.float32),
        "u128c": np.triu(np.ones((128, 128), np.float32)),
        "ones_rowc": np.ones((1, 512), np.float32),
        "ones_colc": np.ones((128, 1), np.float32),
        "iota_rowc": np.tile(np.arange(512, dtype=np.float32)[None, :], (128, 1)),
        "iota_colc": np.arange(128, dtype=np.float32)[:, None].copy(),
        "fvalsc": np.tile(np.array([[0.0, 1.0, 1e-5, 1e-6, 1e-8, 0.1, 0.5, -1.0,
                             1e-30, 0, 0, 0, 0, 0, 0, 0]], np.float32), (128, 1)),
    }


def col_layout(vec, width=128):
    v = np.asarray(vec, np.float32)
    return np.ascontiguousarray(v.reshape(-1, width).T)


def make_weight_inputs(inputs):
    g = lambda k: np.ascontiguousarray(np.asarray(inputs[k], np.float32))
    d = {}
    d["Wq"] = g("L0_attn_Wq"); d["Wk"] = g("L0_attn_Wk"); d["Wv"] = g("L0_attn_Wv")
    d["Wo"] = g("L0_attn_Wo"); d["Wg"] = g("L0_attn_Wg")
    d["spT"] = np.ascontiguousarray(g("L0_attn_splat_positions").T)
    d["ssc_row"] = g("L0_attn_splat_scales")[None, :]
    d["simp_row"] = g("L0_attn_splat_importance")[None, :]
    d["We"] = g("L0_attn_We"); d["be_row"] = g("L0_attn_be")[None, :]
    d["be_col"] = col_layout(g("L0_attn_be"))
    d["btT"] = np.ascontiguousarray(g("L0_attn_bt").T)
    d["Wt"] = g("L0_attn_Wt"); d["bt_r"] = g("L0_attn_bt")
    d["Wd"] = g("L0_attn_Wd"); d["bd_row"] = g("L0_attn_bd")[None, :]
    d["bg_col"] = col_layout(g("L0_attn_bg"))
    d["W1"] = g("L0_ffn_W1"); d["W2"] = g("L0_ffn_W2"); d["Wr"] = g("L0_ffn_Wr")
    ln = [g(f"L1_attn_ln{i}_{p}") for i in (1, 2, 3) for p in ("g", "b")]
    d["ln_col"] = np.concatenate([col_layout(v) for v in ln], axis=1)
    d["w_pot"] = g("L1_attn_w_pot"); d["b_pot_row"] = g("L1_attn_b_pot")[None, :]
    d["w_bk"] = g("L1_attn_w_bk"); d["b_bk_col"] = col_layout(g("L1_attn_b_bk"))
    d["w_qkv"] = g("L1_attn_w_qkv"); d["b_qkv_row"] = g("L1_attn_b_qkv")[None, :]
    d["w_out"] = g("L1_attn_w_out"); d["b_out_col"] = col_layout(g("L1_attn_b_out"))
    d["b_out_row"] = g("L1_attn_b_out")[None, :]
    d["w_ff1"] = g("L1_attn_w_ff1"); d["b_ff1_col"] = col_layout(g("L1_attn_b_ff1"))
    d["bff1_row"] = g("L1_attn_b_ff1")[None, :]
    wf1 = g("L1_attn_w_ff1")
    d["wff1t"] = np.ascontiguousarray(
        wf1.reshape(DC, 128, 32, 128).transpose(2, 0, 1, 3))
    wf2 = g("L1_attn_w_ff2")
    d["wff2t"] = np.ascontiguousarray(
        wf2.reshape(32, 128, DC, 128).transpose(0, 2, 1, 3))
    d["w_ff2"] = g("L1_attn_w_ff2"); d["b_ff2_col"] = col_layout(g("L1_attn_b_ff2"))
    d["rw"] = g("L1_ffn_router_w"); d["rsc_col"] = col_layout(g("L1_ffn_router_scale"))
    d["gwT"] = np.ascontiguousarray(np.transpose(g("L1_ffn_gating_w"), (0, 1, 3, 2)))
    d["lw"] = g("L1_ffn_linear_w")
    d["pes_row"] = g("L1_ffn_per_expert_scale")[None, :]
    return d


def kernel(**inputs):
    x = np.ascontiguousarray(np.asarray(inputs["L0_attn_x"], np.float32))
    x2d = x.reshape(B * N, D)
    wd = make_weight_inputs(inputs)
    nc = build_nc(stage=99)
    in_maps = []
    for c in range(B):
        m = {"xall": x2d, "xown": np.ascontiguousarray(x2d[c * N:(c + 1) * N])}
        m.update(make_consts(c))
        m.update(wd)
        in_maps.append(m)
    res = run_bass_kernel_spmd(nc, in_maps, core_ids=list(range(B)))
    outs = [res.results[c]["out"] for c in range(B)]
    return np.stack(outs, axis=0)


# version 04
# speedup vs baseline: 1.1137x; 1.1137x over previous
import contextlib
import numpy as np
import concourse.bass as bass
import concourse.mybir as mybir
from concourse.bass_utils import run_bass_kernel_spmd

F32 = mybir.dt.float32
AF = mybir.ActivationFunctionType
ALU = mybir.AluOpType
AX = mybir.AxisListType

SUB = 9
B, N, D = 8, 2048, 1024
T = N
DC = D // 128
TC = T // 128
GT = B * TC
H0, DH0 = 16, 64
S0, DB = 64, 512
E = 8
F0 = 4096
LCAP0 = 384
RT0 = E * LCAP0 // 128
H1, HD1 = 8, 64
FF1 = 4096
HID1 = 2048
EPS_LN = 1e-5
BIGF = 1.0e9
STD_RATIO = 0.6
SPLAT_RATIO = 0.4
ETA = 0.1
DT_HEB = 0.01
GAMMA_FLOOR = 0.1


class Prog:
    def __init__(self, nc, stack):
        self.nc = nc
        self.c = {}
        self.h = {}
        for name in ("qT", "qV", "qS", "qG", "qD", "qE"):
            self.h[name] = stack.enter_context(nc.semaphore(name))
            self.c[name] = 0

    def inc(self, instr, q, n=1):
        instr.then_inc(self.h[q], n)
        self.c[q] += n
        return instr

    def dma(self, instr, q="qD"):
        return self.inc(instr, q, 16)

    def wait(self, eng, q, val=None):
        v = self.c[q] if val is None else val
        if v > 0:
            eng.wait_ge(self.h[q], v)

    def mark(self, q):
        return self.c[q]


def apof(x):
    return x if isinstance(x, bass.AP) else x.ap()


def raw_ap(t, off_delta, dims):
    a = apof(t)
    return bass.AP(tensor=a.tensor, offset=a.offset + off_delta, ap=dims)


def bcast_free(ap2d, count):
    a = ap2d
    return bass.AP(tensor=a.tensor, offset=a.offset,
                   ap=list(a.ap) + [[0, count]])


def build_nc(stage=99, sub=99):
    nc = bass.Bass("TRN2", target_bir_lowering=False, detect_race_conditions=False)
    dti = lambda nm, shp: nc.dram_tensor(nm, shp, F32, kind="ExternalInput").ap()

    xall = dti("xall", [B * N, D])
    xown = dti("xown", [N, D])
    coremask = dti("coremask", [1, 64])
    ident = dti("identc", [128, 128])
    u128 = dti("u128c", [128, 128])
    onesr = dti("ones_rowc", [1, 512])
    onesc = dti("ones_colc", [128, 1])
    iotar = dti("iota_rowc", [128, 512])
    iotac = dti("iota_colc", [128, 1])
    fvals = dti("fvalsc", [128, 16])
    Wq = dti("Wq", [D, D]); Wk = dti("Wk", [D, D]); Wv = dti("Wv", [D, D])
    Wo = dti("Wo", [D, D]); Wg = dti("Wg", [D, D])
    spT = dti("spT", [D, S0])
    ssc_r = dti("ssc_row", [1, S0])
    simp_r = dti("simp_row", [1, S0])
    We = dti("We", [D, DB]); be_r = dti("be_row", [1, DB])
    be_c = dti("be_col", [128, 4]); btT = dti("btT", [DB, S0])
    Wt = dti("Wt", [S0, DB, DB]); bt_r = dti("bt_r", [S0, DB])
    Wd = dti("Wd", [DB, D]); bd_r = dti("bd_row", [1, D])
    bg_c = dti("bg_col", [128, DC])
    W1 = dti("W1", [E, D, F0])
    W2 = dti("W2", [E, F0, D])
    Wr = dti("Wr", [D, E])
    ln_c = dti("ln_col", [128, 6 * DC])
    w_pot = dti("w_pot", [D, 2]); b_pot_r = dti("b_pot_row", [1, 2])
    w_bk = dti("w_bk", [2, D]); b_bk_c = dti("b_bk_col", [128, DC])
    w_qkv = dti("w_qkv", [D, 3 * H1 * HD1]); b_qkv_r = dti("b_qkv_row", [1, 3 * H1 * HD1])
    w_out = dti("w_out", [H1 * HD1, D]); b_out_c = dti("b_out_col", [128, DC])
    b_out_r = dti("b_out_row", [1, D])
    w_ff1 = dti("w_ff1", [D, FF1]); b_ff1_c = dti("b_ff1_col", [128, FF1 // 128])
    bff1_r = dti("bff1_row", [1, FF1])
    wff1t = dti("wff1t", [32, DC, 128, 128])
    wff2t = dti("wff2t", [32, DC, 128, 128])
    w_ff2 = dti("w_ff2", [FF1, D]); b_ff2_c = dti("b_ff2_col", [128, DC])
    rw = dti("rw", [D, E])
    rsc_c = dti("rsc_col", [128, DC])
    gwT = dti("gwT", [E, 2, D, HID1])
    lw = dti("lw", [E, HID1, D])
    pes_r = dti("pes_row", [1, E])

    out = nc.dram_tensor("out", [N, D], F32, kind="ExternalOutput").ap()
    xn_tm_d = nc.dram_tensor("xn_tm_d", [N, D], F32, kind="Internal").ap()
    scr = nc.dram_tensor("scr", [20, 4096], F32, kind="Internal").ap()
    xls = nc.dram_tensor("xls", [1024, 2048], F32, kind="Internal").ap()
    dbg = nc.dram_tensor("dbg", [128, 8192], F32,
                         kind="ExternalOutput").ap() if stage < 99 else None

    stack = contextlib.ExitStack()
    sbuf = lambda nm, shp, dtype=F32: stack.enter_context(nc.sbuf_tensor(nm, shp, dtype))
    psum = lambda nm, shp: stack.enter_context(nc.psum_tensor(nm, shp, F32))

    with stack:
        S_x = [sbuf(f"S_x{j}", [128, T]) for j in range(DC)]
        S_n = [sbuf(f"S_n{j}", [128, T]) for j in range(DC)]
        S_s = [sbuf(f"S_s{j}", [128, T]) for j in range(5)]
        xt = [sbuf(f"xt{k}", [128, D]) for k in range(2)]
        sqd = sbuf("sqd", [128, D])
        c_id = sbuf("c_id", [128, 128])
        c_u = sbuf("c_u", [128, 128])
        c_1r = sbuf("c_1r", [1, 512])
        c_1c = sbuf("c_1c", [128, 1])
        c_ir = sbuf("c_ir", [128, 512])
        c_ic = sbuf("c_ic", [128, 1])
        c_msk = sbuf("c_msk", [1, 64])
        c_fv = sbuf("c_fv", [128, 16])
        s_wr = sbuf("s_wr", [128, DC * E])
        r_m = sbuf("r_m", [128, 1]); r_sq = sbuf("r_sq", [128, 1])
        r_v1 = sbuf("r_v1", [128, 1])
        r_rsd = sbuf("r_rsd", [128, 1])
        glg = sbuf("glg", [128, GT * 8])
        olg = sbuf("olg", [128, TC * 8])
        gfo = sbuf("gfo", [128, GT * 8])
        ofo = sbuf("ofo", [128, TC * 8])
        r_t1 = sbuf("r_t1", [128, GT * 8])
        r_mx = sbuf("r_mx", [128, GT])
        r_cnt = sbuf("r_cnt", [1, 64])
        r_pfx = sbuf("r_pfx", [1, 8])
        pfx_bc = sbuf("pfx_bc", [128, 8])
        eL_bc = sbuf("eL_bc", [128, 8])
        own_p = sbuf("own_p", [128, TC])
        own_lrk = sbuf("own_lrk", [128, TC * 8])
        own_slot = sbuf("own_slot", [128, TC])
        own_sc = sbuf("own_sc", [128, TC])
        carry = sbuf("carry", [1, 8])
        rowx = sbuf("rowx", [1, 512])
        colx = sbuf("colx", [128, 8])
        denb = sbuf("denb", [1, 1024])
        mcol = sbuf("mcol", [128, 32])
        own_p2 = sbuf("own_p2", [128, 16])

        ps_big = psum("ps_big", [128, 2048])
        ps = [psum(f"ps{k}", [128, 512]) for k in range(4)]

        block = stack.enter_context(nc.Block())
        P = Prog(nc, stack)

        def on_T(f):
            @block.tensor
            def _(t):
                f(t)

        def on_V(f):
            @block.vector
            def _(v):
                f(v)

        def on_S(f):
            @block.scalar
            def _(s):
                f(s)

        def on_D(f):
            @block.sync
            def _(s):
                f(s)

        def on_G(f):
            @block.gpsimd
            def _(g):
                f(g)

        def row_bcast(row_ap, width, dst, tmp_ps=None):
            pieces = []

            def ft(t):
                P.wait(t, "qV")
                ins = None
                for o in range(0, width, 512):
                    w = min(512, width - o)
                    pp = ps[(o // 512) % 2]
                    ra2 = apof(row_ap)
                    ins = t.matmul(pp[:, 0:w], c_1r[0:1, 0:128],
                                   bass.AP(tensor=ra2.tensor, offset=ra2.offset + o,
                                           ap=[ra2.ap[0], [1, w]]),
                                   start=True, stop=True)
                    pieces.append((o, w, pp))
                P.inc(ins, "qT", 1)
            on_T(ft)
            done = P.mark("qT")

            def fv(v):
                P.wait(v, "qT", done)
                for (o, w, pp) in pieces:
                    v.tensor_copy(dst[:, o:o + w], pp[:, 0:w])
                P.inc(v.nop(), "qV", 1)
            on_V(fv)

        def col_to_bcast(col_ap_list, dst):
            for k, cap in enumerate(col_ap_list):
                pp = ps[k % 2]

                def fv(v, cap=cap):
                    P.wait(v, "qT")
                    P.inc(v.tensor_copy(sqd[:, 0:128], bcast_free(cap, 128)), "qV")
                on_V(fv)
                mk = P.mark("qV")

                def ft(t, pp=pp):
                    P.wait(t, "qV", mk)
                    P.inc(t.transpose(pp[:, 0:128], sqd[:, 0:128], c_id[:]), "qT")
                on_T(ft)
                mt = P.mark("qT")

                def fe(v, pp=pp, k=k):
                    P.wait(v, "qT", mt)
                    P.inc(v.tensor_copy(dst[:, k * 128:(k + 1) * 128], pp[:, 0:128]), "qV")
                on_V(fe)

        FCONSTS = (0.0, 1.0, 1e-5, 1e-6, 1e-8, 0.1, 0.5, -1.0, 1e-30)
        for fi, fv in enumerate(FCONSTS):
            nc.const_aps.register(F32, fv, c_fv[:, fi:fi + 1])

        def phaseA(s):
            for dst, src in ((c_id, ident), (c_u, u128), (c_1r, onesr),
                             (c_1c, onesc), (c_ir, iotar), (c_ic, iotac),
                             (c_msk, coremask), (c_fv, fvals)):
                P.dma(s.dma_start(out=dst[:], in_=src))
            for j in range(DC):
                P.dma(s.dma_start(out=s_wr[:, j * E:(j + 1) * E],
                                  in_=Wr[j * 128:(j + 1) * 128, :]))
        on_D(phaseA)

        def tile_stats(buf, ld):
            def fs(sc):
                P.wait(sc, "qD", ld)
                P.wait(sc, "qV")
                P.inc(sc.activation(sqd[:], buf[:], AF.Square), "qS")
            on_S(fs)
            sq_done = P.mark("qS")

            def fv(v):
                P.wait(v, "qD", ld)
                P.inc(v.reduce_sum(r_m[:], buf[:], axis=AX.X), "qV")
                P.wait(v, "qS", sq_done)
                P.inc(v.reduce_sum(r_sq[:], sqd[:], axis=AX.X), "qV")
                P.wait(v, "qV")
                v.tensor_scalar(r_v1[:], r_m[:], r_m[:, 0:1], None, ALU.mult)
                P.inc(v.tensor_scalar(r_m[:], r_m[:], 1.0 / D, None, ALU.mult), "qV")
                P.wait(v, "qV")
                P.inc(v.scalar_tensor_tensor(r_rsd[:], r_v1[:], -1.0 / D, r_sq[:],
                                             ALU.mult, ALU.add), "qV")
            on_V(fv)
            var_done = P.mark("qV")

            def fr(sc):
                P.wait(sc, "qV", var_done)
                P.inc(sc.activation(r_rsd[:], r_rsd[:], AF.Sqrt,
                                    bias=float(EPS_LN), scale=1.0 / D), "qS")
            on_S(fr)
            sq2 = P.mark("qS")

            def frr(v):
                P.wait(v, "qS", sq2)
                P.inc(v.reciprocal(r_rsd[:], r_rsd[:]), "qV")
                P.wait(v, "qV")
            on_V(frr)
            return P.mark("qV")

        def emit_tile(k, own):
            buf = xt[k % 2]
            src = xown[k * 128:(k + 1) * 128, :] if own else xall[k * 128:(k + 1) * 128, :]

            def fd(s):
                P.wait(s, "qV")
                P.wait(s, "qT")
                P.wait(s, "qS")
                P.dma(s.dma_start(out=buf[:], in_=src))
            on_D(fd)
            if SUB < 1:
                return
            rsd_done = tile_stats(buf, P.mark("qD"))
            if SUB < 2:
                return

            def fxn(v):
                P.wait(v, "qV", rsd_done)
                P.inc(v.tensor_scalar(sqd[:], buf[:], r_m[:, 0:1], r_rsd[:, 0:1],
                                      ALU.subtract, ALU.mult), "qV")
            on_V(fxn)
            xn_done = P.mark("qV")
            if SUB < 3:
                return

            if own:
                def fst(s):
                    P.wait(s, "qV", xn_done)
                    P.dma(s.dma_start(out=xn_tm_d[k * 128:(k + 1) * 128, :], in_=sqd[:]))
                on_D(fst)
            if SUB < 4:
                return

            srcs = [(sqd, True)] + ([(buf, False)] if own else [])
            for si, (sb_src, is_xn) in enumerate(srcs):
                for j in range(DC):
                    pt = ps[(si * DC + j) % 4]

                    def ft(t, j=j, pt=pt, sb_src=sb_src):
                        P.wait(t, "qV")
                        P.inc(t.transpose(pt[:, 0:128],
                                          sb_src[:, j * 128:(j + 1) * 128], c_id[:]), "qT")
                    on_T(ft)
                    mt = P.mark("qT")

                    def fe(v, j=j, pt=pt, k=k, own=own, is_xn=is_xn):
                        P.wait(v, "qT", mt)
                        if is_xn:
                            dst = S_n[j][:, k * 128:(k + 1) * 128] if own \
                                else S_s[0][:, j * 128:(j + 1) * 128]
                        else:
                            dst = S_x[j][:, k * 128:(k + 1) * 128]
                        P.inc(v.tensor_copy(dst, pt[:, 0:128]), "qV")
                    on_V(fe)
            trs_done = P.mark("qV")
            if SUB < 5:
                return
            if SUB < 6:
                return

            def fl(t, k=k, own=own):
                P.wait(t, "qV", trs_done)
                ins = None
                for j in range(DC):
                    src_c = S_n[j][:, k * 128:(k + 1) * 128] if own \
                        else S_s[0][:, j * 128:(j + 1) * 128]
                    ins = t.matmul(ps_big[:, 0:8], src_c, s_wr[:, j * E:(j + 1) * E],
                                   start=(j == 0), stop=(j == DC - 1))
                P.inc(ins, "qT", 1)
            on_T(fl)
            lg_done = P.mark("qT")

            def fc(v, k=k, own=own):
                P.wait(v, "qT", lg_done)
                dst = olg[:, k * 8:(k + 1) * 8] if own else glg[:, k * 8:(k + 1) * 8]
                P.inc(v.tensor_copy(dst, ps_big[:, 0:8]), "qV")
            on_V(fc)

        if stage >= 1:
            for k in range(TC):
                emit_tile(k, own=True)
        if stage >= 2:
            for k in range(GT):
                emit_tile(k, own=False)

        if dbg is not None and stage >= 2:
            def dbg_early(sd):
                P.wait(sd, "qV")
                P.wait(sd, "qT")
                P.dma(sd.dma_start(out=dbg[:, 5120:5120 + 1024], in_=glg[:, 0:1024]))
                P.dma(sd.dma_start(out=dbg[:, 6144:6144 + 1024], in_=S_s[0][:, 0:1024]))
            on_D(dbg_early)

        if stage >= 3:
            def argmax_fo(v, lg, fo, mxout, tmp, cbuf, nt):
                lg3 = lg[:, 0:nt * 8].rearrange("p (t e) -> p t e", e=8)
                fo3 = fo[:, 0:nt * 8].rearrange("p (t e) -> p t e", e=8)
                tm3 = tmp[:, 0:nt * 8].rearrange("p (t e) -> p t e", e=8)
                c3 = cbuf[:, 0:nt * 8].rearrange("p (t e) -> p t e", e=8)
                P.inc(v.reduce_max(mxout[:, 0:nt], lg3, axis=AX.X), "qV")
                P.wait(v, "qV")
                ma = apof(mxout)
                mxb = bass.AP(tensor=ma.tensor, offset=ma.offset,
                              ap=[ma.ap[0], [1, nt], [0, 8]])
                v.tensor_tensor(fo3, lg3, mxb, ALU.is_ge)
                v.tensor_copy(tm3[:, :, 0:1], fo3[:, :, 0:1])
                P.inc(v.tensor_tensor(tm3[:, :, 1:8], fo3[:, :, 1:8],
                                      fo3[:, :, 0:7], ALU.add), "qV")
                P.wait(v, "qV")
                v.tensor_copy(c3[:, :, 0:2], tm3[:, :, 0:2])
                P.inc(v.tensor_tensor(c3[:, :, 2:8], tm3[:, :, 2:8],
                                      tm3[:, :, 0:6], ALU.add), "qV")
                P.wait(v, "qV")
                v.tensor_copy(tm3[:, :, 0:4], c3[:, :, 0:4])
                P.inc(v.tensor_tensor(tm3[:, :, 4:8], c3[:, :, 4:8],
                                      c3[:, :, 0:4], ALU.add), "qV")
                P.wait(v, "qV")
                P.inc(v.tensor_scalar(tmp[:, 0:nt * 8], tmp[:, 0:nt * 8],
                                      1.0, None, ALU.is_equal), "qV")
                P.wait(v, "qV")
                v.tensor_mul(fo[:, 0:nt * 8], fo[:, 0:nt * 8], tmp[:, 0:nt * 8])

            def b3_global(v):
                P.wait(v, "qV")
                argmax_fo(v, glg, gfo, r_mx, r_t1, glg, GT)
                P.inc(v.tensor_copy(r_mx[:, 0:1], r_mx[:, 0:1]), "qV")
            on_V(b3_global)
            gfo_done = P.mark("qV")

            for cb in range(8):
                def ft(t, cb=cb):
                    P.wait(t, "qV")
                    P.inc(t.matmul(ps[0][:1, 0:128], c_1c[:, 0:1],
                                   gfo[:, cb * 128:(cb + 1) * 128],
                                   start=True, stop=True), "qT")
                on_T(ft)
                mt = P.mark("qT")

                def fv(v, cb=cb):
                    P.wait(v, "qT", mt)
                    P.inc(v.tensor_copy(rowx[0:1, 0:128], ps[0][0:1, 0:128]), "qV")
                    P.wait(v, "qV")
                    P.inc(v.tensor_add(rowx[0:1, 0:64], rowx[0:1, 0:64], rowx[0:1, 64:128]), "qV")
                    P.wait(v, "qV")
                    P.inc(v.tensor_add(rowx[0:1, 0:32], rowx[0:1, 0:32], rowx[0:1, 32:64]), "qV")
                    P.wait(v, "qV")
                    P.inc(v.tensor_add(rowx[0:1, 0:16], rowx[0:1, 0:16], rowx[0:1, 16:32]), "qV")
                    P.wait(v, "qV")
                    P.inc(v.tensor_add(rowx[0:1, 0:8], rowx[0:1, 0:8], rowx[0:1, 8:16]), "qV")
                    P.wait(v, "qV")
                    P.inc(v.tensor_copy(r_cnt[0:1, cb * 8:(cb + 1) * 8], rowx[0:1, 0:8]), "qV")
                on_V(fv)

            def b3_prefix(v):
                P.wait(v, "qV")
                P.inc(v.tensor_mul(r_cnt[0:1, :], r_cnt[0:1, :], c_msk[0:1, :]), "qV")
                P.wait(v, "qV")
                P.inc(v.tensor_add(r_cnt[0:1, 0:32], r_cnt[0:1, 0:32], r_cnt[0:1, 32:64]), "qV")
                P.wait(v, "qV")
                P.inc(v.tensor_add(r_cnt[0:1, 0:16], r_cnt[0:1, 0:16], r_cnt[0:1, 16:32]), "qV")
                P.wait(v, "qV")
                P.inc(v.tensor_add(r_cnt[0:1, 0:8], r_cnt[0:1, 0:8], r_cnt[0:1, 8:16]), "qV")
                P.wait(v, "qV")
                P.inc(v.tensor_copy(r_pfx[0:1, :], r_cnt[0:1, 0:8]), "qV")
            on_V(b3_prefix)
            row_bcast(r_pfx[0:1, 0:8], 8, pfx_bc)

            def b3_own1(v):
                argmax_fo(v, olg, ofo, r_mx, r_t1, r_t1[:, 512:1024], TC)
                lg3 = olg[:, 0:TC * 8].rearrange("p (t e) -> p t e", e=8)
                t13 = r_t1[:, 0:TC * 8].rearrange("p (t e) -> p t e", e=8)
                ra = apof(r_mx)
                mxb = bass.AP(tensor=ra.tensor, offset=ra.offset,
                              ap=[ra.ap[0], [1, TC], [0, 8]])
                P.inc(v.tensor_tensor(t13, lg3, mxb, ALU.subtract), "qV")
            on_V(b3_own1)
            own1_done = P.mark("qV")

            def b3_exp(sc):
                P.wait(sc, "qV", own1_done)
                P.inc(sc.activation(r_t1[:, 0:TC * 8], r_t1[:, 0:TC * 8], AF.Exp), "qS")
            on_S(b3_exp)
            exp_done = P.mark("qS")

            def b3_p(v):
                P.wait(v, "qS", exp_done)
                t13 = r_t1[:, 0:TC * 8].rearrange("p (t e) -> p t e", e=8)
                P.inc(v.reduce_sum(own_p[:, 0:TC], t13, axis=AX.X), "qV")
                P.wait(v, "qV")
                P.inc(v.reciprocal(own_p[:, 0:TC], own_p[:, 0:TC]), "qV")
            on_V(b3_p)

            def zcarry(g):
                g.memset(carry[:], 0.0)
                P.inc(g.nop(), "qG", 1)
            on_G(zcarry)
            carry_z = P.mark("qG")

            for k in range(TC):
                def ft(t, k=k):
                    P.wait(t, "qV")
                    if k == 0:
                        P.wait(t, "qG", carry_z)
                    t.matmul(ps[0][:, 0:8], c_1r[0:1, 0:128], carry[0:1, 0:8],
                             start=True, stop=False)
                    t.matmul(ps[0][:, 0:8], c_u[:, :], ofo[:, k * 8:(k + 1) * 8],
                             start=False, stop=True)
                    P.inc(t.matmul(ps[1][:1, 0:8], c_1c[:, 0:1], ofo[:, k * 8:(k + 1) * 8],
                                   start=True, stop=True), "qT")
                on_T(ft)
                mt = P.mark("qT")

                def fv(v, k=k):
                    P.wait(v, "qT", mt)
                    v.tensor_sub(own_lrk[:, k * 8:(k + 1) * 8], ps[0][:, 0:8],
                                 ofo[:, k * 8:(k + 1) * 8])
                    P.inc(v.tensor_add(carry[0:1, 0:8], carry[0:1, 0:8],
                                       ps[1][0:1, 0:8]), "qV")
                on_V(fv)

            def b3_slot2(v):
                P.wait(v, "qV")
                lrk = own_lrk[:, 0:TC * 8]
                lrk3 = lrk.rearrange("p (t e) -> p t e", e=8)
                t1 = r_t1[:, 0:TC * 8]
                t13 = t1.rearrange("p (t e) -> p t e", e=8)
                t2 = glg[:, 0:TC * 8]
                t23 = t2.rearrange("p (t e) -> p t e", e=8)
                v.tensor_scalar(eL_bc[:, 0:8], c_ir[:, 0:8], float(LCAP0), None, ALU.mult)
                pa = apof(pfx_bc)
                pfb = bass.AP(tensor=pa.tensor, offset=pa.offset,
                              ap=[pa.ap[0], [0, TC], [1, 8]])
                v.tensor_tensor(t13, lrk3, pfb, ALU.add)
                v.tensor_scalar(t1, t1, 2048.0, None, ALU.is_lt)
                v.tensor_mul(t1, t1, ofo[:, 0:TC * 8])
                v.tensor_scalar(t2, lrk, float(LCAP0), None, ALU.is_lt)
                v.tensor_mul(t1, t1, t2)
                P.inc(v.reduce_sum(own_sc[:, 0:TC],
                             t1.rearrange("p (t e) -> p t e", e=8), axis=AX.X), "qV")
                P.wait(v, "qV")
                ea = apof(eL_bc)
                eLb = bass.AP(tensor=ea.tensor, offset=ea.offset,
                              ap=[ea.ap[0], [0, TC], [1, 8]])
                v.tensor_tensor(t23, lrk3, eLb, ALU.add)
                v.tensor_mul(t2, t2, t1)
                P.inc(v.reduce_sum(own_slot[:, 0:TC],
                             t2.rearrange("p (t e) -> p t e", e=8), axis=AX.X), "qV")
                P.wait(v, "qV")
                P.inc(v.tensor_scalar(r_mx[:, 0:TC], own_sc[:, 0:TC], -BIGF, BIGF,
                                      ALU.mult, ALU.add), "qV")
                P.wait(v, "qV")
                v.tensor_add(own_slot[:, 0:TC], own_slot[:, 0:TC], r_mx[:, 0:TC])
                P.inc(v.tensor_mul(own_sc[:, 0:TC], own_sc[:, 0:TC], own_p[:, 0:TC]), "qV")
            on_V(b3_slot2)

        A_q, A_k, A_v, A_e, A_std = S_s[0], S_s[1], S_s[2], S_s[3], S_s[4]

        def emit_attn_group(g):
            wbufA, wbufB = xt[0], xt[1]

            def fw(sd):
                P.wait(sd, "qT")
                for j in range(DC):
                    P.dma(sd.dma_start(out=wbufA[:, j * 128:(j + 1) * 128],
                                       in_=Wq[j * 128:(j + 1) * 128,
                                              g * 128:(g + 1) * 128]))
                    P.dma(sd.dma_start(out=wbufB[:, j * 128:(j + 1) * 128],
                                       in_=Wk[j * 128:(j + 1) * 128,
                                              g * 128:(g + 1) * 128]))
            on_D(fw)
            wld = P.mark("qD")

            for Wsrc, dst, scale in ((0, A_q, 0.125), (1, A_k, None)):
                for q4 in range(4):
                    pp = ps[q4 % 2]

                    def fmm(t, q4=q4, pp=pp, Wsrc=Wsrc):
                        P.wait(t, "qV")
                        P.wait(t, "qD", wld)
                        wb = xt[Wsrc]
                        ins = None
                        for hh in range(2):
                            for j in range(DC):
                                ins = t.matmul(
                                    pp[hh * 64:(hh + 1) * 64, 0:512],
                                    wb[:, j * 128 + hh * 64:j * 128 + (hh + 1) * 64],
                                    S_n[j][:, q4 * 512:(q4 + 1) * 512],
                                    start=(j == 0), stop=(j == DC - 1),
                                    skip_group_check=True)
                        P.inc(ins, "qT", 1)
                    on_T(fmm)
                    mt = P.mark("qT")

                    def fev(v, q4=q4, pp=pp, dst=dst, scale=scale):
                        P.wait(v, "qT", mt)
                        if scale is None:
                            P.inc(v.tensor_copy(dst[:, q4 * 512:(q4 + 1) * 512],
                                                pp[:, 0:512]), "qV")
                        else:
                            P.inc(v.tensor_scalar(dst[:, q4 * 512:(q4 + 1) * 512],
                                                  pp[:, 0:512], scale, None,
                                                  ALU.mult), "qV")
                    on_V(fev)

            def fw2(sd):
                P.wait(sd, "qT")
                for j in range(DC):
                    P.dma(sd.dma_start(out=wbufA[:, j * 128:(j + 1) * 128],
                                       in_=Wv[j * 128:(j + 1) * 128,
                                              g * 128:(g + 1) * 128]))
                P.dma(sd.dma_start(out=wbufB[:, :],
                                   in_=Wo[g * 128:(g + 1) * 128, :]))
            on_D(fw2)
            wld2 = P.mark("qD")

            for kc in range(TC):
                pp = ps[kc % 2]

                def fvm(t, kc=kc, pp=pp):
                    P.wait(t, "qV")
                    P.wait(t, "qD", wld2)
                    ins = None
                    for j in range(DC):
                        ins = t.matmul(pp[:, 0:128],
                                       S_n[j][:, kc * 128:(kc + 1) * 128],
                                       wbufA[:, j * 128:(j + 1) * 128],
                                       start=(j == 0), stop=(j == DC - 1))
                    P.inc(ins, "qT", 1)
                on_T(fvm)
                mt = P.mark("qT")

                def fve(v, kc=kc, pp=pp):
                    P.wait(v, "qT", mt)
                    P.inc(v.tensor_copy(A_v[:, kc * 128:(kc + 1) * 128],
                                        pp[:, 0:128]), "qV")
                on_V(fve)

            def fcs(v):
                P.inc(v.reduce_max(colx[:, 0:1], A_q[:, :], axis=AX.X,
                                   apply_absolute_value=True), "qV")
                P.inc(v.reduce_max(colx[:, 1:2], A_k[:, :], axis=AX.X,
                                   apply_absolute_value=True), "qV")
            on_V(fcs)
            mcol = P.mark("qV")

            def fct(t):
                P.wait(t, "qV", mcol)
                P.inc(t.transpose(ps[2][0:2, 0:128], colx[:, 0:2], c_id[:]), "qT")
            on_T(fct)
            mtt = P.mark("qT")

            def fcm(v):
                P.wait(v, "qT", mtt)
                P.inc(v.tensor_copy(glg[0:2, 0:128], ps[2][0:2, 0:128]), "qV")
                P.wait(v, "qV")
                P.inc(v.reduce_max(glg[0:2, 128:129], glg[0:2, 0:64], axis=AX.X), "qV")
                P.inc(v.reduce_max(glg[0:2, 129:130], glg[0:2, 64:128], axis=AX.X), "qV")
                P.inc(v.tensor_copy(glg[0:1, 140:142], glg[0:1, 128:130]), "qV")
            on_V(fcm)
            mshuf0 = P.mark("qV")

            def fshuf(gg):
                P.wait(gg, "qV", mshuf0)
                P.dma(gg.dma_start(out=glg[0:1, 132:134], in_=glg[1:2, 128:130]), "qG")
            on_G(fshuf)
            mshuf = P.mark("qG")

            def fcm2(v):
                P.wait(v, "qG", mshuf)
                P.inc(v.tensor_tensor(glg[0:1, 130:132], glg[0:1, 140:142],
                                      glg[0:1, 132:134], ALU.mult), "qV")
                P.wait(v, "qV")
                P.inc(v.tensor_scalar(glg[0:1, 130:132], glg[0:1, 130:132],
                                      -64.0, None, ALU.mult), "qV")
            on_V(fcm)
            mneg = P.mark("qV")

            def fcb(t):
                P.wait(t, "qV", mneg)
                P.inc(t.matmul(ps[2][:, 0:2], c_1r[0:1, 0:128], glg[0:1, 130:132],
                               start=True, stop=True), "qT")
            on_T(fcb)
            mbb = P.mark("qT")

            def fcbe(v):
                P.wait(v, "qT", mbb)
                P.inc(v.tensor_copy(colx[:, 2:4], ps[2][:, 0:2]), "qV")
                P.wait(v, "qV")
            on_V(fcbe)
            prep_done = P.mark("qV")

            def den_ap(hh, q4):
                seg = hh * 2 + q4 // 2
                cc = (q4 % 2) * 512
                t = (glg, gfo, r_t1, denb)[seg]
                return t[0:1, cc:cc + 512]

            for hh in range(2):
                for q4 in range(4):
                    for kc in range(TC):
                        pp = ps[kc % 2]

                        def fsc(t, hh=hh, q4=q4, kc=kc, pp=pp):
                            P.wait(t, "qS")
                            if hh == 0 and q4 == 0 and kc == 0:
                                P.wait(t, "qV", prep_done)
                            P.inc(t.matmul(
                                pp[:, 0:512],
                                A_k[hh * 64:(hh + 1) * 64, kc * 128:(kc + 1) * 128],
                                A_q[hh * 64:(hh + 1) * 64, q4 * 512:(q4 + 1) * 512],
                                start=True, stop=True), "qT")
                        on_T(fsc)
                        msc = P.mark("qT")

                        def fex(sc, hh=hh, pp=pp, kc=kc):
                            P.wait(sc, "qT", msc)
                            if kc > 1:
                                P.wait(sc, "qT")
                            P.inc(sc.activation(A_e[:, (kc % 2) * 512:(kc % 2 + 1) * 512],
                                                pp[:, 0:512], AF.Exp,
                                                bias=colx[:, 2 + hh:3 + hh]), "qS")
                        on_S(fex)
                        mex = P.mark("qS")

                        def fav(t, hh=hh, q4=q4, kc=kc):
                            P.wait(t, "qS", mex)
                            esub = A_e[:, (kc % 2) * 512:(kc % 2 + 1) * 512]
                            t.matmul(ps_big[hh * 64:(hh + 1) * 64,
                                            q4 * 512:(q4 + 1) * 512],
                                     A_v[:, kc * 128 + hh * 64:kc * 128 + (hh + 1) * 64],
                                     esub,
                                     start=(kc == 0), stop=(kc == TC - 1),
                                     skip_group_check=True)
                            P.inc(t.matmul(ps[2 + hh][0:1, 0:512], c_1c[:, 0:1], esub,
                                           start=(kc == 0), stop=(kc == TC - 1),
                                           skip_group_check=True), "qT")
                        on_T(fav)
                    mdn = P.mark("qT")

                    def fdn(v, hh=hh, q4=q4):
                        P.wait(v, "qT", mdn)
                        P.inc(v.tensor_copy(den_ap(hh, q4),
                                            ps[2 + hh][0:1, 0:512]), "qV")
                    on_V(fdn)

            def fdr(v):
                P.wait(v, "qV")
                v.reciprocal(glg[0:1, 0:1024], glg[0:1, 0:1024])
                v.reciprocal(gfo[0:1, 0:1024], gfo[0:1, 0:1024])
                v.reciprocal(r_t1[0:1, 0:1024], r_t1[0:1, 0:1024])
                P.inc(v.reciprocal(denb[0:1, 0:1024], denb[0:1, 0:1024]), "qV")
                P.wait(v, "qV")
            on_V(fdr)
            mrec = P.mark("qV")

            for hh in range(2):
                for q4 in range(4):
                    def fbc(t, hh=hh, q4=q4):
                        P.wait(t, "qV", mrec)
                        P.wait(t, "qV")
                        P.inc(t.matmul(ps[0][hh * 64:(hh + 1) * 64, 0:512],
                                       c_1r[0:1, 0:64],
                                       den_ap(hh, q4),
                                       start=True, stop=True), "qT")
                    on_T(fbc)
                    mbc = P.mark("qT")

                    def fst(v, hh=hh, q4=q4):
                        P.wait(v, "qT", mbc)
                        P.inc(v.tensor_copy(A_e[hh * 64:(hh + 1) * 64, 0:512],
                                            ps[0][hh * 64:(hh + 1) * 64, 0:512]), "qV")
                        P.wait(v, "qV")
                        P.inc(v.tensor_tensor(
                            A_std[hh * 64:(hh + 1) * 64, q4 * 512:(q4 + 1) * 512],
                            ps_big[hh * 64:(hh + 1) * 64, q4 * 512:(q4 + 1) * 512],
                            A_e[hh * 64:(hh + 1) * 64, 0:512], ALU.mult), "qV")
                    on_V(fst)
            std_done = P.mark("qV")

            for do in range(DC):
                for q4 in range(4):
                    pp = ps[1 + (do * 4 + q4) % 2]

                    def fwo(t, do=do, q4=q4, pp=pp):
                        P.wait(t, "qV")
                        P.inc(t.matmul(pp[:, 0:512],
                                       wbufB[:, do * 128:(do + 1) * 128],
                                       A_std[:, q4 * 512:(q4 + 1) * 512],
                                       start=True, stop=True), "qT")
                    on_T(fwo)
                    mwo = P.mark("qT")

                    def fwe(v, do=do, q4=q4, pp=pp):
                        P.wait(v, "qT", mwo)
                        P.inc(v.scalar_tensor_tensor(
                            S_x[do][:, q4 * 512:(q4 + 1) * 512], pp[:, 0:512],
                            float(STD_RATIO), S_x[do][:, q4 * 512:(q4 + 1) * 512],
                            ALU.mult, ALU.add), "qV")
                    on_V(fwe)

        if stage >= 4:
            for g in range(H0 // 2):
                emit_attn_group(g)

        def emit_splat():
            wbufA, wbufB = xt[0], xt[1]
            D_sm = A_k
            D_tm = A_v

            def f0(sd):
                P.wait(sd, "qT")
                P.wait(sd, "qV")
                for j in range(DC):
                    P.dma(sd.dma_start(out=wbufA[:, j * 64:(j + 1) * 64],
                                       in_=spT[j * 128:(j + 1) * 128, :]))
                P.dma(sd.dma_start(out=denb[0:1, 128:192], in_=ssc_r))
                P.dma(sd.dma_start(out=denb[0:1, 192:256], in_=simp_r))
                P.dma(sd.dma_start(out=mcol[:, 0:DC], in_=bg_c))
                P.dma(sd.dma_start(out=mcol[:, 8:12], in_=be_c))
            on_D(f0)
            ld0 = P.mark("qD")

            for j in range(DC):
                def fsq(sc, j=j):
                    P.wait(sc, "qT")
                    P.inc(sc.activation(A_q[:, :], S_n[j][:, :], AF.Square), "qS")
                on_S(fsq)
                msq = P.mark("qS")

                def fmm(t, j=j):
                    P.wait(t, "qS", msq)
                    if j == 0:
                        P.wait(t, "qV")
                    ins = None
                    for q4 in range(4):
                        ins = t.matmul(ps_big[0:1, q4 * 512:(q4 + 1) * 512],
                                       c_1c[:, 0:1], A_q[:, q4 * 512:(q4 + 1) * 512],
                                       start=(j == 0), stop=(j == DC - 1),
                                       skip_group_check=True)
                    P.inc(ins, "qT", 1)
                on_T(fmm)
            mx2 = P.mark("qT")

            def fx2e(v):
                P.wait(v, "qT", mx2)
                v.tensor_copy(glg[0:1, 0:1024], ps_big[0:1, 0:1024])
                P.inc(v.tensor_copy(gfo[0:1, 0:1024], ps_big[0:1, 1024:2048]), "qV")
            on_V(fx2e)

            def fs2a(sc):
                P.wait(sc, "qD", ld0)
                P.wait(sc, "qT", mx2)
                P.inc(sc.activation(sqd[:, 0:512], wbufA[:, 0:512], AF.Square), "qS")
            on_S(fs2a)
            ms2 = P.mark("qS")

            def fs2m(t):
                P.wait(t, "qS", ms2)
                P.wait(t, "qV")
                ins = None
                for j in range(DC):
                    ins = t.matmul(ps[2][0:1, 0:64], c_1c[:, 0:1],
                                   sqd[:, j * 64:(j + 1) * 64],
                                   start=(j == 0), stop=(j == DC - 1))
                P.inc(ins, "qT", 1)
            on_T(fs2m)
            ms2m = P.mark("qT")

            def fs2e(v):
                P.wait(v, "qT", ms2m)
                P.inc(v.tensor_copy(denb[0:1, 0:64], ps[2][0:1, 0:64]), "qV")
            on_V(fs2e)
            ms2e = P.mark("qV")

            def fct(t):
                P.wait(t, "qV", ms2e)
                t.transpose(ps[3][0:64, 0:1], denb[0:1, 0:64], c_id[0:1, 0:1])
                t.transpose(ps[3][0:64, 1:2], denb[0:1, 128:192], c_id[0:1, 0:1])
                P.inc(t.transpose(ps[3][0:64, 2:3], denb[0:1, 192:256],
                                  c_id[0:1, 0:1]), "qT")
            on_T(fct)
            mct = P.mark("qT")

            def fcc(v):
                P.wait(v, "qT", mct)
                P.inc(v.tensor_copy(sqd[:, 960:963], ps[3][:, 0:3]), "qV")
                P.wait(v, "qV")
                P.inc(v.tensor_tensor(sqd[0:64, 961:962], sqd[0:64, 961:962],
                                      sqd[0:64, 961:962], ALU.mult), "qV")
                P.wait(v, "qV")
                P.inc(v.tensor_scalar(sqd[0:64, 961:962], sqd[0:64, 961:962],
                                      1e-12, None, ALU.max), "qV")
                P.wait(v, "qV")
                P.inc(v.reciprocal(sqd[0:64, 961:962], sqd[0:64, 961:962]), "qV")
                P.wait(v, "qV")
                P.inc(v.tensor_scalar(sqd[0:64, 961:962], sqd[0:64, 961:962],
                                      -0.5, None, ALU.mult), "qV")
                P.wait(v, "qV")
            on_V(fcc)
            mcc0 = P.mark("qV")

            def fcabs(sc):
                P.wait(sc, "qV", mcc0)
                P.inc(sc.activation(sqd[0:64, 962:963], sqd[0:64, 962:963],
                                    AF.Abs), "qS")
            on_S(fcabs)

            def fccw(v):
                P.wait(v, "qS")
            on_V(fccw)
            mcc = P.mark("qV")

            for q4 in range(4):
                def fdm(t, q4=q4):
                    P.wait(t, "qV")
                    ins = None
                    for j in range(DC):
                        ins = t.matmul(ps[2][0:64, 0:512], wbufA[:, j * 64:(j + 1) * 64],
                                       S_n[j][:, q4 * 512:(q4 + 1) * 512],
                                       start=(j == 0), stop=(j == DC - 1))
                    x2seg = glg[0:1, q4 * 512:(q4 + 1) * 512] if q4 < 2 \
                        else gfo[0:1, (q4 - 2) * 512:(q4 - 1) * 512]
                    P.inc(t.matmul(ps[3][0:64, 0:512], c_1r[0:1, 0:64], x2seg,
                                   start=True, stop=True), "qT")
                on_T(fdm)
                mdm = P.mark("qT")

                def fie(v, q4=q4):
                    P.wait(v, "qT", mdm)
                    P.inc(v.tensor_copy(A_q[64:128, 0:512], ps[3][0:64, 0:512]), "qV")
                    P.wait(v, "qV")
                    P.inc(v.scalar_tensor_tensor(A_q[0:64, 0:512], ps[2][0:64, 0:512],
                                                 -2.0, A_q[64:128, 0:512],
                                                 ALU.mult, ALU.add), "qV")
                    P.wait(v, "qV")
                    P.inc(v.tensor_scalar(A_q[0:64, 0:512], A_q[0:64, 0:512],
                                          sqd[0:64, 960:961], 0.0, ALU.add, ALU.max), "qV")
                on_V(fie)
                mie = P.mark("qV")

                def fex(sc, q4=q4):
                    P.wait(sc, "qV", mie)
                    P.inc(sc.activation(D_sm[0:64, q4 * 512:(q4 + 1) * 512],
                                        A_q[0:64, 0:512], AF.Exp,
                                        scale=sqd[0:64, 961:962]), "qS")
                on_S(fex)
                mexq = P.mark("qS")

                def fim(v, q4=q4):
                    P.wait(v, "qS", mexq)
                    P.inc(v.tensor_scalar(D_sm[0:64, q4 * 512:(q4 + 1) * 512],
                                          D_sm[0:64, q4 * 512:(q4 + 1) * 512],
                                          sqd[0:64, 962:963], None, ALU.mult), "qV")
                on_V(fim)

            def fb1(t):
                P.wait(t, "qV")
                t.matmul(ps[2][:, 0:64], c_1r[0:1, 0:128], denb[0:1, 0:64],
                         start=True, stop=True)
                P.inc(t.matmul(ps[3][:, 0:64], c_1r[0:1, 0:128], denb[0:1, 192:256],
                               start=True, stop=True), "qT")
            on_T(fb1)
            mb1 = P.mark("qT")

            def fb1e(v):
                P.wait(v, "qT", mb1)
                v.tensor_copy(A_q[:, 576:640], ps[2][:, 0:64])
                P.inc(v.tensor_copy(A_q[:, 704:768], ps[3][:, 0:64]), "qV")
                P.wait(v, "qV")
            on_V(fb1e)
            mimp = P.mark("qV")

            def fb1abs(sc):
                P.wait(sc, "qV", mimp)
                P.inc(sc.activation(A_q[:, 704:768], A_q[:, 704:768], AF.Abs), "qS")
            on_S(fb1abs)

            def fb1w(v):
                P.wait(v, "qS")
            on_V(fb1w)
            def fnh(v):
                P.inc(v.tensor_tensor(denb[0:1, 256:320], denb[0:1, 128:192],
                                      denb[0:1, 128:192], ALU.mult), "qV")
                P.wait(v, "qV")
                P.inc(v.tensor_scalar(denb[0:1, 256:320], denb[0:1, 256:320],
                                      1e-12, None, ALU.max), "qV")
                P.wait(v, "qV")
                P.inc(v.reciprocal(denb[0:1, 256:320], denb[0:1, 256:320]), "qV")
                P.wait(v, "qV")
                P.inc(v.tensor_scalar(denb[0:1, 256:320], denb[0:1, 256:320],
                                      -0.5, None, ALU.mult), "qV")
                P.wait(v, "qV")
            on_V(fnh)
            mnh = P.mark("qV")

            def fb2(t):
                P.wait(t, "qV", mnh)
                P.inc(t.matmul(ps[2][:, 64:128], c_1r[0:1, 0:128], denb[0:1, 256:320],
                               start=True, stop=True), "qT")
            on_T(fb2)
            mb2 = P.mark("qT")

            def fb2e(v):
                P.wait(v, "qT", mb2)
                P.inc(v.tensor_copy(A_q[:, 640:704], ps[2][:, 64:128]), "qV")
                P.wait(v, "qV")
            on_V(fb2e)

            for kc in range(TC):
                seg = glg if kc < 8 else gfo
                off = (kc % 8) * 128
                def fxb(t, kc=kc, seg=seg, off=off):
                    P.wait(t, "qV")
                    P.inc(t.matmul(ps[2][:, 128:256], c_1r[0:1, 0:128],
                                   seg[0:1, off:off + 128], start=True, stop=True), "qT")
                on_T(fxb)
                mxb1 = P.mark("qT")

                def fxc(v, kc=kc):
                    P.wait(v, "qT", mxb1)
                    P.inc(v.tensor_copy(sqd[:, 963:963 + 128][:, 0:128] if False else A_q[:, 832:960],
                                        ps[2][:, 128:256]), "qV")
                on_V(fxc)
                mxc = P.mark("qV")

                def fxt(t, kc=kc):
                    P.wait(t, "qV", mxc)
                    P.inc(t.transpose(ps[3][:, 128:256], A_q[:, 832:960], c_id[:]), "qT")
                on_T(fxt)
                mxt = P.mark("qT")

                def fxe(v, kc=kc):
                    P.wait(v, "qT", mxt)
                    P.inc(v.tensor_copy(mcol[:, 16 + kc:17 + kc], ps[3][:, 128:129]), "qV")
                on_V(fxe)

            for kc in range(TC):
                def ftm(t, kc=kc):
                    P.wait(t, "qV")
                    ins = None
                    for j in range(DC):
                        ins = t.matmul(ps[2][:, 0:64], S_n[j][:, kc * 128:(kc + 1) * 128],
                                       wbufA[:, j * 64:(j + 1) * 64],
                                       start=(j == 0), stop=(j == DC - 1))
                    P.inc(ins, "qT", 1)
                on_T(ftm)
                mtm = P.mark("qT")

                def fti(v, kc=kc):
                    P.wait(v, "qT", mtm)
                    P.inc(v.tensor_scalar(A_q[:, 0:64], ps[2][:, 0:64],
                                          -2.0, mcol[:, 16 + kc:17 + kc],
                                          ALU.mult, ALU.add), "qV")
                    P.wait(v, "qV")
                    v.tensor_tensor(A_q[:, 0:64], A_q[:, 0:64], A_q[:, 576:640], ALU.add)
                    v.tensor_scalar(A_q[:, 0:64], A_q[:, 0:64], 0.0, None, ALU.max)
                    P.inc(v.tensor_tensor(A_q[:, 0:64], A_q[:, 0:64],
                                          A_q[:, 640:704], ALU.mult), "qV")
                on_V(fti)
                mti = P.mark("qV")

                def fte(sc, kc=kc):
                    P.wait(sc, "qV", mti)
                    P.inc(sc.activation(A_q[:, 64:128], A_q[:, 0:64], AF.Exp), "qS")
                on_S(fte)
                mte = P.mark("qS")

                def ftf(v, kc=kc):
                    P.wait(v, "qS", mte)
                    P.inc(v.tensor_tensor(D_tm[:, kc * 64:(kc + 1) * 64],
                                          A_q[:, 64:128], A_q[:, 704:768], ALU.mult), "qV")
                on_V(ftf)

            def ftt(t):
                P.wait(t, "qV")
                ins = None
                for kc in range(TC):
                    ins = t.matmul(ps[3][0:1, 0:64], c_1c[:, 0:1],
                                   D_tm[:, kc * 64:(kc + 1) * 64],
                                   start=(kc == 0), stop=(kc == TC - 1))
                P.inc(ins, "qT", 1)
            on_T(ftt)
            mtt2 = P.mark("qT")

            def ftr(v):
                P.wait(v, "qT", mtt2)
                P.inc(v.tensor_scalar(denb[0:1, 64:128], ps[3][0:1, 0:64],
                                      1e-8, None, ALU.max), "qV")
                P.wait(v, "qV")
                P.inc(v.reciprocal(denb[0:1, 64:128], denb[0:1, 64:128]), "qV")
                P.wait(v, "qV")
            on_V(ftr)
            mrt = P.mark("qV")

            gslot = [ps_big[:, 0:64], ps_big[:, 512:576], ps_big[:, 1024:1088],
                     ps_big[:, 1536:1600], ps[0][:, 0:64], ps[1][:, 0:64],
                     ps[2][:, 0:64], ps[3][:, 0:64]]
            for kc in range(TC):
                def fgl(sd, kc=kc):
                    P.wait(sd, "qT")
                    P.dma(sd.dma_start(out=wbufB[:, :],
                                       in_=xn_tm_d[kc * 128:(kc + 1) * 128, :]))
                on_D(fgl)
                mgl = P.mark("qD")

                def fgm(t, kc=kc):
                    P.wait(t, "qD", mgl)
                    if kc == 0:
                        P.wait(t, "qV")
                    ins = None
                    for ds in range(DC):
                        ins = t.matmul(gslot[ds], wbufB[:, ds * 128:(ds + 1) * 128],
                                       D_tm[:, kc * 64:(kc + 1) * 64],
                                       start=(kc == 0), stop=(kc == TC - 1),
                                       skip_group_check=True)
                    P.inc(ins, "qT", 1)
                on_T(fgm)
            mgm = P.mark("qT")

            def fge(v):
                P.wait(v, "qT", mgm)
                for ds in range(DC):
                    v.tensor_copy(A_std[:, ds * 64:(ds + 1) * 64], gslot[ds])
                P.inc(v.tensor_copy(A_q[:, 0:64], A_q[:, 0:64]), "qV")
            on_V(fge)
            mge = P.mark("qV")

            def fgb(t):
                P.wait(t, "qV", mge)
                P.inc(t.matmul(ps[2][:, 64:128], c_1r[0:1, 0:128], denb[0:1, 64:128],
                               start=True, stop=True), "qT")
            on_T(fgb)
            mgb = P.mark("qT")

            def fgs(v):
                P.wait(v, "qT", mgb)
                P.inc(v.tensor_copy(A_q[:, 768:832], ps[2][:, 64:128]), "qV")
                P.wait(v, "qV")
                rb = A_q[:, 768:832]
                ra = apof(rb)
                rb3 = bass.AP(tensor=ra.tensor, offset=ra.offset,
                              ap=[ra.ap[0], [0, DC], [1, 64]])
                a3 = A_std[:, 0:512].rearrange("p (d s) -> p d s", s=64)
                P.inc(v.tensor_tensor(a3, a3, rb3, ALU.mult), "qV")
                P.wait(v, "qV")
            on_V(fgs)

            for j in range(DC):
                def fel(sd, j=j):
                    P.wait(sd, "qT")
                    P.dma(sd.dma_start(out=wbufB[:, 0:512],
                                       in_=We[j * 128:(j + 1) * 128, :]))
                on_D(fel)
                mel = P.mark("qD")

                def fem(t, j=j):
                    P.wait(t, "qD", mel)
                    if j == 0:
                        P.wait(t, "qV")
                    ins = None
                    for dbs in range(4):
                        ins = t.matmul(ps[dbs][:, 64:128],
                                       wbufB[:, dbs * 128:(dbs + 1) * 128],
                                       A_std[:, j * 64:(j + 1) * 64],
                                       start=(j == 0), stop=(j == DC - 1),
                                       skip_group_check=True)
                    P.inc(ins, "qT", 1)
                on_T(fem)
            mem = P.mark("qT")

            for dbs in range(4):
                def fee(sc, dbs=dbs):
                    P.wait(sc, "qT", mem)
                    P.inc(sc.activation(sqd[:, dbs * 64:(dbs + 1) * 64],
                                        ps[dbs][:, 64:128], AF.Relu,
                                        bias=mcol[:, 8 + dbs:9 + dbs]), "qS")
                on_S(fee)
            menc = P.mark("qS")

            def fbt(sd):
                P.wait(sd, "qV")
                for ec in range(4):
                    P.dma(sd.dma_start(out=A_q[:, 512 + ec * 64:512 + (ec + 1) * 64],
                                       in_=btT[ec * 128:(ec + 1) * 128, :]))
            on_D(fbt)
            mbt = P.mark("qD")

            for sp in range(S0):
                def ftl(sd, sp=sp):
                    P.wait(sd, "qT")
                    for dbs in range(2):
                        P.dma(sd.dma_start(
                            out=wbufB[:, dbs * 512:(dbs + 1) * 512],
                            in_=Wt[sp, dbs * 128:(dbs + 1) * 128, :]))
                on_D(ftl)
                mt1 = P.mark("qD")

                def ftm1(t, sp=sp):
                    P.wait(t, "qD", mt1)
                    P.wait(t, "qS", menc)
                    if sp == 0:
                        P.wait(t, "qD", mbt)
                    P.wait(t, "qV")
                    ins = None
                    for dbs in range(2):
                        for es in range(4):
                            ins = t.matmul(ps[es][:, 192:193],
                                           wbufB[:, dbs * 512 + es * 128:
                                                 dbs * 512 + (es + 1) * 128],
                                           sqd[:, dbs * 64 + sp:dbs * 64 + sp + 1]
                                           if False else
                                           sqd[:, dbs * 64 + sp:dbs * 64 + sp + 1],
                                           start=(dbs == 0), stop=False,
                                           skip_group_check=True)
                    P.inc(ins, "qT", 1)
                on_T(ftm1)
                mt2 = P.mark("qT")

                def ftl2(sd, sp=sp):
                    P.wait(sd, "qT", mt2)
                    for dbs in range(2, 4):
                        P.dma(sd.dma_start(
                            out=wbufB[:, (dbs - 2) * 512:(dbs - 1) * 512],
                            in_=Wt[sp, dbs * 128:(dbs + 1) * 128, :]))
                on_D(ftl2)
                mt3 = P.mark("qD")

                def ftm2(t, sp=sp):
                    P.wait(t, "qD", mt3)
                    ins = None
                    for dbs in range(2, 4):
                        for es in range(4):
                            ins = t.matmul(ps[es][:, 192:193],
                                           wbufB[:, (dbs - 2) * 512 + es * 128:
                                                 (dbs - 2) * 512 + (es + 1) * 128],
                                           sqd[:, dbs * 64 + sp:dbs * 64 + sp + 1],
                                           start=False, stop=(dbs == 3),
                                           skip_group_check=True)
                    P.inc(ins, "qT", 1)
                on_T(ftm2)
                mt4 = P.mark("qT")

                def fte2(v, sp=sp):
                    P.wait(v, "qT", mt4)
                    ins = None
                    for es in range(4):
                        ins = v.tensor_tensor(A_q[:, 768 + es * 64 + sp:769 + es * 64 + sp]
                                              if False else
                                              A_q[:, 768 + es * 64 + sp - (sp % 64) + (sp % 64):769 + es * 64 + sp - (sp % 64) + (sp % 64)],
                                              ps[es][:, 192:193],
                                              A_q[:, 512 + es * 64 + sp:513 + es * 64 + sp]
                                              if False else
                                              A_q[:, 512 + es * 64 + sp - (sp % 64) + (sp % 64):513 + es * 64 + sp - (sp % 64) + (sp % 64)],
                                              ALU.add)
                    P.inc(ins, "qV", 1)
                on_V(fte2)


            def fbd(sd):
                P.dma(sd.dma_start(out=glg[0:1, 0:1024], in_=bd_r))
            on_D(fbd)
            mbd = P.mark("qD")

            for ec in range(4):
                def fdl(sd, ec=ec):
                    P.wait(sd, "qT")
                    P.dma(sd.dma_start(out=wbufB[:, :],
                                       in_=Wd[ec * 128:(ec + 1) * 128, :]))
                on_D(fdl)
                mdl = P.mark("qD")

                def fdm2(t, ec=ec):
                    P.wait(t, "qD", mdl)
                    P.wait(t, "qV")
                    if ec == 0:
                        P.wait(t, "qD", mbd)
                        for half in range(2):
                            t.matmul(ps_big[0:64, half * 512:(half + 1) * 512],
                                     c_1r[0:1, 0:64],
                                     glg[0:1, half * 512:(half + 1) * 512],
                                     start=True, stop=False, skip_group_check=True)
                    ins = None
                    for half in range(2):
                        ins = t.matmul(ps_big[0:64, half * 512:(half + 1) * 512],
                                       A_q[:, 768 + ec * 64:768 + (ec + 1) * 64],
                                       wbufB[:, half * 512:(half + 1) * 512],
                                       start=False, stop=(ec == 3),
                                       skip_group_check=True)
                    P.inc(ins, "qT", 1)
                on_T(fdm2)
            mdm2 = P.mark("qT")

            def fde(v):
                P.wait(v, "qT", mdm2)
                P.inc(v.tensor_copy(A_std[0:64, 0:1024], ps_big[0:64, 0:1024]), "qV")
            on_V(fde)

            for j in range(DC):
                def ffl(sd, j=j):
                    P.wait(sd, "qT")
                    for jp in range(DC):
                        P.dma(sd.dma_start(out=wbufA[:, jp * 128:(jp + 1) * 128],
                                           in_=Wg[jp * 128:(jp + 1) * 128,
                                                  j * 128:(j + 1) * 128]))
                    P.dma(sd.dma_start(out=wbufB[:, :],
                                       in_=Wo[j * 128:(j + 1) * 128, :]))
                on_D(ffl)
                mfl = P.mark("qD")

                for q4 in range(4):
                    def fgm2(t, j=j, q4=q4):
                        P.wait(t, "qD", mfl)
                        P.wait(t, "qS")
                        P.wait(t, "qV")
                        ins = None
                        for jp in range(DC):
                            ins = t.matmul(ps[2][:, 0:512],
                                           wbufA[:, jp * 128:(jp + 1) * 128],
                                           S_n[jp][:, q4 * 512:(q4 + 1) * 512],
                                           start=(jp == 0), stop=(jp == DC - 1))
                        t.matmul(ps[3][:, 0:512],
                                 A_std[0:64, j * 128:(j + 1) * 128],
                                 D_sm[0:64, q4 * 512:(q4 + 1) * 512],
                                 start=True, stop=True)
                        P.inc(ins, "qT", 1)
                    on_T(fgm2)
                    mgm2 = P.mark("qT")

                    def fsg(sc, j=j, q4=q4):
                        P.wait(sc, "qT", mgm2)
                        P.inc(sc.activation(A_v[:, 0:512], ps[2][:, 0:512],
                                            AF.Sigmoid, bias=mcol[:, j:j + 1]), "qS")
                    on_S(fsg)
                    msg = P.mark("qS")

                    def ffm(v, j=j, q4=q4):
                        P.wait(v, "qS", msg)
                        P.inc(v.tensor_tensor(A_v[:, 512:1024], ps[3][:, 0:512],
                                              A_v[:, 0:512], ALU.mult), "qV")
                    on_V(ffm)
                    mfm = P.mark("qV")

                    for do in range(DC):
                        pp = ps[(do % 2)]

                        def fwo2(t, j=j, q4=q4, do=do, pp=pp):
                            P.wait(t, "qV")
                            P.inc(t.matmul(pp[:, 256:256 + 512][:, 0:512]
                                           if False else pp[:, 0:512],
                                           wbufB[:, do * 128:(do + 1) * 128],
                                           A_v[:, 512:1024],
                                           start=True, stop=True), "qT")
                        on_T(fwo2)
                        mwo2 = P.mark("qT")

                        def fwe2(v, j=j, q4=q4, do=do, pp=pp):
                            P.wait(v, "qT", mwo2)
                            P.inc(v.scalar_tensor_tensor(
                                S_x[do][:, q4 * 512:(q4 + 1) * 512], pp[:, 0:512],
                                float(SPLAT_RATIO),
                                S_x[do][:, q4 * 512:(q4 + 1) * 512],
                                ALU.mult, ALU.add), "qV")
                        on_V(fwe2)

        if stage >= 5:
            emit_splat()

        def emit_l0ffn():
            wbufA, wbufB = xt[0], xt[1]
            gslot = [ps_big[:, 0:384], ps_big[:, 512:896], ps_big[:, 1024:1408],
                     ps_big[:, 1536:1920], ps[0][:, 0:384], ps[1][:, 0:384],
                     ps[2][:, 0:384], ps[3][:, 0:384]]

            def xg_ap(j):
                return A_k[:, j * 384:(j + 1) * 384] if j < 5 \
                    else A_v[:, (j - 5) * 384:(j - 4) * 384]

            def h_ap(c):
                return S_n[c // 5][:, (c % 5) * 384:(c % 5 + 1) * 384]

            for kc in range(TC):
                def fsb(v, kc=kc):
                    P.wait(v, "qT")
                    P.inc(v.tensor_copy(A_q[:, 0:128],
                                        bcast_free(own_slot[:, kc:kc + 1], 128)), "qV")
                on_V(fsb)
                msb = P.mark("qV")

                def fst2(t, kc=kc):
                    P.wait(t, "qV", msb)
                    P.inc(t.transpose(ps[2][:, 0:128], A_q[:, 0:128], c_id[:]), "qT")
                on_T(fst2)
                mst = P.mark("qT")

                def fse(v, kc=kc):
                    P.wait(v, "qT", mst)
                    P.inc(v.tensor_copy(A_std[:, kc * 128:(kc + 1) * 128],
                                        ps[2][:, 0:128]), "qV")
                on_V(fse)

            for e in range(E):
                for kc in range(TC):
                    def fgl(sd, kc=kc, e=e):
                        P.wait(sd, "qT")
                        P.dma(sd.dma_start(out=wbufB[:, :],
                                           in_=xn_tm_d[kc * 128:(kc + 1) * 128, :]))
                    on_D(fgl)
                    mgl = P.mark("qD")

                    sl_pts = A_q[:, (kc % 2) * 384:(kc % 2) * 384 + 384]

                    def fpt(v, kc=kc, e=e, sl_pts=sl_pts):
                        P.wait(v, "qT")
                        P.inc(v.tensor_scalar(sl_pts, c_ir[:, 0:384],
                                              own_slot[:, kc:kc + 1],
                                              float(-e * LCAP0), ALU.subtract,
                                              ALU.is_equal), "qV")
                        P.wait(v, "qV")
                        P.inc(v.tensor_scalar(sl_pts, sl_pts,
                                              own_sc[:, kc:kc + 1], None,
                                              ALU.mult), "qV")
                    on_V(fpt)
                    mpt = P.mark("qV")

                    def fgm(t, kc=kc, e=e, sl_pts=sl_pts):
                        P.wait(t, "qV", mpt)
                        P.wait(t, "qD", mgl)
                        ins = None
                        for ds in range(DC):
                            ins = t.matmul(gslot[ds],
                                           wbufB[:, ds * 128:(ds + 1) * 128],
                                           sl_pts,
                                           start=(kc == 0), stop=(kc == TC - 1),
                                           skip_group_check=True)
                        P.inc(ins, "qT", 1)
                    on_T(fgm)
                mgm = P.mark("qT")

                def fge(v, e=e):
                    P.wait(v, "qT", mgm)
                    ins = None
                    for ds in range(DC):
                        ins = v.tensor_copy(xg_ap(ds), gslot[ds])
                    P.inc(ins, "qV", 1)
                on_V(fge)
                mge = P.mark("qV")

                hslot = gslot
                for hg in range(4):
                    for j in range(DC):
                        wb = wbufA if (hg * DC + j) % 2 == 0 else wbufB

                        def fwl(sd, hg=hg, j=j, e=e, wb=wb):
                            P.wait(sd, "qT")
                            P.dma(sd.dma_start(
                                out=wb[:, :],
                                in_=W1[e, j * 128:(j + 1) * 128,
                                       hg * 1024:(hg + 1) * 1024]))
                        on_D(fwl)
                        mwl = P.mark("qD")

                        def fhm(t, hg=hg, j=j, e=e, wb=wb):
                            P.wait(t, "qD", mwl)
                            if hg == 0 and j == 0:
                                P.wait(t, "qV", mge)
                            ins = None
                            for hs in range(DC):
                                ins = t.matmul(hslot[hs],
                                               wb[:, hs * 128:(hs + 1) * 128],
                                               xg_ap(j),
                                               start=(j == 0), stop=(j == DC - 1),
                                               skip_group_check=True)
                            P.inc(ins, "qT", 1)
                        on_T(fhm)
                    mhm = P.mark("qT")

                    for hs in range(DC):
                        def fhe(sc, hg=hg, hs=hs):
                            P.wait(sc, "qT", mhm)
                            P.inc(sc.activation(h_ap(hg * 8 + hs), hslot[hs],
                                                AF.Relu), "qS")
                        on_S(fhe)

                def yg_ap(r3):
                    return (A_v[:, 0:1024], A_v[:, 1024:2048], A_k[:, 0:1024])[r3]

                for c in range(32):
                    wb = wbufA if c % 2 == 0 else wbufB

                    def fyl(sd, c=c, e=e, wb=wb):
                        P.wait(sd, "qT")
                        P.dma(sd.dma_start(out=wb[:, :],
                                           in_=W2[e, c * 128:(c + 1) * 128, :]))
                    on_D(fyl)
                    myl = P.mark("qD")

                    def fym(t, c=c, e=e, wb=wb):
                        P.wait(t, "qD", myl)
                        P.wait(t, "qS")
                        if c == 0:
                            P.wait(t, "qV")
                        ins = None
                        for r3 in range(3):
                            ins = t.matmul(ps_big[:, r3 * 512:(r3 + 1) * 512],
                                           h_ap(c)[:, r3 * 128:(r3 + 1) * 128],
                                           wb[:, 0:512],
                                           start=(c == 0), stop=(c == 31),
                                           skip_group_check=True)
                            ins = t.matmul(ps[r3][:, 0:512],
                                           h_ap(c)[:, r3 * 128:(r3 + 1) * 128],
                                           wb[:, 512:1024],
                                           start=(c == 0), stop=(c == 31),
                                           skip_group_check=True)
                        P.inc(ins, "qT", 1)
                    on_T(fym)
                mym = P.mark("qT")

                def fye(v, e=e):
                    P.wait(v, "qT", mym)
                    ins = None
                    for r3 in range(3):
                        v.tensor_copy(yg_ap(r3)[:, 0:512], ps_big[:, r3 * 512:(r3 + 1) * 512])
                        ins = v.tensor_copy(yg_ap(r3)[:, 512:1024], ps[r3][:, 0:512])
                    P.inc(ins, "qV", 1)
                on_V(fye)
                mye = P.mark("qV")

                pch = (S_n[5], S_n[6], A_q)

                for r3 in range(3):
                    def fpc(v, r3=r3, e=e):
                        P.wait(v, "qT")
                        P.inc(v.tensor_scalar(pch[r3][:, 0:2048], A_std[:, 0:2048],
                                              c_ic[:, 0:1],
                                              float(e * LCAP0 + r3 * 128),
                                              ALU.subtract, ALU.is_equal), "qV")
                    on_V(fpc)
                mpc = P.mark("qV")

                for do in range(DC):
                    for q4 in range(4):
                        pp = ps[2 + (do * 4 + q4) % 2]

                        def fsm(t, do=do, q4=q4, e=e, pp=pp):
                            P.wait(t, "qV")
                            if do == 0 and q4 == 0:
                                P.wait(t, "qV", mpc)
                            ins = None
                            for r3 in range(3):
                                ins = t.matmul(pp[:, 0:512],
                                               yg_ap(r3)[:, do * 128:(do + 1) * 128],
                                               pch[r3][:, q4 * 512:(q4 + 1) * 512],
                                               start=(r3 == 0), stop=(r3 == 2))
                            P.inc(ins, "qT", 1)
                        on_T(fsm)
                        msm = P.mark("qT")

                        def fsa(v, do=do, q4=q4, pp=pp):
                            P.wait(v, "qT", msm)
                            P.inc(v.tensor_tensor(
                                S_x[do][:, q4 * 512:(q4 + 1) * 512],
                                S_x[do][:, q4 * 512:(q4 + 1) * 512],
                                pp[:, 0:512], ALU.add), "qV")
                        on_V(fsa)

        if stage >= 6:
            emit_l0ffn()

        def emit_l1_ln():
            A_a, A_b = S_s[0], S_s[1]
            segs = ((glg, r_t1), (gfo, denb))

            def f1(t):
                P.wait(t, "qV")
                P.wait(t, "qS")
                ins = None
                for j in range(DC):
                    for q4 in range(4):
                        ins = t.matmul(ps_big[0:1, q4 * 512:(q4 + 1) * 512],
                                       c_1c[:, 0:1], S_x[j][:, q4 * 512:(q4 + 1) * 512],
                                       start=(j == 0), stop=(j == DC - 1),
                                       skip_group_check=True)
                P.inc(ins, "qT", 1)
            on_T(f1)
            m1 = P.mark("qT")

            def f1e(v):
                P.wait(v, "qT", m1)
                v.tensor_copy(glg[0:1, 0:1024], ps_big[0:1, 0:1024])
                P.inc(v.tensor_copy(gfo[0:1, 0:1024], ps_big[0:1, 1024:2048]), "qV")
            on_V(f1e)
            m1e = P.mark("qV")

            for j in range(DC):
                def f2s(sc, j=j):
                    P.wait(sc, "qT")
                    if j == 0:
                        P.wait(sc, "qV", m1e)
                    P.inc(sc.activation(A_a[:, :], S_x[j][:, :], AF.Square), "qS")
                on_S(f2s)
                m2s = P.mark("qS")

                def f2m(t, j=j):
                    P.wait(t, "qS", m2s)
                    ins = None
                    for q4 in range(4):
                        ins = t.matmul(ps_big[0:1, q4 * 512:(q4 + 1) * 512],
                                       c_1c[:, 0:1], A_a[:, q4 * 512:(q4 + 1) * 512],
                                       start=(j == 0), stop=(j == DC - 1),
                                       skip_group_check=True)
                    P.inc(ins, "qT", 1)
                on_T(f2m)
            m2 = P.mark("qT")

            def f3(v):
                P.wait(v, "qT", m2)
                v.tensor_copy(r_t1[0:1, 0:1024], ps_big[0:1, 0:1024])
                P.inc(v.tensor_copy(denb[0:1, 0:1024], ps_big[0:1, 1024:2048]), "qV")
                P.wait(v, "qV")
                for sg, qg in segs:
                    v.tensor_scalar(sg[0:1, 0:1024], sg[0:1, 0:1024], 1.0 / D,
                                    None, ALU.mult)
                    P.inc(v.tensor_scalar(qg[0:1, 0:1024], qg[0:1, 0:1024], 1.0 / D,
                                          None, ALU.mult), "qV")
                    P.wait(v, "qV")
                    P.inc(v.tensor_tensor(A_a[0:1, 0:1024], sg[0:1, 0:1024],
                                          sg[0:1, 0:1024], ALU.mult), "qV")
                    P.wait(v, "qV")
                    P.inc(v.tensor_sub(qg[0:1, 0:1024], qg[0:1, 0:1024],
                                       A_a[0:1, 0:1024]), "qV")
                    P.wait(v, "qV")
                P.inc(v.nop(), "qV", 1)
            on_V(f3)
            m3 = P.mark("qV")

            def f4(sc):
                P.wait(sc, "qV", m3)
                sc.activation(r_t1[0:1, 0:1024], r_t1[0:1, 0:1024], AF.Sqrt,
                              bias=float(EPS_LN))
                P.inc(sc.activation(denb[0:1, 0:1024], denb[0:1, 0:1024], AF.Sqrt,
                                    bias=float(EPS_LN)), "qS")
            on_S(f4)
            m4 = P.mark("qS")

            def f5(v):
                P.wait(v, "qS", m4)
                v.reciprocal(r_t1[0:1, 0:1024], r_t1[0:1, 0:1024])
                P.inc(v.reciprocal(denb[0:1, 0:1024], denb[0:1, 0:1024]), "qV")
                P.wait(v, "qV")
                for sg, qg in segs:
                    P.inc(v.tensor_tensor(sg[0:1, 0:1024], sg[0:1, 0:1024],
                                          qg[0:1, 0:1024], ALU.mult), "qV")
                    P.wait(v, "qV")
                    P.inc(v.tensor_scalar(sg[0:1, 0:1024], sg[0:1, 0:1024], -1.0,
                                          None, ALU.mult), "qV")
                    P.wait(v, "qV")
            on_V(f5)
            m5 = P.mark("qV")

            for q4 in range(4):
                qg = r_t1 if q4 < 2 else denb
                sg = glg if q4 < 2 else gfo
                cc = (q4 % 2) * 512

                def fb(t, q4=q4, qg=qg, sg=sg, cc=cc):
                    P.wait(t, "qV")
                    t.matmul(ps[0][:, 0:512], c_1r[0:1, 0:128], qg[0:1, cc:cc + 512],
                             start=True, stop=True)
                    P.inc(t.matmul(ps[1][:, 0:512], c_1r[0:1, 0:128],
                                   sg[0:1, cc:cc + 512], start=True, stop=True), "qT")
                on_T(fb)
                mb = P.mark("qT")

                def fbe(v, q4=q4):
                    P.wait(v, "qT", mb)
                    v.tensor_copy(A_a[:, q4 * 512:(q4 + 1) * 512], ps[0][:, 0:512])
                    P.inc(v.tensor_copy(A_b[:, q4 * 512:(q4 + 1) * 512],
                                        ps[1][:, 0:512]), "qV")
                on_V(fbe)

            for j in range(DC):
                def fx(v, j=j):
                    P.wait(v, "qV")
                    v.tensor_mul(S_n[j][:, :], S_x[j][:, :], A_a[:, :])
                    v.tensor_add(S_n[j][:, :], S_n[j][:, :], A_b[:, :])
                    P.inc(v.tensor_add(S_x[j][:, :], S_x[j][:, :], S_n[j][:, :]), "qV")
                on_V(fx)
            mxn = P.mark("qV")

            for kc in range(TC):
                for j in range(DC):
                    pp = ps[j % 2]

                    def ftp(t, kc=kc, j=j, pp=pp):
                        P.wait(t, "qV")
                        if j == 0:
                            P.wait(t, "qD")
                        if kc == 0 and j == 0:
                            P.wait(t, "qV", mxn)
                        P.inc(t.transpose(pp[:, 0:128],
                                          S_n[j][:, kc * 128:(kc + 1) * 128],
                                          c_id[:]), "qT")
                    on_T(ftp)
                    mtp = P.mark("qT")

                    def fte(v, kc=kc, j=j, pp=pp):
                        P.wait(v, "qT", mtp)
                        P.inc(v.tensor_copy(sqd[:, j * 128:(j + 1) * 128],
                                            pp[:, 0:128]), "qV")
                    on_V(fte)
                mte = P.mark("qV")

                def ftd(sd, kc=kc):
                    P.wait(sd, "qV", mte)
                    P.dma(sd.dma_start(out=xn_tm_d[kc * 128:(kc + 1) * 128, :],
                                       in_=sqd[:, :]))
                on_D(ftd)

            def ftw(v):
                P.wait(v, "qD")
            on_V(ftw)

        if stage >= 7:
            emit_l1_ln()

        LCAP1 = 768

        def emit_l1ffn():
            wbufA, wbufB = xt[0], xt[1]
            U0, U1, U2, U3, U4 = S_s[0], S_s[1], S_s[2], S_s[3], S_s[4]
            slot2 = own_lrk

            def fr0(sd):
                P.wait(sd, "qT")
                P.wait(sd, "qV")
                for j in range(DC):
                    P.dma(sd.dma_start(out=s_wr[:, j * E:(j + 1) * E],
                                       in_=rw[j * 128:(j + 1) * 128, :]))
                P.dma(sd.dma_start(out=colx[:, 0:DC], in_=rsc_c))
                P.dma(sd.dma_start(out=denb[0:1, 512:520], in_=pes_r))
            on_D(fr0)
            mr0 = P.mark("qD")

            def fr1(v):
                P.wait(v, "qD", mr0)
                for j in range(DC):
                    P.inc(v.tensor_scalar(s_wr[:, j * E:(j + 1) * E],
                                          s_wr[:, j * E:(j + 1) * E],
                                          colx[:, j:j + 1], None, ALU.mult), "qV")
                P.wait(v, "qV")
            on_V(fr1)
            mr1 = P.mark("qV")

            for j in range(DC):
                def fq(sc, j=j):
                    P.wait(sc, "qT")
                    P.inc(sc.activation(U0[:, :], S_n[j][:, :], AF.Square), "qS")
                on_S(fq)
                mq = P.mark("qS")

                def fqm(t, j=j):
                    P.wait(t, "qS", mq)
                    if j == 0:
                        P.wait(t, "qV", mr1)
                    ins = None
                    for q4 in range(4):
                        ins = t.matmul(ps_big[0:1, q4 * 512:(q4 + 1) * 512],
                                       c_1c[:, 0:1], U0[:, q4 * 512:(q4 + 1) * 512],
                                       start=(j == 0), stop=(j == DC - 1),
                                       skip_group_check=True)
                    P.inc(ins, "qT", 1)
                on_T(fqm)
            mq2 = P.mark("qT")

            def fq3(v):
                P.wait(v, "qT", mq2)
                v.tensor_copy(glg[0:1, 0:1024], ps_big[0:1, 0:1024])
                P.inc(v.tensor_copy(gfo[0:1, 0:1024], ps_big[0:1, 1024:2048]), "qV")
                P.wait(v, "qV")
            on_V(fq3)
            mq3 = P.mark("qV")

            def fq4(sc):
                P.wait(sc, "qV", mq3)
                sc.activation(glg[0:1, 0:1024], glg[0:1, 0:1024], AF.Sqrt,
                              bias=1e-6, scale=1.0 / D)
                P.inc(sc.activation(gfo[0:1, 0:1024], gfo[0:1, 0:1024], AF.Sqrt,
                                    bias=1e-6, scale=1.0 / D), "qS")
            on_S(fq4)
            mq4 = P.mark("qS")

            def fq5(v):
                P.wait(v, "qS", mq4)
                v.reciprocal(glg[0:1, 0:1024], glg[0:1, 0:1024])
                P.inc(v.reciprocal(gfo[0:1, 0:1024], gfo[0:1, 0:1024]), "qV")
                P.wait(v, "qV")
                v.tensor_scalar(glg[0:1, 0:1024], glg[0:1, 0:1024],
                                float(D ** -0.5), None, ALU.mult)
                P.inc(v.tensor_scalar(gfo[0:1, 0:1024], gfo[0:1, 0:1024],
                                      float(D ** -0.5), None, ALU.mult), "qV")
                P.wait(v, "qV")
            on_V(fq5)
            mq5 = P.mark("qV")

            for kc in range(TC):
                seg = glg if kc < 8 else gfo
                off = (kc % 8) * 128

                def fxb(t, kc=kc, seg=seg, off=off):
                    P.wait(t, "qV")
                    if kc == 0:
                        P.wait(t, "qV", mq5)
                    P.inc(t.matmul(ps[2][:, 128:256], c_1r[0:1, 0:128],
                                   seg[0:1, off:off + 128], start=True, stop=True), "qT")
                on_T(fxb)
                mxb1 = P.mark("qT")

                def fxc(v, kc=kc):
                    P.wait(v, "qT", mxb1)
                    P.inc(v.tensor_copy(U0[:, 832:960], ps[2][:, 128:256]), "qV")
                on_V(fxc)
                mxc = P.mark("qV")

                def fxt(t, kc=kc):
                    P.wait(t, "qV", mxc)
                    P.inc(t.transpose(ps[3][:, 128:256], U0[:, 832:960], c_id[:]), "qT")
                on_T(fxt)
                mxt = P.mark("qT")

                def fxe(v, kc=kc):
                    P.wait(v, "qT", mxt)
                    P.inc(v.tensor_copy(mcol[:, 16 + kc:17 + kc], ps[3][:, 128:129]), "qV")
                on_V(fxe)

            for kc in range(TC):
                def flg(t, kc=kc):
                    P.wait(t, "qV")
                    ins = None
                    for j in range(DC):
                        ins = t.matmul(ps[2][:, 0:8], S_n[j][:, kc * 128:(kc + 1) * 128],
                                       s_wr[:, j * E:(j + 1) * E],
                                       start=(j == 0), stop=(j == DC - 1))
                    P.inc(ins, "qT", 1)
                on_T(flg)
                mlg = P.mark("qT")

                def fle(v, kc=kc):
                    P.wait(v, "qT", mlg)
                    P.inc(v.tensor_scalar(olg[:, kc * 8:(kc + 1) * 8], ps[2][:, 0:8],
                                          mcol[:, 16 + kc:17 + kc], None, ALU.mult), "qV")
                on_V(fle)
            mlgs = P.mark("qV")

            def ft2(v):
                P.wait(v, "qV", mlgs)
                argmax_fo(v, olg, ofo, r_mx, r_t1, r_t1[:, 512:1024], TC)
                lg3 = olg[:, 0:TC * 8].rearrange("p (t e) -> p t e", e=8)
                t13 = r_t1[:, 0:TC * 8].rearrange("p (t e) -> p t e", e=8)
                ra = apof(r_mx)
                mxb = bass.AP(tensor=ra.tensor, offset=ra.offset,
                              ap=[ra.ap[0], [1, TC], [0, 8]])
                P.inc(v.tensor_tensor(t13, lg3, mxb, ALU.subtract), "qV")
            on_V(ft2)
            mt2a = P.mark("qV")

            def ft2e(sc):
                P.wait(sc, "qV", mt2a)
                P.inc(sc.activation(r_t1[:, 0:TC * 8], r_t1[:, 0:TC * 8], AF.Exp), "qS")
            on_S(ft2e)
            mt2e = P.mark("qS")

            def ft3(v):
                P.wait(v, "qS", mt2e)
                ex3 = r_t1[:, 0:TC * 8].rearrange("p (t e) -> p t e", e=8)
                P.inc(v.reduce_sum(own_p[:, 0:TC], ex3, axis=AX.X), "qV")
                P.wait(v, "qV")
                P.inc(v.reciprocal(own_p[:, 0:TC], own_p[:, 0:TC]), "qV")
                P.wait(v, "qV")
                P.inc(v.tensor_mul(r_t1[:, 512:512 + TC * 8], r_t1[:, 0:TC * 8],
                                   ofo[:, 0:TC * 8]), "qV")
                P.wait(v, "qV")
                P.inc(v.reduce_sum(own_sc[:, 0:TC],
                                   r_t1[:, 512:512 + TC * 8].rearrange(
                                       "p (t e) -> p t e", e=8), axis=AX.X), "qV")
                P.wait(v, "qV")
                P.inc(v.tensor_mul(own_sc[:, 0:TC], own_sc[:, 0:TC],
                                   own_p[:, 0:TC]), "qV")
                P.inc(v.scalar_tensor_tensor(glg[:, 0:TC * 8], ofo[:, 0:TC * 8],
                                             -BIGF, olg[:, 0:TC * 8],
                                             ALU.mult, ALU.add), "qV")
                P.wait(v, "qV")
                argmax_fo(v, glg, gfo, r_mx, r_t1[:, 512:1024],
                          r_t1[:, 0:512], TC)
                P.inc(v.nop(), "qV", 1)
            on_V(ft3)
            def ft3b(v):
                lg3 = olg[:, 0:TC * 8].rearrange("p (t e) -> p t e", e=8)
                t13 = r_t1[:, 0:TC * 8].rearrange("p (t e) -> p t e", e=8)
                P.inc(v.reduce_max(r_mx[:, 0:TC], lg3, axis=AX.X), "qV")
                P.wait(v, "qV")
                ra = apof(r_mx)
                mxb = bass.AP(tensor=ra.tensor, offset=ra.offset,
                              ap=[ra.ap[0], [1, TC], [0, 8]])
                P.inc(v.tensor_tensor(t13, lg3, mxb, ALU.subtract), "qV")
            on_V(ft3b)
            mt3b = P.mark("qV")

            def ft3c(sc):
                P.wait(sc, "qV", mt3b)
                P.inc(sc.activation(r_t1[:, 0:TC * 8], r_t1[:, 0:TC * 8], AF.Exp), "qS")
            on_S(ft3c)
            mt3c = P.mark("qS")

            def ft4(v):
                P.wait(v, "qS", mt3c)
                P.inc(v.tensor_mul(glg[:, 0:TC * 8], r_t1[:, 0:TC * 8],
                                   gfo[:, 0:TC * 8]), "qV")
                P.wait(v, "qV")
                P.inc(v.reduce_sum(own_p2[:, 0:TC],
                                   glg[:, 0:TC * 8].rearrange("p (t e) -> p t e", e=8),
                                   axis=AX.X), "qV")
                P.wait(v, "qV")
                P.inc(v.tensor_mul(own_p2[:, 0:TC], own_p2[:, 0:TC],
                                   own_p[:, 0:TC]), "qV")
                P.wait(v, "qV")
                P.inc(v.tensor_tensor(own_p[:, 0:TC], own_sc[:, 0:TC],
                                      own_p2[:, 0:TC], ALU.add), "qV")
                P.wait(v, "qV")
                P.inc(v.tensor_scalar(r_mx[:, 0:TC], own_p[:, 0:TC], 0.0, None,
                                      ALU.is_gt), "qV")
                P.wait(v, "qV")
                P.inc(v.tensor_scalar(r_mx[:, 16:16 + TC], r_mx[:, 0:TC], -1.0, 1.0,
                                      ALU.mult, ALU.add), "qV")
                P.wait(v, "qV")
                P.inc(v.scalar_tensor_tensor(own_p[:, 0:TC], own_p[:, 0:TC], 1.0,
                                             r_mx[:, 16:16 + TC], ALU.mult,
                                             ALU.add), "qV")
                P.wait(v, "qV")
                P.inc(v.reciprocal(own_p[:, 0:TC], own_p[:, 0:TC]), "qV")
                P.wait(v, "qV")
                v.tensor_mul(own_sc[:, 0:TC], own_sc[:, 0:TC], own_p[:, 0:TC])
                P.inc(v.tensor_mul(own_p2[:, 0:TC], own_p2[:, 0:TC],
                                   own_p[:, 0:TC]), "qV")
                P.wait(v, "qV")
            on_V(ft4)
            mt4 = P.mark("qV")

            def fpe(t):
                P.wait(t, "qV", mt4)
                P.inc(t.matmul(ps[2][:, 0:8], c_1r[0:1, 0:128], denb[0:1, 512:520],
                               start=True, stop=True), "qT")
            on_T(fpe)
            mpe = P.mark("qT")

            def fpe2(v):
                P.wait(v, "qT", mpe)
                P.inc(v.tensor_copy(eL_bc[:, 0:8], ps[2][:, 0:8]), "qV")
                P.wait(v, "qV")
                for oh, ccol in ((ofo, own_sc), (gfo, own_p2)):
                    ea = apof(eL_bc)
                    eb = bass.AP(tensor=ea.tensor, offset=ea.offset,
                                 ap=[ea.ap[0], [0, TC], [1, 8]])
                    P.inc(v.tensor_tensor(r_t1[:, 0:TC * 8].rearrange(
                        "p (t e) -> p t e", e=8),
                        oh[:, 0:TC * 8].rearrange("p (t e) -> p t e", e=8),
                        eb, ALU.mult), "qV")
                    P.wait(v, "qV")
                    P.inc(v.reduce_sum(r_mx[:, 0:TC],
                                       r_t1[:, 0:TC * 8].rearrange(
                                           "p (t e) -> p t e", e=8), axis=AX.X), "qV")
                    P.wait(v, "qV")
                    P.inc(v.tensor_mul(ccol[:, 0:TC], ccol[:, 0:TC],
                                       r_mx[:, 0:TC]), "qV")
                    P.wait(v, "qV")
            on_V(fpe2)
            mpe2 = P.mark("qV")

            def zc2(g):
                P.wait(g, "qV", mpe2)
                g.memset(carry[:], 0.0)
                P.inc(g.memset(pfx_bc[:, 0:8], 0.0), "qG")
            on_G(zc2)
            mzc = P.mark("qG")

            for pass_i, (oh, slot_dst) in enumerate(((ofo, own_slot), (gfo, slot2))):
                for k in range(TC):
                    def ftk(t, k=k, oh=oh, pass_i=pass_i):
                        P.wait(t, "qV")
                        if k == 0 and pass_i == 0:
                            P.wait(t, "qG", mzc)
                        t.matmul(ps[0][:, 0:8], c_1r[0:1, 0:128], carry[0:1, 0:8],
                                 start=True, stop=False)
                        t.matmul(ps[0][:, 0:8], c_u[:, :], oh[:, k * 8:(k + 1) * 8],
                                 start=False, stop=True)
                        P.inc(t.matmul(ps[1][:1, 0:8], c_1c[:, 0:1],
                                       oh[:, k * 8:(k + 1) * 8],
                                       start=True, stop=True), "qT")
                    on_T(ftk)
                    mtk = P.mark("qT")

                    def fvk(v, k=k, oh=oh):
                        P.wait(v, "qT", mtk)
                        v.tensor_sub(r_t1[:, k * 8:(k + 1) * 8], ps[0][:, 0:8],
                                     oh[:, k * 8:(k + 1) * 8])
                        P.inc(v.tensor_add(carry[0:1, 0:8], carry[0:1, 0:8],
                                           ps[1][0:1, 0:8]), "qV")
                    on_V(fvk)
                mrank = P.mark("qV")

                def fsl(v, oh=oh, slot_dst=slot_dst, pass_i=pass_i):
                    P.wait(v, "qV", mrank)
                    lrk3 = r_t1[:, 0:TC * 8].rearrange("p (t e) -> p t e", e=8)
                    sv = glg[:, 0:TC * 8]
                    sv3 = sv.rearrange("p (t e) -> p t e", e=8)
                    q = r_t1[:, 512:512 + TC * 8]
                    q3 = q.rearrange("p (t e) -> p t e", e=8)
                    P.inc(v.tensor_scalar(eL_bc[:, 0:8], c_ir[:, 0:8],
                                          float(LCAP1), None, ALU.mult), "qV")
                    P.wait(v, "qV")
                    P.inc(v.tensor_scalar(r_mx[:, 32:40], eL_bc[:, 0:8],
                                          float(LCAP1), None, ALU.add), "qV")
                    if pass_i == 1:
                        P.inc(v.tensor_add(eL_bc[:, 0:8], eL_bc[:, 0:8],
                                           pfx_bc[:, 0:8]), "qV")
                    P.wait(v, "qV")
                    ea = apof(eL_bc)
                    eb = bass.AP(tensor=ea.tensor, offset=ea.offset,
                                 ap=[ea.ap[0], [0, TC], [1, 8]])
                    P.inc(v.tensor_tensor(sv3, lrk3, eb, ALU.add), "qV")
                    P.wait(v, "qV")
                    ca = apof(r_mx)
                    cb = bass.AP(tensor=ca.tensor, offset=ca.offset + 32,
                                 ap=[ca.ap[0], [0, TC], [1, 8]])
                    P.inc(v.tensor_tensor(q3, sv3, cb, ALU.is_lt), "qV")
                    P.wait(v, "qV")
                    P.inc(v.tensor_mul(q, q, oh[:, 0:TC * 8]), "qV")
                    P.wait(v, "qV")
                    P.inc(v.reduce_sum(r_mx[:, 0:TC], q3, axis=AX.X), "qV")
                    P.inc(v.tensor_mul(sv, sv, q), "qV")
                    P.wait(v, "qV")
                    P.inc(v.reduce_sum(slot_dst[:, 0:TC], sv3, axis=AX.X), "qV")
                    P.wait(v, "qV")
                    P.inc(v.tensor_scalar(r_mx[:, 16:16 + TC], r_mx[:, 0:TC],
                                          -BIGF, BIGF, ALU.mult, ALU.add), "qV")
                    P.wait(v, "qV")
                    P.inc(v.tensor_add(slot_dst[:, 0:TC], slot_dst[:, 0:TC],
                                       r_mx[:, 16:16 + TC]), "qV")
                    P.wait(v, "qV")
                on_V(fsl)

                if pass_i == 0:
                    def fcb(t):
                        P.wait(t, "qV")
                        P.inc(t.matmul(ps[2][:, 0:8], c_1r[0:1, 0:128],
                                       carry[0:1, 0:8], start=True, stop=True), "qT")
                    on_T(fcb)
                    mcb = P.mark("qT")

                    def fce(v):
                        P.wait(v, "qT", mcb)
                        P.inc(v.tensor_copy(pfx_bc[:, 0:8], ps[2][:, 0:8]), "qV")
                        P.inc(v.memset(carry[0:1, 0:8], 0.0), "qV")
                        P.wait(v, "qV")
                    on_V(fce)

            gsl = [ps_big[:, 0:384], ps_big[:, 512:896], ps_big[:, 1024:1408],
                   ps_big[:, 1536:1920], ps[0][:, 0:384], ps[1][:, 0:384],
                   ps[2][:, 0:384], ps[3][:, 0:384]]

            def xg_ap1(j):
                return U0[:, j * 384:(j + 1) * 384] if j < 5 \
                    else U1[:, (j - 5) * 384:(j - 4) * 384]

            def h_ap1(hs):
                if hs < 5:
                    return U2[:, hs * 384:(hs + 1) * 384]
                return U3[:, (hs - 5) * 384:(hs - 4) * 384]

            pts_sl = U1[:, 1536:1920]

            def yg_ap1(r3):
                return (U4[:, 0:1024], U4[:, 1024:2048], sqd[:, 0:1024])[r3]

            for e in range(E):
                for hf in range(2):
                    base = e * LCAP1 + hf * 384
                    for kc in range(TC):
                        def fgl(sd, kc=kc):
                            P.wait(sd, "qT")
                            P.dma(sd.dma_start(out=wbufB[:, :],
                                               in_=xn_tm_d[kc * 128:(kc + 1) * 128, :]))
                        on_D(fgl)
                        mgl = P.mark("qD")

                        def fpt(v, kc=kc, base=base):
                            P.wait(v, "qT")
                            P.inc(v.tensor_scalar(pts_sl, c_ir[:, 0:384],
                                                  own_slot[:, kc:kc + 1],
                                                  float(-base), ALU.subtract,
                                                  ALU.is_equal), "qV")
                            P.inc(v.tensor_scalar(U0[:, 832:832 + 384]
                                                  if False else r_t1[:, 0:384],
                                                  c_ir[:, 0:384],
                                                  slot2[:, kc:kc + 1],
                                                  float(-base), ALU.subtract,
                                                  ALU.is_equal), "qV")
                            P.wait(v, "qV")
                            P.inc(v.tensor_add(pts_sl, pts_sl, r_t1[:, 0:384]), "qV")
                            P.wait(v, "qV")
                        on_V(fpt)
                        mpt = P.mark("qV")

                        def fgm(t, kc=kc):
                            P.wait(t, "qV", mpt)
                            P.wait(t, "qD", mgl)
                            ins = None
                            for ds in range(DC):
                                ins = t.matmul(gsl[ds],
                                               wbufB[:, ds * 128:(ds + 1) * 128],
                                               pts_sl,
                                               start=(kc == 0), stop=(kc == TC - 1),
                                               skip_group_check=True)
                            P.inc(ins, "qT", 1)
                        on_T(fgm)
                    mgm = P.mark("qT")

                    def fge(v):
                        P.wait(v, "qT", mgm)
                        ins = None
                        for ds in range(DC):
                            ins = v.tensor_copy(xg_ap1(ds), gsl[ds])
                        P.inc(ins, "qV", 1)
                    on_V(fge)
                    mge = P.mark("qV")

                    if dbg is not None and e == 7 and hf == 1:
                        def fdbg1(sd):
                            P.wait(sd, "qV", mge)
                            P.dma(sd.dma_start(out=dbg[:, 4848:4848 + 384],
                                               in_=U0[:, 0:384]))
                        on_D(fdbg1)

                        def fdbg1w(t):
                            P.wait(t, "qD")
                        on_T(fdbg1w)

                    for kc in range(TC):
                        def fcg(v, kc=kc, base=base):
                            P.wait(v, "qT")
                            P.inc(v.tensor_scalar(pts_sl, c_ir[:, 0:384],
                                                  own_slot[:, kc:kc + 1],
                                                  float(-base), ALU.subtract,
                                                  ALU.is_equal), "qV")
                            P.inc(v.tensor_scalar(r_t1[:, 0:384], c_ir[:, 0:384],
                                                  slot2[:, kc:kc + 1],
                                                  float(-base), ALU.subtract,
                                                  ALU.is_equal), "qV")
                            P.wait(v, "qV")
                        on_V(fcg)
                        mcg = P.mark("qV")

                        def fcgm(t, kc=kc):
                            P.wait(t, "qV", mcg)
                            ins = None
                            for r3 in range(3):
                                ins = t.matmul(ps[1 + r3][:, 400:401],
                                               pts_sl[:, r3 * 128:(r3 + 1) * 128],
                                               own_sc[:, kc:kc + 1],
                                               start=(kc == 0), stop=False,
                                               skip_group_check=True)
                                ins = t.matmul(ps[1 + r3][:, 400:401],
                                               r_t1[:, r3 * 128:(r3 + 1) * 128],
                                               own_p2[:, kc:kc + 1],
                                               start=False, stop=(kc == TC - 1),
                                               skip_group_check=True)
                            P.inc(ins, "qT", 1)
                        on_T(fcgm)
                    mcgm = P.mark("qT")

                    def fcge(v):
                        P.wait(v, "qT", mcgm)
                        for r3 in range(3):
                            v.tensor_copy(mcol[:, 19 + r3:20 + r3],
                                          ps[1 + r3][:, 400:401])
                        P.inc(v.nop(), "qV", 1)
                        P.wait(v, "qV")
                    on_V(fcge)

                    for wv in range(2):
                        for gate in range(2):
                            for j in range(DC):
                                wb = wbufA if j % 2 == 0 else wbufB

                                def fwl(sd, e=e, hf=hf, wv=wv, gate=gate, j=j, wb=wb):
                                    P.wait(sd, "qT")
                                    P.dma(sd.dma_start(
                                        out=wb[:, :],
                                        in_=gwT[e, gate, j * 128:(j + 1) * 128,
                                                hf * 0 + wv * 1024:wv * 1024 + 1024]))
                                on_D(fwl)
                                mwl = P.mark("qD")

                                def fhm(t, j=j, wb=wb, gate=gate):
                                    P.wait(t, "qD", mwl)
                                    P.wait(t, "qV")
                                    ins = None
                                    for hs in range(DC):
                                        ins = t.matmul(gsl[hs],
                                                       wb[:, hs * 128:(hs + 1) * 128],
                                                       xg_ap1(j),
                                                       start=(j == 0),
                                                       stop=(j == DC - 1),
                                                       skip_group_check=True)
                                    P.inc(ins, "qT", 1)
                                on_T(fhm)
                            mhm = P.mark("qT")

                            if gate == 0:
                                for hs in range(DC):
                                    def fhe(sc, hs=hs):
                                        P.wait(sc, "qT", mhm)
                                        P.inc(sc.activation(h_ap1(hs), gsl[hs],
                                                            AF.Gelu_apprx_tanh), "qS")
                                    on_S(fhe)

                                def fhw(v):
                                    P.wait(v, "qS")
                                    P.inc(v.tensor_copy(eL_bc[:, 0:1],
                                                        eL_bc[:, 0:1]), "qV")
                                on_V(fhw)
                            else:
                                for hs in range(DC):
                                    def fhe2(v, hs=hs):
                                        if hs == 0:
                                            P.wait(v, "qT", mhm)
                                        P.inc(v.tensor_tensor(h_ap1(hs), h_ap1(hs),
                                                              gsl[hs], ALU.mult), "qV")
                                    on_V(fhe2)
                        mact = P.mark("qV")

                        if dbg is not None and e == 7 and hf == 1:
                            def fdbg2(sd, wv=wv):
                                P.wait(sd, "qV", mact)
                                P.dma(sd.dma_start(
                                    out=dbg[:, 5232 + wv * 768:5232 + wv * 768 + 384],
                                    in_=h_ap1(0)))
                            on_D(fdbg2)

                            def fdbg2w(t):
                                P.wait(t, "qD")
                            on_T(fdbg2w)

                            def fdbg2v(v):
                                P.wait(v, "qD")
                            on_V(fdbg2v)

                        for hs in range(DC):
                            wb = wbufA if hs % 2 == 0 else wbufB

                            def fyl(sd, e=e, wv=wv, hs=hs, wb=wb):
                                P.wait(sd, "qT")
                                P.dma(sd.dma_start(
                                    out=wb[:, :],
                                    in_=lw[e, (wv * 8 + hs) * 128:
                                           (wv * 8 + hs + 1) * 128, :]))
                            on_D(fyl)
                            myl = P.mark("qD")

                            def fym(t, hs=hs, wv=wv, wb=wb):
                                P.wait(t, "qD", myl)
                                if hs == 0:
                                    P.wait(t, "qV", mact)
                                ins = None
                                for r3 in range(3):
                                    for nh in range(2):
                                        ins = t.matmul(
                                            (ps_big[:, r3 * 512:(r3 + 1) * 512]
                                             if nh == 0 else
                                             ps[r3][:, 0:512]),
                                            h_ap1(hs)[:, r3 * 128:(r3 + 1) * 128],
                                            wb[:, nh * 512:(nh + 1) * 512],
                                            start=(hs == 0),
                                            stop=(hs == DC - 1),
                                            skip_group_check=True)
                                P.inc(ins, "qT", 1)
                            on_T(fym)
                        mymw = P.mark("qT")

                        def fyew(v, wv=wv):
                            P.wait(v, "qT", mymw)
                            ins = None
                            for r3 in range(3):
                                for nh in range(2):
                                    src = ps_big[:, r3 * 512:(r3 + 1) * 512] \
                                        if nh == 0 else ps[r3][:, 0:512]
                                    dst = yg_ap1(r3)[:, nh * 512:(nh + 1) * 512]
                                    if wv == 0:
                                        ins = v.tensor_copy(dst, src)
                                    else:
                                        ins = v.tensor_tensor(dst, dst, src, ALU.add)
                            P.inc(ins, "qV", 1)
                            P.wait(v, "qV")
                            if wv == 1:
                                ins = None
                                for r3 in range(3):
                                    ins = v.tensor_scalar(yg_ap1(r3), yg_ap1(r3),
                                                          mcol[:, 19 + r3:20 + r3],
                                                          None, ALU.mult)
                                P.inc(ins, "qV", 1)
                                P.wait(v, "qV")
                        on_V(fyew)
                    mye = P.mark("qV")

                    if dbg is not None and e == 7 and hf == 1:
                        def fdbgy(sd):
                            P.wait(sd, "qV", mye)
                            P.dma(sd.dma_start(out=dbg[:, 5616:5616 + 384],
                                               in_=yg_ap1(0)[:, 0:384]))
                        on_D(fdbgy)

                        def fdbgyw(t):
                            P.wait(t, "qD")
                        on_T(fdbgyw)

                    for src_col, dstU in ((own_slot, U2), (slot2, U3)):
                        for kc in range(TC):
                            def fsb(v, kc=kc, src_col=src_col):
                                P.wait(v, "qT")
                                P.inc(v.tensor_copy(
                                    r_t1[:, 0:128],
                                    bcast_free(src_col[:, kc:kc + 1], 128)), "qV")
                            on_V(fsb)
                            msb = P.mark("qV")

                            def fst2(t, kc=kc):
                                P.wait(t, "qV", msb)
                                P.inc(t.transpose(ps[2][:, 0:128], r_t1[:, 0:128],
                                                  c_id[:]), "qT")
                            on_T(fst2)
                            mst = P.mark("qT")

                            def fse(v, kc=kc, dstU=dstU):
                                P.wait(v, "qT", mst)
                                P.inc(v.tensor_copy(dstU[:, kc * 128:(kc + 1) * 128],
                                                    ps[2][:, 0:128]), "qV")
                            on_V(fse)

                    for grp in ((0, 1), (2,)):
                        for gi, r3 in enumerate(grp):
                            if len(grp) == 2:
                                dsts = ((U0[:, 0:1024], U0[:, 1024:2048]),
                                        (U1[:, 0:1024], U1[:, 1024:2048]))[gi]
                            else:
                                dsts = (U0[:, 0:1024], U0[:, 1024:2048])

                            def fpc(v, r3=r3, dsts=dsts, base=base):
                                P.wait(v, "qT")
                                for th in range(2):
                                    dst = dsts[th]
                                    P.inc(v.tensor_scalar(
                                        dst, U2[:, th * 1024:(th + 1) * 1024],
                                        c_ic[:, 0:1], float(base + r3 * 128),
                                        ALU.subtract, ALU.is_equal), "qV")
                                    P.inc(v.tensor_scalar(
                                        r_t1[:, 0:1024],
                                        U3[:, th * 1024:(th + 1) * 1024],
                                        c_ic[:, 0:1], float(base + r3 * 128),
                                        ALU.subtract, ALU.is_equal), "qV")
                                    P.wait(v, "qV")
                                    P.inc(v.tensor_add(dst, dst, r_t1[:, 0:1024]),
                                          "qV")
                                    P.wait(v, "qV")
                            on_V(fpc)
                        mpc = P.mark("qV")

                        def pch_ap(gi, q4, grp=grp):
                            if len(grp) == 2:
                                pl = ((U0[:, 0:1024], U0[:, 1024:2048]),
                                      (U1[:, 0:1024], U1[:, 1024:2048]))[gi]
                            else:
                                pl = (U0[:, 0:1024], U0[:, 1024:2048])
                            th, cc = divmod(q4 * 512, 1024)
                            return pl[th][:, cc:cc + 512]

                        for do in range(DC):
                            for q4 in range(4):
                                pp = ps[2 + (do * 4 + q4) % 2]

                                def fsm(t, do=do, q4=q4, grp=grp, pp=pp):
                                    P.wait(t, "qV")
                                    if do == 0 and q4 == 0:
                                        P.wait(t, "qV", mpc)
                                    ins = None
                                    for gi, r3 in enumerate(grp):
                                        ins = t.matmul(
                                            pp[:, 0:512],
                                            yg_ap1(r3)[:, do * 128:(do + 1) * 128],
                                            pch_ap(gi, q4),
                                            start=(gi == 0),
                                            stop=(gi == len(grp) - 1))
                                    P.inc(ins, "qT", 1)
                                on_T(fsm)
                                msm = P.mark("qT")

                                def fsa(v, do=do, q4=q4, pp=pp):
                                    P.wait(v, "qT", msm)
                                    P.inc(v.tensor_tensor(
                                        S_x[do][:, q4 * 512:(q4 + 1) * 512],
                                        S_x[do][:, q4 * 512:(q4 + 1) * 512],
                                        pp[:, 0:512], ALU.add), "qV")
                                on_V(fsa)

        if stage >= 8:
            emit_l1ffn()

        def emit_l1_bk():
            wbufA, wbufB = xt[0], xt[1]
            U0, U1, U2, U3 = S_s[0], S_s[1], S_s[2], S_s[3]
            U4s = S_s[4]

            def f0(sd):
                P.wait(sd, "qT")
                P.wait(sd, "qV")
                for j in range(DC):
                    P.dma(sd.dma_start(out=s_wr[:, j * 2:(j + 1) * 2],
                                       in_=w_pot[j * 128:(j + 1) * 128, :]))
                P.dma(sd.dma_start(out=denb[0:1, 512:514], in_=b_pot_r))
                P.dma(sd.dma_start(out=mcol[:, 0:6 * DC], in_=ln_c)) \
                    if False else None
            on_D(f0)
            def f0b(sd):
                P.dma(sd.dma_start(out=wbufA[:, 0:48], in_=ln_c))
            on_D(f0b)
            ml0 = P.mark("qD")

            def f1(v):
                P.wait(v, "qD", ml0)
                for j in range(DC):
                    P.inc(v.tensor_scalar(s_wr[:, j * 2:(j + 1) * 2],
                                          s_wr[:, j * 2:(j + 1) * 2],
                                          wbufA[:, j:j + 1], None, ALU.mult), "qV")
                P.wait(v, "qV")
            on_V(f1)
            mf1 = P.mark("qV")

            def f1b(sd):
                P.wait(sd, "qV", mf1)
                for j in range(DC):
                    P.dma(sd.dma_start(out=wbufB[:, j * 2:(j + 1) * 2],
                                       in_=w_pot[j * 128:(j + 1) * 128, :]))
            on_D(f1b)
            mf1b = P.mark("qD")

            def f1c(v):
                P.wait(v, "qD", mf1b)
                for j in range(DC):
                    P.inc(v.tensor_scalar(wbufB[:, j * 2:(j + 1) * 2],
                                          wbufB[:, j * 2:(j + 1) * 2],
                                          wbufA[:, 8 + j:9 + j], None, ALU.mult), "qV")
                P.wait(v, "qV")
            on_V(f1c)
            mf1c = P.mark("qV")

            def f1d(t):
                P.wait(t, "qV", mf1c)
                ins = None
                for j in range(DC):
                    ins = t.matmul(ps[0][0:1, 0:2], c_1c[:, 0:1],
                                   wbufB[:, j * 2:(j + 1) * 2],
                                   start=(j == 0), stop=(j == DC - 1))
                P.inc(ins, "qT", 1)
            on_T(f1d)
            mf1d = P.mark("qT")

            def f1e(v):
                P.wait(v, "qT", mf1d)
                P.inc(v.tensor_copy(denb[0:1, 514:516], ps[0][0:1, 0:2]), "qV")
                P.wait(v, "qV")
                P.inc(v.tensor_add(denb[0:1, 514:516], denb[0:1, 514:516],
                                   denb[0:1, 512:514]), "qV")
                P.wait(v, "qV")
            on_V(f1e)
            mf1e = P.mark("qV")

            def f2(t):
                P.wait(t, "qV", mf1e)
                insv = None
                for q4 in range(4):
                    t.matmul(ps_big[0:1, q4 * 512:(q4 + 1) * 512],
                             denb[0:1, 514:515], c_1r[0:1, 0:512],
                             start=True, stop=False, skip_group_check=True)
                    t.matmul(ps[q4][0:1, 0:512], denb[0:1, 515:516],
                             c_1r[0:1, 0:512], start=True, stop=False,
                             skip_group_check=True)
                for j in range(DC):
                    for q4 in range(4):
                        t.matmul(ps_big[0:1, q4 * 512:(q4 + 1) * 512],
                                 s_wr[:, j * 2:j * 2 + 1],
                                 S_n[j][:, q4 * 512:(q4 + 1) * 512],
                                 start=False, stop=(j == DC - 1),
                                 skip_group_check=True)
                        insv = t.matmul(ps[q4][0:1, 0:512],
                                        s_wr[:, j * 2 + 1:j * 2 + 2],
                                        S_n[j][:, q4 * 512:(q4 + 1) * 512],
                                        start=False, stop=(j == DC - 1),
                                        skip_group_check=True)
                P.inc(insv, "qT", 1)
            on_T(f2)
            mf2 = P.mark("qT")

            def f3(v):
                P.wait(v, "qT", mf2)
                v.tensor_copy(glg[0:1, 0:1024], ps_big[0:1, 0:1024])
                v.tensor_copy(gfo[0:1, 0:1024], ps_big[0:1, 1024:2048])
                v.tensor_copy(r_t1[0:1, 0:512], ps[0][0:1, 0:512])
                v.tensor_copy(r_t1[0:1, 512:1024], ps[1][0:1, 0:512])
                v.tensor_copy(denb[0:1, 0:512], ps[2][0:1, 0:512])
                P.inc(v.tensor_copy(denb[0:1, 512:1024], ps[3][0:1, 0:512]), "qV")
                P.wait(v, "qV")
            on_V(f3)
            mf3 = P.mark("qV")

            def f4(sc):
                P.wait(sc, "qV", mf3)
                sc.activation(U0[0:1, 0:1024], r_t1[0:1, 0:1024], AF.Abs)
                P.inc(sc.activation(U0[0:1, 1024:2048], denb[0:1, 0:1024],
                                    AF.Abs), "qS")
            on_S(f4)
            mf4a = P.mark("qS")

            def f4b(sc):
                P.inc(sc.activation(U0[0:1, 0:2048], U0[0:1, 0:2048], AF.Exp,
                                    scale=-1.0), "qS")
            on_S(f4b)
            mf4b = P.mark("qS")

            def f4c(sc):
                P.inc(sc.activation(U0[0:1, 0:2048], U0[0:1, 0:2048], AF.Ln,
                                    bias=1.0), "qS")
            on_S(f4c)
            mf4c = P.mark("qS")

            def f5(v):
                P.wait(v, "qS", mf4c)
                v.tensor_scalar(r_t1[0:1, 0:1024], r_t1[0:1, 0:1024], 0.0,
                                None, ALU.max)
                P.inc(v.tensor_scalar(denb[0:1, 0:1024], denb[0:1, 0:1024], 0.0,
                                      None, ALU.max), "qV")
                P.wait(v, "qV")
                v.tensor_add(r_t1[0:1, 0:1024], r_t1[0:1, 0:1024],
                             U0[0:1, 0:1024])
                P.inc(v.tensor_add(denb[0:1, 0:1024], denb[0:1, 0:1024],
                                   U0[0:1, 1024:2048]), "qV")
                P.wait(v, "qV")
                v.tensor_scalar(r_t1[0:1, 0:1024], r_t1[0:1, 0:1024], 0.1,
                                None, ALU.add)
                P.inc(v.tensor_scalar(denb[0:1, 0:1024], denb[0:1, 0:1024], 0.1,
                                      None, ALU.add), "qV")
                P.wait(v, "qV")
            on_V(f5)
            mf5 = P.mark("qV")

            def f6(v):
                P.wait(v, "qV", mf5)
                v.tensor_scalar(glg[0:1, 0:1024], glg[0:1, 0:1024], 2.0, None, ALU.add)
                P.inc(v.tensor_scalar(gfo[0:1, 0:1024], gfo[0:1, 0:1024], 2.0,
                                      None, ALU.add), "qV")
                P.wait(v, "qV")
                v.tensor_scalar(U0[0:1, 0:1024], r_t1[0:1, 0:1024], -1.0, None,
                                ALU.mult)
                P.inc(v.tensor_scalar(U0[0:1, 1024:2048], denb[0:1, 0:1024], -1.0,
                                      None, ALU.mult), "qV")
                P.wait(v, "qV")
                for (srcA, srcB, dst) in (
                        ((glg, 0), (gfo, 0), (U1, 0)),):
                    pass
                def revap(row, off):
                    a = apof(row)
                    return bass.AP(tensor=a.tensor, offset=a.offset + off,
                                   ap=[[a.ap[0][0], 1], [-1, 1024]])
                v.tensor_copy(U1[0:1, 0:1024], revap(gfo[0:1, :], 1023))
                P.inc(v.tensor_copy(U1[0:1, 1024:2048],
                                    revap(glg[0:1, :], 1023)), "qV")
                v.tensor_copy(U1[32:33, 0:1024], revap(U0[0:1, :], 2047))
                P.inc(v.tensor_copy(U1[32:33, 1024:2048],
                                    revap(U0[0:1, :], 1023)), "qV")
                P.wait(v, "qV")
            on_V(f6)
            mf6 = P.mark("qV")

            def f7(sd):
                P.wait(sd, "qV", mf6)
                P.dma(sd.dma_start(out=scr[0:1, 0:1024], in_=glg[0:1, 0:1024]))
                P.dma(sd.dma_start(out=scr[1:2, 0:1024], in_=gfo[0:1, 0:1024]))
                P.dma(sd.dma_start(out=scr[2:3, 0:1024], in_=U0[0:1, 0:1024]))
                P.dma(sd.dma_start(out=scr[3:4, 0:1024], in_=U0[0:1, 1024:2048]))
                P.dma(sd.dma_start(out=scr[4:5, 0:1024], in_=U1[0:1, 0:1024]))
                P.dma(sd.dma_start(out=scr[5:6, 0:1024], in_=U1[0:1, 1024:2048]))
                P.dma(sd.dma_start(out=scr[6:7, 0:1024], in_=U1[32:33, 0:1024]))
                P.dma(sd.dma_start(out=scr[7:8, 0:1024], in_=U1[32:33, 1024:2048]))
            on_D(f7)
            mf7 = P.mark("qD")

            def scan_cf(dir_i):
                C = lambda a, b=None: U4s[:, a:(b if b else a + 16)]

                def g1(sd, dir_i=dir_i):
                    P.wait(sd, "qV")
                    P.wait(sd, "qD", mf7)
                    r0 = 4 * dir_i
                    for hh2 in range(2):
                        P.dma(sd.dma_start(
                            out=U4s[hh2 * 64:(hh2 + 1) * 64, 0:16],
                            in_=scr[r0 + hh2:r0 + hh2 + 1, 0:1024].rearrange(
                                "a (p i) -> (a p) i", i=16)))
                        P.dma(sd.dma_start(
                            out=U4s[hh2 * 64:(hh2 + 1) * 64, 16:32],
                            in_=scr[r0 + 2 + hh2:r0 + 3 + hh2, 0:1024].rearrange(
                                "a (p i) -> (a p) i", i=16)))
                on_D(g1)
                mg1 = P.mark("qD")

                def g2(v):
                    P.wait(v, "qD", mg1)
                    v.tensor_copy(C(32)[:, 0:1], C(0)[:, 0:1])
                    v.tensor_copy(C(48)[:, 0:1], C(16)[:, 0:1])
                    v.memset(C(64)[:, 0:1], -1.0)
                    v.memset(C(80)[:, 0:1], 0.0)
                    P.inc(v.nop(), "qV", 1)
                    P.wait(v, "qV")
                    for i in range(1, 16):
                        dre_i = C(0)[:, i:i + 1]
                        dim_i = C(16)[:, i:i + 1]
                        for (pr, pi) in ((32, 48), (64, 80)):
                            prev_r = C(pr)[:, i - 1:i]
                            prev_i = C(pi)[:, i - 1:i]
                            v.tensor_tensor(C(160)[:, 0:1], dre_i, prev_r, ALU.mult)
                            v.tensor_tensor(C(160)[:, 1:2], dim_i, prev_i, ALU.mult)
                            v.tensor_tensor(C(160)[:, 2:3], dre_i, prev_i, ALU.mult)
                            P.inc(v.tensor_tensor(C(160)[:, 3:4], dim_i, prev_r,
                                                  ALU.mult), "qV")
                            P.wait(v, "qV")
                            v.tensor_sub(C(pr)[:, i:i + 1], C(160)[:, 0:1],
                                         C(160)[:, 1:2])
                            P.inc(v.tensor_add(C(pi)[:, i:i + 1], C(160)[:, 2:3],
                                               C(160)[:, 3:4]), "qV")
                            P.wait(v, "qV")
                            if i >= 2:
                                v.tensor_sub(C(pr)[:, i:i + 1], C(pr)[:, i:i + 1],
                                             C(pr)[:, i - 2:i - 1])
                                P.inc(v.tensor_sub(C(pi)[:, i:i + 1],
                                                   C(pi)[:, i:i + 1],
                                                   C(pi)[:, i - 2:i - 1]), "qV")
                                P.wait(v, "qV")
                            elif i == 1 and pr == 32:
                                P.inc(v.tensor_scalar(C(pr)[:, 1:2], C(pr)[:, 1:2],
                                                      -1.0, None, ALU.add), "qV")
                                P.wait(v, "qV")
                    P.inc(v.nop(), "qV", 1)
                on_V(g2)
                mg2 = P.mark("qV")

                def g3(v):
                    P.wait(v, "qV", mg2)
                    v.tensor_copy(C(160)[:, 0:1], C(32)[:, 15:16])
                    v.tensor_copy(C(160)[:, 1:2], C(48)[:, 15:16])
                    v.tensor_copy(C(160)[:, 2:3], C(64)[:, 15:16])
                    v.tensor_copy(C(160)[:, 3:4], C(80)[:, 15:16])
                    v.tensor_copy(C(160)[:, 4:5], C(32)[:, 14:15])
                    v.tensor_copy(C(160)[:, 5:6], C(48)[:, 14:15])
                    v.tensor_copy(C(160)[:, 6:7], C(64)[:, 14:15])
                    P.inc(v.tensor_copy(C(160)[:, 7:8], C(80)[:, 14:15]), "qV")
                    P.wait(v, "qV")
                    P.inc(v.tensor_tensor(C(168)[:, 0:8], C(160)[:, 0:8],
                                          C(160)[:, 0:8], ALU.mult), "qV")
                    P.wait(v, "qV")
                    P.inc(v.reduce_sum(C(176)[:, 0:1], C(168)[:, 0:8], axis=AX.X),
                          "qV")
                    P.wait(v, "qV")
                on_V(g3)
                mg3 = P.mark("qV")

                def g3b(sc):
                    P.wait(sc, "qV", mg3)
                    P.inc(sc.activation(C(176)[:, 0:1], C(176)[:, 0:1], AF.Sqrt,
                                        bias=1e-30), "qS")
                on_S(g3b)
                mg3b = P.mark("qS")

                def g3c(v):
                    P.wait(v, "qS", mg3b)
                    P.inc(v.reciprocal(C(176)[:, 0:1], C(176)[:, 0:1]), "qV")
                    P.wait(v, "qV")
                    P.inc(v.tensor_scalar(C(160)[:, 0:8], C(160)[:, 0:8],
                                          C(176)[:, 0:1], None, ALU.mult), "qV")
                    P.wait(v, "qV")
                on_V(g3c)

                for lvl in range(7):
                    sft = 1 << lvl
                    mpre = P.mark("qV")

                    def g4a(g, sft=sft, mpre=mpre):
                        P.wait(g, "qV", mpre)
                        g.memset(U4s[:, 168:176], 0.0)
                        g.memset(U4s[:, 168:169], 1.0)
                        P.inc(g.memset(U4s[:, 174:175], 1.0), "qG")
                    on_G(g4a)
                    mga = P.mark("qG")

                    def g4b(g, sft=sft, mga=mga):
                        P.wait(g, "qG", mga)
                        P.dma(g.dma_start(out=U4s[sft:128, 168:176],
                                          in_=U4s[0:128 - sft, 160:168]), "qG")
                    on_G(g4b)
                    mgb = P.mark("qG")

                    def g4c(v, sft=sft, mgb=mgb):
                        P.wait(v, "qG", mgb)
                        def cm(dst_r, dst_i, x, y):
                            xr, xi = C(160)[:, x:x + 1], C(160)[:, x + 1:x + 2]
                            yr, yi = C(168)[:, y:y + 1], C(168)[:, y + 1:y + 2]
                            v.tensor_tensor(C(176)[:, 8:9], xr, yr, ALU.mult)
                            v.tensor_tensor(C(176)[:, 9:10], xi, yi, ALU.mult)
                            v.tensor_tensor(C(176)[:, 10:11], xr, yi, ALU.mult)
                            P.inc(v.tensor_tensor(C(176)[:, 11:12], xi, yr,
                                                  ALU.mult), "qV")
                            P.wait(v, "qV")
                            v.tensor_sub(C(176)[:, 12:13], C(176)[:, 8:9],
                                         C(176)[:, 9:10])
                            P.inc(v.tensor_add(C(176)[:, 13:14], C(176)[:, 10:11],
                                               C(176)[:, 11:12]), "qV")
                            P.wait(v, "qV")
                            if dst_r is not None:
                                v.tensor_add(dst_r, dst_r, C(176)[:, 12:13])
                                P.inc(v.tensor_add(dst_i, dst_i,
                                                   C(176)[:, 13:14]), "qV")
                            else:
                                v.tensor_copy(C(176)[:, 14:15], C(176)[:, 12:13])
                                P.inc(v.tensor_copy(C(176)[:, 15:16],
                                                    C(176)[:, 13:14]), "qV")
                            P.wait(v, "qV")

                        for oi, (x1, y1, x2, y2) in enumerate(
                                ((0, 0, 2, 4), (0, 2, 2, 6),
                                 (4, 0, 6, 4), (4, 2, 6, 6))):
                            cm(None, None, x1, y1)
                            v.tensor_copy(C(200)[:, 2 * oi:2 * oi + 1],
                                          C(176)[:, 14:15])
                            P.inc(v.tensor_copy(C(200)[:, 2 * oi + 1:2 * oi + 2],
                                                C(176)[:, 15:16]), "qV")
                            P.wait(v, "qV")
                            cm(C(200)[:, 2 * oi:2 * oi + 1],
                               C(200)[:, 2 * oi + 1:2 * oi + 2], x2, y2)
                        P.inc(v.tensor_copy(C(160)[:, 0:8], C(200)[:, 0:8]), "qV")
                        P.wait(v, "qV")
                        P.inc(v.tensor_tensor(C(168)[:, 0:8], C(160)[:, 0:8],
                                              C(160)[:, 0:8], ALU.mult), "qV")
                        P.wait(v, "qV")
                        P.inc(v.reduce_sum(C(176)[:, 0:1], C(168)[:, 0:8],
                                           axis=AX.X), "qV")
                        P.wait(v, "qV")
                    on_V(g4c)
                    mgc = P.mark("qV")

                    def g4d(sc, mgc=mgc):
                        P.wait(sc, "qV", mgc)
                        P.inc(sc.activation(C(176)[:, 0:1], C(176)[:, 0:1],
                                            AF.Sqrt, bias=1e-30), "qS")
                    on_S(g4d)
                    mgd = P.mark("qS")

                    def g4e(v, mgd=mgd):
                        P.wait(v, "qS", mgd)
                        P.inc(v.reciprocal(C(176)[:, 0:1], C(176)[:, 0:1]), "qV")
                        P.wait(v, "qV")
                        P.inc(v.tensor_scalar(C(160)[:, 0:8], C(160)[:, 0:8],
                                              C(176)[:, 0:1], None, ALU.mult), "qV")
                        P.wait(v, "qV")
                    on_V(g4e)

                mfin = P.mark("qV")

                def g5(g, mfin=mfin):
                    P.wait(g, "qV", mfin)
                    g.memset(U4s[:, 168:176], 0.0)
                    g.memset(U4s[:, 168:169], 1.0)
                    P.inc(g.memset(U4s[:, 174:175], 1.0), "qG")
                on_G(g5)
                mg5 = P.mark("qG")

                def g5b(g, mg5=mg5):
                    P.wait(g, "qG", mg5)
                    P.dma(g.dma_start(out=U4s[1:128, 168:176],
                                      in_=U4s[0:127, 160:168]), "qG")
                on_G(g5b)
                mg5b = P.mark("qG")

                def g6(v, mg5b=mg5b):
                    P.wait(v, "qG", mg5b)
                    v.tensor_copy(C(96)[:, 1:16], C(32)[:, 0:15])
                    v.tensor_copy(C(112)[:, 1:16], C(48)[:, 0:15])
                    v.tensor_copy(C(128)[:, 1:16], C(64)[:, 0:15])
                    v.tensor_copy(C(144)[:, 1:16], C(80)[:, 0:15])
                    v.memset(C(96)[:, 0:1], 1.0)
                    v.memset(C(112)[:, 0:1], 0.0)
                    v.memset(C(128)[:, 0:1], 0.0)
                    P.inc(v.memset(C(144)[:, 0:1], 0.0), "qV")
                    P.wait(v, "qV")
                    def lin(outr, outi, P_r, P_i, Q_r, Q_i):
                        ar, ai = C(168)[:, 0:1], C(168)[:, 1:2]
                        cr, ci = C(168)[:, 4:5], C(168)[:, 5:6]
                        v.tensor_scalar(C(208), C(P_r), ar, None, ALU.mult)
                        v.tensor_scalar(C(224), C(P_i), ai, None, ALU.mult)
                        v.tensor_scalar(C(240), C(P_r), ai, None, ALU.mult)
                        P.inc(v.tensor_scalar(C(256), C(P_i), ar, None,
                                              ALU.mult), "qV")
                        P.wait(v, "qV")
                        v.tensor_sub(C(outr), C(208), C(224))
                        P.inc(v.tensor_add(C(outi), C(240),
                                           C(256)), "qV")
                        P.wait(v, "qV")
                        v.tensor_scalar(C(208), C(Q_r), cr, None, ALU.mult)
                        v.tensor_scalar(C(224), C(Q_i), ci, None, ALU.mult)
                        v.tensor_scalar(C(240), C(Q_r), ci, None, ALU.mult)
                        P.inc(v.tensor_scalar(C(256), C(Q_i), cr, None,
                                              ALU.mult), "qV")
                        P.wait(v, "qV")
                        v.tensor_sub(C(208), C(208),
                                     C(224))
                        P.inc(v.tensor_add(C(240), C(240),
                                           C(256)), "qV")
                        P.wait(v, "qV")
                        v.tensor_add(C(outr), C(outr), C(208))
                        P.inc(v.tensor_add(C(outi), C(outi), C(240)), "qV")
                        P.wait(v, "qV")

                    lin(272, 288, 32, 48, 64, 80)
                    lin(304, 320, 96, 112, 128, 144)
                    v.tensor_tensor(C(208), C(304), C(304), ALU.mult)
                    P.inc(v.tensor_tensor(C(224), C(320), C(320),
                                          ALU.mult), "qV")
                    P.wait(v, "qV")
                    P.inc(v.tensor_add(C(208), C(208),
                                       C(224)), "qV")
                    P.wait(v, "qV")
                    P.inc(v.reciprocal(C(208), C(208)), "qV")
                    P.wait(v, "qV")
                    v.tensor_tensor(C(240), C(272), C(304), ALU.mult)
                    P.inc(v.tensor_tensor(C(256), C(288), C(320),
                                          ALU.mult), "qV")
                    P.wait(v, "qV")
                    P.inc(v.tensor_add(C(176), C(240),
                                       C(256)), "qV")
                    P.wait(v, "qV")
                    v.tensor_tensor(C(240), C(288), C(304), ALU.mult)
                    P.inc(v.tensor_tensor(C(256), C(272), C(320),
                                          ALU.mult), "qV")
                    P.wait(v, "qV")
                    P.inc(v.tensor_sub(C(192), C(240),
                                       C(256)), "qV")
                    P.wait(v, "qV")
                    v.tensor_scalar(C(176), C(176), C(208)[:, 0:1] if False else C(208), None, ALU.mult) if False else None
                    P.inc(v.tensor_tensor(C(176), C(176), C(208),
                                          ALU.mult), "qV")
                    P.wait(v, "qV")
                    P.inc(v.tensor_tensor(C(192), C(192), C(208),
                                          ALU.mult), "qV")
                    P.wait(v, "qV")
                on_V(g6)
                mg6 = P.mark("qV")

                def g7(sd, dir_i=dir_i, mg6=mg6):
                    P.wait(sd, "qV", mg6)
                    rr = 8 + 2 * dir_i
                    P.dma(sd.dma_start(
                        out=scr[rr:rr + 1, 0:2048].rearrange("a (p i) -> (a p) i",
                                                        i=16),
                        in_=U4s[:, 176:192]))
                    P.dma(sd.dma_start(
                        out=scr[rr + 1:rr + 2, 0:2048].rearrange("a (p i) -> (a p) i",
                                                            i=16),
                        in_=U4s[:, 192:208]))
                on_D(g7)
                mg7 = P.mark("qD")

                def g8(sd, dir_i=dir_i, mg7=mg7):
                    P.wait(sd, "qD", mg7)
                    rr = 8 + 2 * dir_i
                    if dir_i == 0:
                        P.dma(sd.dma_start(out=U1[0:1, :], in_=scr[rr:rr + 1, 0:2048]))
                        P.dma(sd.dma_start(out=U2[0:1, :],
                                           in_=scr[rr + 1:rr + 2, 0:2048]))
                    else:
                        P.dma(sd.dma_start(out=U0[32:33, :], in_=scr[rr:rr + 1, 0:2048]))
                        P.dma(sd.dma_start(out=U4s[32:33, 0:2048],
                                           in_=scr[rr + 1:rr + 2, 0:2048]))
                on_D(g8)
                if dir_i == 1:
                    mg8 = P.mark("qD")

                    def g9(v, mg8=mg8):
                        P.wait(v, "qD", mg8)
                        def revap(row, off):
                            a = apof(row)
                            return bass.AP(tensor=a.tensor, offset=a.offset + off,
                                           ap=[[a.ap[0][0], 1], [-1, 1024]])
                        v.tensor_copy(U3[0:1, 0:1024],
                                      revap(U0[32:33, :], 2047))
                        P.inc(v.tensor_copy(U3[0:1, 1024:2048],
                                            revap(U0[32:33, :], 1023)), "qV")
                        v.tensor_copy(wbufA[0:1, 0:1024],
                                      revap(U4s[32:33, 0:2048], 2047))
                        P.inc(v.tensor_copy(wbufB[0:1, 0:1024],
                                            revap(U4s[32:33, 0:2048], 1023)), "qV")
                        P.wait(v, "qV")
                    on_V(g9)

            scan_cf(0)
            scan_cf(1)

            def h1(v):
                P.wait(v, "qD")
                P.wait(v, "qV")
                P.inc(v.tensor_add(U4s[0:1, 0:2048], U1[0:1, :], U3[0:1, :]), "qV")
                v.tensor_add(U1[0:1, 0:1024], U2[0:1, 0:1024], wbufA[0:1, 0:1024])
                P.inc(v.tensor_add(U1[0:1, 1024:2048], U2[0:1, 1024:2048],
                                   wbufB[0:1, 0:1024]), "qV")
                P.wait(v, "qV")
                v.tensor_sub(U4s[0:1, 0:1024], U4s[0:1, 0:1024], glg[0:1, 0:1024])
                v.tensor_sub(U4s[0:1, 1024:2048], U4s[0:1, 1024:2048],
                             gfo[0:1, 0:1024])
                P.inc(v.tensor_sub(U1[0:1, :], U1[0:1, :], U0[0:1, :]), "qV")
                P.wait(v, "qV")
                v.tensor_tensor(U2[0:1, :], U4s[0:1, 0:2048], U4s[0:1, 0:2048],
                                ALU.mult)
                P.inc(v.tensor_tensor(U3[0:1, :], U1[0:1, :], U1[0:1, :],
                                      ALU.mult), "qV")
                P.wait(v, "qV")
                P.inc(v.tensor_add(U2[0:1, :], U2[0:1, :], U3[0:1, :]), "qV")
                P.wait(v, "qV")
                P.inc(v.reciprocal(U2[0:1, :], U2[0:1, :]), "qV")
                P.wait(v, "qV")
                P.inc(v.tensor_tensor(U3[0:1, :], U4s[0:1, 0:2048], U2[0:1, :],
                                      ALU.mult), "qV")
                v.tensor_tensor(U0[0:1, :], U1[0:1, :], U2[0:1, :], ALU.mult)
                P.inc(v.tensor_scalar(U0[0:1, :], U0[0:1, :], -1.0, None,
                                      ALU.mult), "qV")
                P.wait(v, "qV")
            on_V(h1)
            mh1 = P.mark("qV")

            def h2(sd):
                P.wait(sd, "qV", mh1)
                P.dma(sd.dma_start(out=scr[12:13, 0:2048], in_=U3[0:1, :]))
                P.dma(sd.dma_start(out=scr[13:14, 0:2048], in_=U0[0:1, :]))
            on_D(h2)
            mh2 = P.mark("qD")

            def h3(sd):
                P.wait(sd, "qD", mh2)
                P.dma(sd.dma_start(out=U4s[0:1, 0:2048], in_=scr[12:13, 0:2048]))
                P.dma(sd.dma_start(out=U4s[1:2, 0:2048], in_=scr[13:14, 0:2048]))
                P.dma(sd.dma_start(out=wbufA[0:2, 0:1024], in_=w_bk))
                P.dma(sd.dma_start(out=colx[:, 0:DC], in_=b_bk_c))
            on_D(h3)
            mh3 = P.mark("qD")

            for do in range(DC):
                for q4 in range(4):
                    pp = ps[(do * 4 + q4) % 2]

                    def hm(t, do=do, q4=q4, pp=pp):
                        P.wait(t, "qV")
                        P.wait(t, "qD", mh3)
                        P.inc(t.matmul(pp[:, 0:512],
                                       wbufA[0:2, do * 128:(do + 1) * 128],
                                       U4s[0:2, q4 * 512:(q4 + 1) * 512],
                                       start=True, stop=True), "qT")
                    on_T(hm)
                    mhm = P.mark("qT")

                    def he(v, do=do, q4=q4, pp=pp):
                        P.wait(v, "qT", mhm)
                        P.inc(v.tensor_scalar(U0[:, 0:512], pp[:, 0:512],
                                              colx[:, do:do + 1], None, ALU.add),
                              "qV")
                        P.wait(v, "qV")
                        v.tensor_add(S_n[do][:, q4 * 512:(q4 + 1) * 512],
                                     S_n[do][:, q4 * 512:(q4 + 1) * 512],
                                     U0[:, 0:512])
                        P.inc(v.tensor_add(S_x[do][:, q4 * 512:(q4 + 1) * 512],
                                           S_x[do][:, q4 * 512:(q4 + 1) * 512],
                                           U0[:, 0:512]), "qV")
                    on_V(he)

        if stage >= 9:
            emit_l1_bk()
        def emit_l1_heb():
            wbufA, wbufB = xt[0], xt[1]
            U0, U1, U2, U3, U4s = S_s[0], S_s[1], S_s[2], S_s[3], S_s[4]

            def xsp(sd):
                P.wait(sd, "qV")
                for j in range(DC):
                    P.dma(sd.dma_start(out=xls[j * 128:(j + 1) * 128, :],
                                       in_=S_n[j][:, :]))
            on_D(xsp)
            mxsp = P.mark("qD")

            def e0(t):
                P.wait(t, "qV")
                P.wait(t, "qS")
                ins = None
                for j in range(DC):
                    for q4 in range(4):
                        ins = t.matmul(ps_big[0:1, q4 * 512:(q4 + 1) * 512],
                                       c_1c[:, 0:1],
                                       S_n[j][:, q4 * 512:(q4 + 1) * 512],
                                       start=(j == 0), stop=(j == DC - 1),
                                       skip_group_check=True)
                P.inc(ins, "qT", 1)
            on_T(e0)
            me0 = P.mark("qT")

            def e0b(v):
                P.wait(v, "qT", me0)
                v.tensor_copy(glg[0:1, 0:1024], ps_big[0:1, 0:1024])
                P.inc(v.tensor_copy(gfo[0:1, 0:1024], ps_big[0:1, 1024:2048]), "qV")
                P.wait(v, "qV")
            on_V(e0b)
            me0b = P.mark("qV")

            for j in range(DC):
                def e1s(sc, j=j):
                    P.wait(sc, "qT")
                    if j == 0:
                        P.wait(sc, "qV", me0b)
                    P.inc(sc.activation(U0[:, :], S_n[j][:, :], AF.Square), "qS")
                on_S(e1s)
                me1s = P.mark("qS")

                def e1m(t, j=j):
                    P.wait(t, "qS", me1s)
                    ins = None
                    for q4 in range(4):
                        ins = t.matmul(ps_big[0:1, q4 * 512:(q4 + 1) * 512],
                                       c_1c[:, 0:1], U0[:, q4 * 512:(q4 + 1) * 512],
                                       start=(j == 0), stop=(j == DC - 1),
                                       skip_group_check=True)
                    P.inc(ins, "qT", 1)
                on_T(e1m)
            me1 = P.mark("qT")

            def e2(v):
                P.wait(v, "qT", me1)
                v.tensor_copy(U0[0:1, 0:1024], ps_big[0:1, 0:1024])
                P.inc(v.tensor_copy(U0[0:1, 1024:2048], ps_big[0:1, 1024:2048]),
                      "qV")
                P.wait(v, "qV")
                for sgt, qo in ((glg, 0), (gfo, 1024)):
                    P.inc(v.tensor_scalar(sgt[0:1, 0:1024], sgt[0:1, 0:1024],
                                          1.0 / D, None, ALU.mult), "qV")
                    P.wait(v, "qV")
                    P.inc(v.tensor_scalar(U0[0:1, qo:qo + 1024],
                                          U0[0:1, qo:qo + 1024],
                                          1.0 / D, None, ALU.mult), "qV")
                    P.wait(v, "qV")
                    P.inc(v.tensor_tensor(U1[0:1, 0:1024], sgt[0:1, 0:1024],
                                          sgt[0:1, 0:1024], ALU.mult), "qV")
                    P.wait(v, "qV")
                    P.inc(v.tensor_sub(U0[0:1, qo:qo + 1024],
                                       U0[0:1, qo:qo + 1024],
                                       U1[0:1, 0:1024]), "qV")
                    P.wait(v, "qV")
            on_V(e2)
            me2 = P.mark("qV")

            def e3(sc):
                P.wait(sc, "qV", me2)
                P.inc(sc.activation(U0[0:1, 0:2048], U0[0:1, 0:2048], AF.Sqrt,
                                    bias=float(EPS_LN)), "qS")
            on_S(e3)
            me3 = P.mark("qS")

            def e4(v):
                P.wait(v, "qS", me3)
                P.inc(v.reciprocal(U0[0:1, 0:2048], U0[0:1, 0:2048]), "qV")
                P.wait(v, "qV")
                for sgt, qo in ((glg, 0), (gfo, 1024)):
                    P.inc(v.tensor_tensor(sgt[0:1, 0:1024], sgt[0:1, 0:1024],
                                          U0[0:1, qo:qo + 1024], ALU.mult), "qV")
                    P.wait(v, "qV")
                    P.inc(v.tensor_scalar(sgt[0:1, 0:1024], sgt[0:1, 0:1024],
                                          -1.0, None, ALU.mult), "qV")
                    P.wait(v, "qV")
            on_V(e4)
            me4 = P.mark("qV")

            for q4 in range(4):
                def rb(t, q4=q4):
                    P.wait(t, "qV")
                    if q4 == 0:
                        P.wait(t, "qV", me4)
                    P.inc(t.matmul(ps[2][:, 0:512], c_1r[0:1, 0:128],
                                   U0[0:1, q4 * 512:(q4 + 1) * 512],
                                   start=True, stop=True), "qT")
                on_T(rb)
                mrb = P.mark("qT")

                def rbe(v, q4=q4):
                    P.wait(v, "qT", mrb)
                    P.inc(v.tensor_copy(U4s[:, q4 * 512:(q4 + 1) * 512],
                                        ps[2][:, 0:512]), "qV")
                on_V(rbe)
            mrsd = P.mark("qV")
            if sub < 2:
                return

            def e5z(g):
                P.wait(g, "qV", mrsd)
                P.inc(g.memset(own_p2[:, 0:1], 0.0), "qG")
            on_G(e5z)
            me5z = P.mark("qG")

            for kc in range(TC):
                seg = r_t1 if kc < 8 else denb
                off = (kc % 8) * 128

                def p1(t, kc=kc, seg=seg, off=off):
                    P.wait(t, "qV")
                    if kc == 0:
                        P.wait(t, "qG", me5z)
                    P.inc(t.matmul(ps[2][:, 0:128], c_1r[0:1, 0:128],
                                   seg[0:1, off:off + 128], start=True, stop=True),
                          "qT")
                on_T(p1)
                mp1 = P.mark("qT")

                def p2(v, kc=kc):
                    P.wait(v, "qT", mp1)
                    P.inc(v.tensor_scalar(U1[:, 0:128], ps[2][:, 0:128],
                                          float(DT_HEB), None, ALU.mult), "qV")
                    P.wait(v, "qV")
                on_V(p2)
                mp2 = P.mark("qV")

                def p3(t, kc=kc):
                    P.wait(t, "qV", mp2)
                    P.inc(t.transpose(ps[3][:, 0:128], U1[:, 0:128], c_id[:]), "qT")
                on_T(p3)
                mp3 = P.mark("qT")

                def p4(v, kc=kc):
                    P.wait(v, "qT", mp3)
                    P.inc(v.tensor_copy(mcol[:, kc:kc + 1], ps[3][:, 0:1]), "qV")
                    P.wait(v, "qV")
                on_V(p4)
                mp4 = P.mark("qV")

                def p5(t, kc=kc):
                    P.wait(t, "qV", mp4)
                    t.matmul(ps[2][:, 128:129], c_1r[0:1, 0:128],
                             own_p2[0:1, 0:1], start=True, stop=False)
                    t.matmul(ps[2][:, 128:129], c_u[:, :], mcol[:, kc:kc + 1],
                             start=False, stop=True)
                    P.inc(t.matmul(ps[3][0:1, 128:129], c_1c[:, 0:1],
                                   mcol[:, kc:kc + 1], start=True, stop=True), "qT")
                on_T(p5)
                mp5 = P.mark("qT")

                def p6(v, kc=kc):
                    P.wait(v, "qT", mp5)
                    v.tensor_copy(mcol[:, 16 + kc:17 + kc], ps[2][:, 128:129])
                    P.inc(v.tensor_add(own_p2[0:1, 0:1], own_p2[0:1, 0:1],
                                       ps[3][0:1, 128:129]), "qV")
                    P.wait(v, "qV")
                on_V(p6)
            mpc = P.mark("qV")

            def p7(sc):
                P.wait(sc, "qV", mpc)
                P.inc(sc.activation(own_p[:, 0:16], mcol[:, 16:32], AF.Exp), "qS")
            on_S(p7)
            mp7 = P.mark("qS")

            def p8(v):
                P.wait(v, "qS", mp7)
                P.inc(v.tensor_scalar(own_p[:, 0:16], own_p[:, 0:16], float(ETA),
                                      None, ALU.mult), "qV")
                P.wait(v, "qV")
            on_V(p8)

            for kc in range(TC):
                def q1(v, kc=kc):
                    P.wait(v, "qT")
                    P.inc(v.tensor_copy(U1[:, 0:128],
                                        bcast_free(mcol[:, 16 + kc:17 + kc], 128)),
                          "qV")
                on_V(q1)
                mq1 = P.mark("qV")

                def q2m(t, kc=kc):
                    P.wait(t, "qV", mq1)
                    P.wait(t, "qS")
                    P.inc(t.transpose(ps[2][:, 0:128], U1[:, 0:128], c_id[:]), "qT")
                on_T(q2m)
                mq2 = P.mark("qT")

                def q3(sc, kc=kc):
                    P.wait(sc, "qT", mq2)
                    P.inc(sc.activation(U0[:, kc * 128:(kc + 1) * 128],
                                        ps[2][:, 0:128], AF.Exp, scale=-1.0), "qS")
                on_S(q3)
            mexp = P.mark("qS")

            def q4wv(v):
                P.wait(v, "qS", mexp)
            on_V(q4wv)
            if sub < 3:
                return

            def e6(sd):
                P.wait(sd, "qV")
                P.dma(sd.dma_start(out=r_t1[0:1, 0:1024],
                                   in_=b_qkv_r[0:1, 0:1024]))
                P.dma(sd.dma_start(out=denb[0:1, 0:1024], in_=b_out_r))
                P.dma(sd.dma_start(out=own_sc[:, 0:8], in_=ln_c[:, 2 * DC:3 * DC]))
            on_D(e6)
            me6 = P.mark("qD")

            def btok_ap(q4):
                t0 = glg if q4 < 2 else gfo
                return t0[0:1, (q4 % 2) * 512:(q4 % 2) * 512 + 512]

            for pr in range(4):
                def cg0(sd, pr=pr):
                    P.wait(sd, "qT")
                    P.dma(sd.dma_start(out=own_p2[:, 8:16],
                                       in_=ln_c[:, 3 * DC:4 * DC]))
                on_D(cg0)
                mcg0 = P.mark("qD")

                for j in range(DC):
                    wb = wbufA if j % 2 == 0 else wbufB

                    def cg1(sd, pr=pr, j=j, wb=wb):
                        P.wait(sd, "qT")
                        for kk, base_c in enumerate((pr * 128, 512 + pr * 128,
                                                     1024 + pr * 128)):
                            P.dma(sd.dma_start(
                                out=wb[:, kk * 128:(kk + 1) * 128],
                                in_=w_qkv[j * 128:(j + 1) * 128,
                                          base_c:base_c + 128]))
                    on_D(cg1)
                    mcg1 = P.mark("qD")

                    def cg2(v, j=j, wb=wb):
                        P.wait(v, "qD", mcg1)
                        P.wait(v, "qD", mcg0)
                        P.inc(v.tensor_scalar(wb[:, 0:384], wb[:, 0:384],
                                              own_sc[:, j:j + 1], None, ALU.mult),
                              "qV")
                        P.wait(v, "qV")
                    on_V(cg2)
                    mcg2 = P.mark("qV")

                    def cg3(t, j=j, wb=wb):
                        P.wait(t, "qV", mcg2)
                        P.inc(t.matmul(ps[2][0:1, 0:384], c_1c[:, 0:1],
                                       wb[:, 0:384], start=(j == 0),
                                       stop=(j == DC - 1), skip_group_check=True),
                              "qT")
                    on_T(cg3)
                mcg3 = P.mark("qT")

                def cg4(v):
                    P.wait(v, "qT", mcg3)
                    P.inc(v.tensor_copy(wbufA[0:1, 384:768], ps[2][0:1, 0:384]),
                          "qV")
                    P.wait(v, "qV")
                on_V(cg4)

                for j in range(DC):
                    wb = wbufB

                    def cb1(sd, pr=pr, j=j):
                        P.wait(sd, "qT")
                        for kk, base_c in enumerate((pr * 128, 512 + pr * 128,
                                                     1024 + pr * 128)):
                            P.dma(sd.dma_start(
                                out=wbufB[:, kk * 128:(kk + 1) * 128],
                                in_=w_qkv[j * 128:(j + 1) * 128,
                                          base_c:base_c + 128]))
                    on_D(cb1)
                    mcb1 = P.mark("qD")

                    def cb2(v, j=j):
                        P.wait(v, "qD", mcb1)
                        P.inc(v.tensor_scalar(wbufB[:, 0:384], wbufB[:, 0:384],
                                              own_p2[:, 8 + j:9 + j], None,
                                              ALU.mult), "qV")
                        P.wait(v, "qV")
                    on_V(cb2)
                    mcb2 = P.mark("qV")

                    def cb3(t, j=j):
                        P.wait(t, "qV", mcb2)
                        P.inc(t.matmul(ps[3][0:1, 0:384], c_1c[:, 0:1],
                                       wbufB[:, 0:384], start=(j == 0),
                                       stop=(j == DC - 1), skip_group_check=True),
                              "qT")
                    on_T(cb3)
                mcb3 = P.mark("qT")

                def cb4(v):
                    P.wait(v, "qT", mcb3)
                    P.inc(v.tensor_copy(sqd[0:1, 576:960],
                                        ps[3][0:1, 0:384]), "qV")
                    P.wait(v, "qV")
                on_V(cb4)
                mcb4 = P.mark("qV")

                def cb5(sd, pr=pr):
                    P.wait(sd, "qV", mcb4)
                    P.dma(sd.dma_start(
                        out=wbufB[0:1, 0:128],
                        in_=b_qkv_r[0:1, 1024 + pr * 128:1024 + (pr + 1) * 128]))
                on_D(cb5)
                mcb5 = P.mark("qD")

                def cb6(v, pr=pr):
                    P.wait(v, "qD", mcb5)
                    v.tensor_add(sqd[0:1, 576:704], sqd[0:1, 576:704],
                                 r_t1[0:1, pr * 128:(pr + 1) * 128])
                    v.tensor_add(sqd[0:1, 704:832], sqd[0:1, 704:832],
                                 r_t1[0:1, 512 + pr * 128:512 + (pr + 1) * 128])
                    P.inc(v.tensor_add(sqd[0:1, 832:960], sqd[0:1, 832:960],
                                       wbufB[0:1, 0:128]), "qV")
                    P.wait(v, "qV")
                on_V(cb6)

                def cgq(part):
                    return wbufA[0:1, 384 + part * 128:384 + (part + 1) * 128]

                def cbq(part):
                    return sqd[0:1, 576 + part * 128:576 + (part + 1) * 128]

                for pas in range(2):
                    for q4 in range(4):
                        for j in range(DC):
                            wb = wbufA if j % 2 == 0 else wbufB

                            def w1(sd, pr=pr, j=j, wb=wb, pas=pas, q4=q4):
                                P.wait(sd, "qT")
                                if pr == 0 and pas == 0 and q4 == 0 and j == 0:
                                    P.wait(sd, "qD", mxsp)
                                P.dma(sd.dma_start(
                                    out=sqd[:, 0:512],
                                    in_=xls[j * 128:(j + 1) * 128,
                                            q4 * 512:(q4 + 1) * 512]))
                                if pas == 0:
                                    P.dma(sd.dma_start(
                                        out=wb[:, 0:128],
                                        in_=w_qkv[j * 128:(j + 1) * 128,
                                                  pr * 128:(pr + 1) * 128]))
                                    P.dma(sd.dma_start(
                                        out=wb[:, 128:256],
                                        in_=w_qkv[j * 128:(j + 1) * 128,
                                                  512 + pr * 128:
                                                  512 + (pr + 1) * 128]))
                                else:
                                    P.dma(sd.dma_start(
                                        out=wb[:, 128:256],
                                        in_=w_qkv[j * 128:(j + 1) * 128,
                                                  1024 + pr * 128:
                                                  1024 + (pr + 1) * 128]))
                            on_D(w1)
                            mw1 = P.mark("qD")

                            def sc1(v, j=j, wb=wb, pas=pas, q4=q4):
                                P.wait(v, "qD", mw1)
                                P.wait(v, "qT")
                                if pas == 0:
                                    P.inc(v.tensor_scalar(wb[:, 0:256],
                                                          wb[:, 0:256],
                                                          own_sc[:, j:j + 1],
                                                          None, ALU.mult), "qV")
                                else:
                                    P.inc(v.tensor_scalar(wb[:, 128:256],
                                                          wb[:, 128:256],
                                                          own_sc[:, j:j + 1],
                                                          None, ALU.mult), "qV")
                                P.inc(v.tensor_tensor(
                                    sqd[:, 0:512],
                                    sqd[:, 0:512],
                                    U4s[:, q4 * 512:(q4 + 1) * 512],
                                    ALU.mult), "qV")
                                P.wait(v, "qV")
                            on_V(sc1)
                            msc1 = P.mark("qV")

                            def m1(t, j=j, q4=q4, wb=wb, pas=pas, pr=pr):
                                P.wait(t, "qV", msc1)
                                ins = None
                                if pas == 0:
                                    if j == 0:
                                        t.matmul(ps_big[:, q4 * 512:(q4 + 1) * 512],
                                                 cbq(0), c_1r[0:1, 0:512],
                                                 start=True, stop=False,
                                                 skip_group_check=True)
                                        t.matmul(ps_big[:, q4 * 512:(q4 + 1) * 512],
                                                 cgq(0), btok_ap(q4),
                                                 start=False, stop=False,
                                                 skip_group_check=True)
                                    ins = t.matmul(
                                        ps_big[:, q4 * 512:(q4 + 1) * 512],
                                        wb[:, 0:128], sqd[:, 0:512],
                                        start=False, stop=(j == DC - 1),
                                        skip_group_check=True)
                                part = 1 if pas == 0 else 2
                                for ts4 in range(4):
                                    bt = btok_ap(q4)
                                    ba = apof(bt)
                                    btsub = bass.AP(tensor=ba.tensor,
                                                    offset=ba.offset + ts4 * 128,
                                                    ap=[ba.ap[0], [1, 128]])
                                    if j == 0:
                                        t.matmul(ps[ts4][:, 0:128],
                                                 c_1r[0:1, 0:128], cbq(part),
                                                 start=True, stop=False,
                                                 skip_group_check=True)
                                        t.matmul(ps[ts4][:, 0:128],
                                                 btsub, cgq(part),
                                                 start=False, stop=False,
                                                 skip_group_check=True)
                                    ins = t.matmul(
                                        ps[ts4][:, 0:128],
                                        sqd[:, ts4 * 128:(ts4 + 1) * 128],
                                        wb[:, 128:256],
                                        start=False, stop=(j == DC - 1),
                                        skip_group_check=True)
                                P.inc(ins, "qT", 1)
                            on_T(m1)
                        mj = P.mark("qT")

                        def ev1(v, q4=q4, pas=pas):
                            P.wait(v, "qT", mj)
                            ins = None
                            if pas == 0:
                                v.tensor_copy(U1[:, q4 * 512:(q4 + 1) * 512],
                                              ps_big[:, q4 * 512:(q4 + 1) * 512])
                                for ts4 in range(4):
                                    ins = v.tensor_copy(
                                        U2[:, q4 * 512 + ts4 * 128:
                                           q4 * 512 + (ts4 + 1) * 128],
                                        ps[ts4][:, 0:128])
                            else:
                                for ts4 in range(4):
                                    ins = v.tensor_copy(
                                        U3[:, q4 * 512 + ts4 * 128:
                                           q4 * 512 + (ts4 + 1) * 128],
                                        ps[ts4][:, 0:128])
                            P.inc(ins, "qV", 1)
                        on_V(ev1)

                def wo1(sd, pr=pr):
                    P.wait(sd, "qT")
                    P.dma(sd.dma_start(out=wbufB[:, :],
                                       in_=w_out[pr * 128:(pr + 1) * 128, :]))
                on_D(wo1)
                mwo1 = P.mark("qD")

                def hz(g, pr=pr):
                    P.wait(g, "qV")
                    P.inc(g.memset(sqd[:, 384:448], 0.0), "qG")
                on_G(hz)
                mhz = P.mark("qG")

                for c in range(TC):
                    def h1t(t, c=c):
                        P.wait(t, "qV")
                        if c == 0:
                            P.wait(t, "qG", mhz)
                            P.wait(t, "qD", mwo1)
                        P.inc(t.transpose(ps[3][:, 0:128],
                                          U2[:, c * 128:(c + 1) * 128], c_id[:]),
                              "qT")
                    on_T(h1t)
                    mh1t = P.mark("qT")

                    def h2v(v, c=c):
                        P.wait(v, "qT", mh1t)
                        P.inc(v.tensor_copy(sqd[:, 0:128], ps[3][:, 0:128]), "qV")
                        P.wait(v, "qV")
                    on_V(h2v)
                    mh2v = P.mark("qV")

                    def h3t(t, c=c):
                        P.wait(t, "qV", mh2v)
                        t.matmul(ps[0][:, 0:128], sqd[0:64, 0:128],
                                 U1[0:64, c * 128:(c + 1) * 128],
                                 start=True, stop=True)
                        P.inc(t.matmul(ps[1][:, 0:128], sqd[64:128, 0:128],
                                       U1[64:128, c * 128:(c + 1) * 128],
                                       start=True, stop=True), "qT")
                    on_T(h3t)
                    mh3t = P.mark("qT")

                    def h4v(v, c=c):
                        P.wait(v, "qT", mh3t)
                        for hh in range(2):
                            P.inc(v.tensor_scalar(
                                sqd[:, 128 + hh * 128:256 + hh * 128],
                                ps[hh][:, 0:128], own_p[:, c:c + 1], None,
                                ALU.mult), "qV")
                        P.wait(v, "qV")
                        ins = None
                        for hh in range(2):
                            ins = v.tensor_tensor(
                                sqd[:, 128 + hh * 128:256 + hh * 128],
                                sqd[:, 128 + hh * 128:256 + hh * 128],
                                c_u[:, :], ALU.mult)
                        P.inc(ins, "qV", 1)
                        P.inc(v.tensor_scalar(sqd[:, 448:576],
                                              U2[:, c * 128:(c + 1) * 128],
                                              own_p[:, c:c + 1], None, ALU.mult),
                              "qV")
                        P.wait(v, "qV")
                    on_V(h4v)
                    mh4v = P.mark("qV")

                    def h5t(t, c=c):
                        P.wait(t, "qV", mh4v)
                        for hh in range(2):
                            pp = ps[hh]
                            t.matmul(pp[0:64, 0:128],
                                     U3[:, c * 128 + hh * 64:
                                        c * 128 + (hh + 1) * 64],
                                     sqd[:, 128 + hh * 128:256 + hh * 128],
                                     start=True, stop=False,
                                     skip_group_check=True)
                            t.matmul(pp[0:64, 0:128],
                                     sqd[hh * 64:(hh + 1) * 64, 384:448],
                                     U1[hh * 64:(hh + 1) * 64,
                                        c * 128:(c + 1) * 128],
                                     start=False, stop=True,
                                     skip_group_check=True)
                        t.matmul(ps[2][0:64, 0:64], sqd[:, 448:512],
                                 U3[:, c * 128:c * 128 + 64],
                                 start=(c == 0), stop=(c == TC - 1),
                                 skip_group_check=True)
                        P.inc(t.matmul(ps_big[0:64, 0:64], sqd[:, 512:576],
                                       U3[:, c * 128 + 64:c * 128 + 128],
                                       start=(c == 0), stop=(c == TC - 1),
                                       skip_group_check=True), "qT")
                    on_T(h5t)
                    mh5t = P.mark("qT")

                    def h6v(v, c=c):
                        P.wait(v, "qT", mh5t)
                        v.tensor_copy(sqd[0:64, 576:704], ps[0][0:64, 0:128])
                        P.inc(v.tensor_copy(sqd[64:128, 576:704],
                                            ps[1][0:64, 0:128]), "qV")
                        v.tensor_copy(sqd[0:64, 384:448], ps[2][0:64, 0:64])
                        P.inc(v.tensor_copy(sqd[64:128, 384:448],
                                            ps_big[0:64, 0:64]), "qV")
                        P.wait(v, "qV")
                        P.inc(v.tensor_tensor(sqd[:, 704:832], sqd[:, 576:704],
                                              U0[:, c * 128:(c + 1) * 128],
                                              ALU.mult), "qV")
                        P.wait(v, "qV")
                    on_V(h6v)
                    mh6v = P.mark("qV")

                    for do in range(DC):
                        def d2m(t, c=c, do=do, pr=pr):
                            P.wait(t, "qV")
                            if do == 0:
                                P.wait(t, "qV", mh6v)
                            if pr == 0:
                                t.matmul(ps[3][:, 128:256],
                                         denb[0:1, do * 128:(do + 1) * 128],
                                         c_1r[0:1, 0:128],
                                         start=True, stop=False,
                                         skip_group_check=True)
                                P.inc(t.matmul(ps[3][:, 128:256],
                                               wbufB[:, do * 128:(do + 1) * 128],
                                               sqd[:, 704:832],
                                               start=False, stop=True,
                                               skip_group_check=True), "qT")
                            else:
                                P.inc(t.matmul(ps[3][:, 128:256],
                                               wbufB[:, do * 128:(do + 1) * 128],
                                               sqd[:, 704:832],
                                               start=True, stop=True), "qT")
                        on_T(d2m)
                        md2 = P.mark("qT")

                        def d2e(v, c=c, do=do):
                            P.wait(v, "qT", md2)
                            P.inc(v.tensor_copy(sqd[:, 832:960],
                                                ps[3][:, 128:256]), "qV")
                            P.wait(v, "qV")
                            v.tensor_add(S_n[do][:, c * 128:(c + 1) * 128],
                                         S_n[do][:, c * 128:(c + 1) * 128],
                                         sqd[:, 832:960])
                            P.inc(v.tensor_add(S_x[do][:, c * 128:(c + 1) * 128],
                                               S_x[do][:, c * 128:(c + 1) * 128],
                                               sqd[:, 832:960]), "qV")
                        on_V(d2e)


            if sub < 4:
                return
            def f6a(sd):
                P.wait(sd, "qV")
                P.dma(sd.dma_start(out=own_sc[:, 0:8], in_=ln_c[:, 4 * DC:5 * DC]))
                P.dma(sd.dma_start(out=own_p2[:, 8:16],
                                   in_=ln_c[:, 5 * DC:6 * DC]))
                P.dma(sd.dma_start(out=colx[:, 0:DC], in_=b_ff2_c))
            on_D(f6a)
            mf6a = P.mark("qD")

            def e0c(t):
                P.wait(t, "qV")
                P.wait(t, "qS")
                ins = None
                for j in range(DC):
                    for q4 in range(4):
                        ins = t.matmul(ps_big[0:1, q4 * 512:(q4 + 1) * 512],
                                       c_1c[:, 0:1],
                                       S_n[j][:, q4 * 512:(q4 + 1) * 512],
                                       start=(j == 0), stop=(j == DC - 1),
                                       skip_group_check=True)
                P.inc(ins, "qT", 1)
            on_T(e0c)
            me0c = P.mark("qT")

            def e0d(v):
                P.wait(v, "qT", me0c)
                v.tensor_copy(glg[0:1, 0:1024], ps_big[0:1, 0:1024])
                P.inc(v.tensor_copy(gfo[0:1, 0:1024], ps_big[0:1, 1024:2048]),
                      "qV")
                P.wait(v, "qV")
            on_V(e0d)
            me0d = P.mark("qV")

            for j in range(DC):
                def e1c(sc, j=j):
                    P.wait(sc, "qT")
                    if j == 0:
                        P.wait(sc, "qV", me0d)
                    P.inc(sc.activation(U0[:, :], S_n[j][:, :], AF.Square), "qS")
                on_S(e1c)
                me1c = P.mark("qS")

                def e1d(t, j=j):
                    P.wait(t, "qS", me1c)
                    ins = None
                    for q4 in range(4):
                        ins = t.matmul(ps_big[0:1, q4 * 512:(q4 + 1) * 512],
                                       c_1c[:, 0:1],
                                       U0[:, q4 * 512:(q4 + 1) * 512],
                                       start=(j == 0), stop=(j == DC - 1),
                                       skip_group_check=True)
                    P.inc(ins, "qT", 1)
                on_T(e1d)
            me1b = P.mark("qT")

            def e2b(v):
                P.wait(v, "qT", me1b)
                v.tensor_copy(U0[0:1, 0:1024], ps_big[0:1, 0:1024])
                P.inc(v.tensor_copy(U0[0:1, 1024:2048], ps_big[0:1, 1024:2048]),
                      "qV")
                P.wait(v, "qV")
                for sgt, qo in ((glg, 0), (gfo, 1024)):
                    P.inc(v.tensor_scalar(sgt[0:1, 0:1024], sgt[0:1, 0:1024],
                                          1.0 / D, None, ALU.mult), "qV")
                    P.wait(v, "qV")
                    P.inc(v.tensor_scalar(U0[0:1, qo:qo + 1024],
                                          U0[0:1, qo:qo + 1024],
                                          1.0 / D, None, ALU.mult), "qV")
                    P.wait(v, "qV")
                    P.inc(v.tensor_tensor(U1[0:1, 0:1024], sgt[0:1, 0:1024],
                                          sgt[0:1, 0:1024], ALU.mult), "qV")
                    P.wait(v, "qV")
                    P.inc(v.tensor_sub(U0[0:1, qo:qo + 1024],
                                       U0[0:1, qo:qo + 1024],
                                       U1[0:1, 0:1024]), "qV")
                    P.wait(v, "qV")
            on_V(e2b)
            me2b = P.mark("qV")

            def e3b(sc):
                P.wait(sc, "qV", me2b)
                P.inc(sc.activation(U0[0:1, 0:2048], U0[0:1, 0:2048], AF.Sqrt,
                                    bias=float(EPS_LN)), "qS")
            on_S(e3b)
            me3b = P.mark("qS")

            def e4b(v):
                P.wait(v, "qS", me3b)
                P.inc(v.reciprocal(U0[0:1, 0:2048], U0[0:1, 0:2048]), "qV")
                P.wait(v, "qV")
                for sgt, qo in ((glg, 0), (gfo, 1024)):
                    P.inc(v.tensor_tensor(sgt[0:1, 0:1024], sgt[0:1, 0:1024],
                                          U0[0:1, qo:qo + 1024], ALU.mult), "qV")
                    P.wait(v, "qV")
                    P.inc(v.tensor_scalar(sgt[0:1, 0:1024], sgt[0:1, 0:1024],
                                          -1.0, None, ALU.mult), "qV")
                    P.wait(v, "qV")
            on_V(e4b)
            me4b = P.mark("qV")

            for q4 in range(4):
                def rb2(t, q4=q4):
                    P.wait(t, "qV")
                    if q4 == 0:
                        P.wait(t, "qV", me4b)
                    P.inc(t.matmul(ps[2][:, 0:512], c_1r[0:1, 0:128],
                                   U0[0:1, q4 * 512:(q4 + 1) * 512],
                                   start=True, stop=True), "qT")
                on_T(rb2)
                mrb2 = P.mark("qT")

                def rbe2(v, q4=q4):
                    P.wait(v, "qT", mrb2)
                    P.inc(v.tensor_copy(U4s[:, q4 * 512:(q4 + 1) * 512],
                                        ps[2][:, 0:512]), "qV")
                on_V(rbe2)
            mrsd3 = P.mark("qV")

            for hc in range(32):
                for j in range(DC):
                    wb = wbufA if j % 2 == 0 else wbufB

                    def cf1(sd, hc=hc, j=j, wb=wb):
                        P.wait(sd, "qT")
                        P.wait(sd, "qV")
                        P.dma(sd.dma_start(out=wb[:, 0:128],
                                           in_=wff1t[hc, j, :, :]))
                    on_D(cf1)
                    mcf1 = P.mark("qD")

                    def cf2(v, j=j, wb=wb):
                        P.wait(v, "qD", mcf1)
                        if j < 2:
                            P.wait(v, "qV", mrsd3)
                        v.tensor_scalar(wb[:, 128:256], wb[:, 0:128],
                                        own_sc[:, j:j + 1], None, ALU.mult)
                        P.inc(v.tensor_scalar(wb[:, 256:384], wb[:, 0:128],
                                              own_p2[:, 8 + j:9 + j], None,
                                              ALU.mult), "qV")
                        P.wait(v, "qV")
                    on_V(cf2)
                    mcf2 = P.mark("qV")

                    def cf3(t, hc=hc, j=j, wb=wb):
                        P.wait(t, "qV", mcf2)
                        t.matmul(ps[2][0:1, 0:128], c_1c[:, 0:1], wb[:, 128:256],
                                 start=(j == 0), stop=(j == DC - 1),
                                 skip_group_check=True)
                        P.inc(t.matmul(ps[3][0:1, 0:128], c_1c[:, 0:1],
                                       wb[:, 256:384],
                                       start=(j == 0), stop=(j == DC - 1),
                                       skip_group_check=True), "qT")
                    on_T(cf3)
                mcf3 = P.mark("qT")

                def cf4(v, hc=hc):
                    P.wait(v, "qT", mcf3)
                    v.tensor_copy(sqd[0:1, 768:896], ps[2][0:1, 0:128])
                    P.inc(v.tensor_copy(sqd[0:1, 896:1024], ps[3][0:1, 0:128]),
                          "qV")
                    P.wait(v, "qV")
                on_V(cf4)
                mcf4 = P.mark("qV")

                def cf5(sd, hc=hc):
                    P.wait(sd, "qV", mcf4)
                    P.dma(sd.dma_start(out=wbufA[0:1, 0:128],
                                       in_=bff1_r[0:1, hc * 128:(hc + 1) * 128]))
                on_D(cf5)
                mcf5 = P.mark("qD")

                def cf6(v, hc=hc):
                    P.wait(v, "qD", mcf5)
                    P.inc(v.tensor_add(sqd[0:1, 896:1024], sqd[0:1, 896:1024],
                                       wbufA[0:1, 0:128]), "qV")
                    P.wait(v, "qV")
                on_V(cf6)
                mcf6 = P.mark("qV")

                def cf7(sd, hc=hc):
                    P.wait(sd, "qV", mcf6)
                    P.dma(sd.dma_start(out=scr[14:15, hc * 128:(hc + 1) * 128],
                                       in_=sqd[0:1, 768:896]))
                    P.dma(sd.dma_start(out=scr[15:16, hc * 128:(hc + 1) * 128],
                                       in_=sqd[0:1, 896:1024]))
                on_D(cf7)

            def cgw(v):
                P.wait(v, "qD")
            on_V(cgw)

            def rowq(hc):
                return (sqd[0:2, 768:896] if hc % 2 == 0
                        else sqd[0:2, 896:1024])

            def wb1q(hc):
                return wbufA if hc % 2 == 0 else U1[:, 0:1024]

            def wb2q(hc):
                return wbufB if hc % 2 == 0 else U1[:, 1024:2048]

            def onc(sd):
                P.wait(sd, "qT")
                P.dma(sd.dma_start(out=U3[1:2, 0:256], in_=c_1r[0:1, 0:256]))
            on_D(onc)

            m_fm1g = []
            m_fm2g = []
            for tt in range(8):
                for j in range(DC):
                    def xa1(v, tt=tt, j=j):
                        P.wait(v, "qT")
                        P.inc(v.tensor_tensor(U0[:, j * 256:(j + 1) * 256],
                                              S_n[j][:, tt * 256:(tt + 1) * 256],
                                              U4s[:, tt * 256:(tt + 1) * 256],
                                              ALU.mult), "qV")
                    on_V(xa1)
                mxa = P.mark("qV")

                def btd(sd, tt=tt):
                    P.wait(sd, "qT")
                    t0 = glg if tt < 4 else gfo
                    P.dma(sd.dma_start(
                        out=U3[0:1, 0:256],
                        in_=t0[0:1, (tt % 4) * 256:(tt % 4) * 256 + 256]))
                on_D(btd)

                fa2m = []
                for hc in range(32):
                    hps = ps[hc % 2]
                    gidx = tt * 32 + hc

                    def fl1(sd, tt=tt, hc=hc, gidx=gidx):
                        if gidx < 2:
                            P.wait(sd, "qT")
                        else:
                            P.wait(sd, "qT", m_fm1g[gidx - 2])
                        wbd = wb1q(hc)
                        for j in range(DC):
                            P.dma(sd.dma_start(out=wbd[:, j * 128:(j + 1) * 128],
                                               in_=wff1t[hc, j, :, :]))
                        co = 768 if hc % 2 == 0 else 896
                        P.dma(sd.dma_start(out=sqd[0:1, co:co + 128],
                                           in_=scr[14:15, hc * 128:(hc + 1) * 128]))
                        P.dma(sd.dma_start(out=sqd[1:2, co:co + 128],
                                           in_=scr[15:16, hc * 128:(hc + 1) * 128]))
                    on_D(fl1)
                    mfl1 = P.mark("qD")

                    def fm1(t, tt=tt, hc=hc, hps=hps):
                        P.wait(t, "qD", mfl1)
                        P.wait(t, "qS")
                        if hc < 2:
                            P.wait(t, "qV", mxa)
                        wbd = wb1q(hc)
                        t.matmul(hps[:, 0:256], rowq(hc), U3[0:2, 0:256],
                                 start=True, stop=False, skip_group_check=True)
                        ins = None
                        for j in range(DC):
                            ins = t.matmul(hps[:, 0:256],
                                           wbd[:, j * 128:(j + 1) * 128],
                                           U0[:, j * 256:(j + 1) * 256],
                                           start=False, stop=(j == DC - 1),
                                           skip_group_check=True)
                        P.inc(ins, "qT", 1)
                    on_T(fm1)
                    mfm1 = P.mark("qT")
                    m_fm1g.append(mfm1)

                    def fg1(sc, hc=hc, hps=hps):
                        P.wait(sc, "qT", mfm1)
                        P.inc(sc.activation(sqd[:, 0:256], hps[:, 0:256],
                                            AF.Gelu_apprx_tanh), "qS")
                    on_S(fg1)
                    mfg1 = P.mark("qS")

                    def fl2(sd, tt=tt, hc=hc, gidx=gidx):
                        if gidx < 2:
                            P.wait(sd, "qT")
                        else:
                            P.wait(sd, "qT", m_fm2g[gidx - 2])
                        wbd = wb2q(hc)
                        for do in range(DC):
                            P.dma(sd.dma_start(
                                out=wbd[:, do * 128:(do + 1) * 128],
                                in_=wff2t[hc, do, :, :]))
                    on_D(fl2)
                    mfl2 = P.mark("qD")

                    for do in range(DC):
                        ops = ps[2 + do % 2]

                        def fm2(t, hc=hc, do=do, ops=ops, idx=None):
                            P.wait(t, "qS", mfg1)
                            P.wait(t, "qD", mfl2)
                            if len(fa2m) < 2:
                                P.wait(t, "qV")
                            else:
                                P.wait(t, "qV", fa2m[-2])
                            wbd = wb2q(hc)
                            P.inc(t.matmul(ops[:, 256:512],
                                           wbd[:, do * 128:(do + 1) * 128],
                                           sqd[:, 0:256],
                                           start=True, stop=True), "qT")
                        on_T(fm2)
                        mfm2 = P.mark("qT")

                        def fa2(v, hc=hc, do=do, ops=ops, tt=tt):
                            P.wait(v, "qT", mfm2)
                            dst = U2[:, do * 256:(do + 1) * 256]
                            if hc == 0:
                                P.inc(v.tensor_copy(dst, ops[:, 256:512]), "qV")
                            else:
                                P.inc(v.tensor_tensor(dst, dst, ops[:, 256:512],
                                                      ALU.add), "qV")
                        on_V(fa2)
                        fa2m.append(P.mark("qV"))
                    m_fm2g.append(P.mark("qT"))

                for do in range(DC):
                    def fe1(v, tt=tt, do=do):
                        P.wait(v, "qV")
                        P.inc(v.tensor_scalar(sqd[:, 256:512],
                                              U2[:, do * 256:(do + 1) * 256],
                                              colx[:, do:do + 1], None, ALU.add),
                              "qV")
                        P.wait(v, "qV")
                        P.inc(v.tensor_add(S_x[do][:, tt * 256:(tt + 1) * 256],
                                           S_x[do][:, tt * 256:(tt + 1) * 256],
                                           sqd[:, 256:512]), "qV")
                    on_V(fe1)

        if stage >= 10:
            emit_l1_heb()

        def emit_out():
            for kc in range(TC):
                for j in range(DC):
                    pp = ps[j % 2]

                    def g1t(t, kc=kc, j=j, pp=pp):
                        P.wait(t, "qV")
                        if j == 0:
                            P.wait(t, "qD")
                        P.inc(t.transpose(pp[:, 0:128],
                                          S_x[j][:, kc * 128:(kc + 1) * 128],
                                          c_id[:]), "qT")
                    on_T(g1t)
                    mg1t = P.mark("qT")

                    def g2v(v, kc=kc, j=j, pp=pp):
                        P.wait(v, "qT", mg1t)
                        P.inc(v.tensor_copy(sqd[:, j * 128:(j + 1) * 128],
                                            pp[:, 0:128]), "qV")
                    on_V(g2v)
                mg2v = P.mark("qV")

                def g3d(sd, kc=kc):
                    P.wait(sd, "qV", mg2v)
                    P.dma(sd.dma_start(out=out[kc * 128:(kc + 1) * 128, :],
                                       in_=sqd[:, :]))
                on_D(g3d)

        if stage >= 11:
            emit_out()


        def fin(s):
            P.wait(s, "qV")
            P.wait(s, "qT")
            P.wait(s, "qS")
            if dbg is not None and stage >= 1:
                P.dma(s.dma_start(out=dbg[:, 0:2048], in_=S_n[1][:, :]))
                P.dma(s.dma_start(out=dbg[:, 2048:4096], in_=S_x[1][:, :]))
                P.dma(s.dma_start(out=dbg[:, 4096:4096 + TC * 8], in_=olg[:, :]))
                P.dma(s.dma_start(out=dbg[0:1, 4224:4224 + 64], in_=r_cnt[0:1, 0:64]))
                P.dma(s.dma_start(out=dbg[0:1, 4288:4288 + 8], in_=r_pfx[0:1, 0:8]))
                P.dma(s.dma_start(out=dbg[:, 4352:4352 + TC], in_=own_slot[:, :]))
                P.dma(s.dma_start(out=dbg[:, 4368:4368 + TC], in_=own_sc[:, :]))
                P.dma(s.dma_start(out=dbg[:, 4384:4384 + TC], in_=own_p[:, :]))
                P.dma(s.dma_start(out=dbg[:, 4400:4400 + TC * 8], in_=ofo[:, :]))
                P.dma(s.dma_start(out=dbg[:, 4528:4528 + TC * 8], in_=own_lrk[:, :]))
                P.dma(s.dma_start(out=dbg[:, 7168:7168 + 1024], in_=gfo[:, 0:1024]))
                if stage >= 8:
                    P.dma(s.dma_start(out=dbg[:, 4656:4656 + 16], in_=own_slot[:, 0:16]))
                    P.dma(s.dma_start(out=dbg[:, 4672:4672 + 16], in_=own_lrk[:, 0:16]))
                    P.dma(s.dma_start(out=dbg[:, 4688:4688 + 16], in_=own_sc[:, 0:16]))
                    P.dma(s.dma_start(out=dbg[:, 4704:4704 + 16], in_=own_p2[:, 0:16]))
                    P.dma(s.dma_start(out=dbg[:, 4720:4720 + 128], in_=olg[:, 0:128]))
            if stage < 11:
                P.dma(s.dma_start(out=out[0:128, :], in_=xt[0][:, :]))
            P.wait(s, "qD")
        on_D(fin)

    return nc


def make_consts(core):
    mask = (np.arange(8) < core).astype(np.float32)
    return {
        "coremask": np.repeat(mask, 8)[None, :].copy(),
        "identc": np.eye(128, dtype=np.float32),
        "u128c": np.triu(np.ones((128, 128), np.float32)),
        "ones_rowc": np.ones((1, 512), np.float32),
        "ones_colc": np.ones((128, 1), np.float32),
        "iota_rowc": np.tile(np.arange(512, dtype=np.float32)[None, :], (128, 1)),
        "iota_colc": np.arange(128, dtype=np.float32)[:, None].copy(),
        "fvalsc": np.tile(np.array([[0.0, 1.0, 1e-5, 1e-6, 1e-8, 0.1, 0.5, -1.0,
                             1e-30, 0, 0, 0, 0, 0, 0, 0]], np.float32), (128, 1)),
    }


def col_layout(vec, width=128):
    v = np.asarray(vec, np.float32)
    return np.ascontiguousarray(v.reshape(-1, width).T)


def make_weight_inputs(inputs):
    g = lambda k: np.ascontiguousarray(np.asarray(inputs[k], np.float32))
    d = {}
    d["Wq"] = g("L0_attn_Wq"); d["Wk"] = g("L0_attn_Wk"); d["Wv"] = g("L0_attn_Wv")
    d["Wo"] = g("L0_attn_Wo"); d["Wg"] = g("L0_attn_Wg")
    d["spT"] = np.ascontiguousarray(g("L0_attn_splat_positions").T)
    d["ssc_row"] = g("L0_attn_splat_scales")[None, :]
    d["simp_row"] = g("L0_attn_splat_importance")[None, :]
    d["We"] = g("L0_attn_We"); d["be_row"] = g("L0_attn_be")[None, :]
    d["be_col"] = col_layout(g("L0_attn_be"))
    d["btT"] = np.ascontiguousarray(g("L0_attn_bt").T)
    d["Wt"] = g("L0_attn_Wt"); d["bt_r"] = g("L0_attn_bt")
    d["Wd"] = g("L0_attn_Wd"); d["bd_row"] = g("L0_attn_bd")[None, :]
    d["bg_col"] = col_layout(g("L0_attn_bg"))
    d["W1"] = g("L0_ffn_W1"); d["W2"] = g("L0_ffn_W2"); d["Wr"] = g("L0_ffn_Wr")
    ln = [g(f"L1_attn_ln{i}_{p}") for i in (1, 2, 3) for p in ("g", "b")]
    d["ln_col"] = np.concatenate([col_layout(v) for v in ln], axis=1)
    d["w_pot"] = g("L1_attn_w_pot"); d["b_pot_row"] = g("L1_attn_b_pot")[None, :]
    d["w_bk"] = g("L1_attn_w_bk"); d["b_bk_col"] = col_layout(g("L1_attn_b_bk"))
    d["w_qkv"] = g("L1_attn_w_qkv"); d["b_qkv_row"] = g("L1_attn_b_qkv")[None, :]
    d["w_out"] = g("L1_attn_w_out"); d["b_out_col"] = col_layout(g("L1_attn_b_out"))
    d["b_out_row"] = g("L1_attn_b_out")[None, :]
    d["w_ff1"] = g("L1_attn_w_ff1"); d["b_ff1_col"] = col_layout(g("L1_attn_b_ff1"))
    d["bff1_row"] = g("L1_attn_b_ff1")[None, :]
    wf1 = g("L1_attn_w_ff1")
    d["wff1t"] = np.ascontiguousarray(
        wf1.reshape(DC, 128, 32, 128).transpose(2, 0, 1, 3))
    wf2 = g("L1_attn_w_ff2")
    d["wff2t"] = np.ascontiguousarray(
        wf2.reshape(32, 128, DC, 128).transpose(0, 2, 1, 3))
    d["w_ff2"] = g("L1_attn_w_ff2"); d["b_ff2_col"] = col_layout(g("L1_attn_b_ff2"))
    d["rw"] = g("L1_ffn_router_w"); d["rsc_col"] = col_layout(g("L1_ffn_router_scale"))
    d["gwT"] = np.ascontiguousarray(np.transpose(g("L1_ffn_gating_w"), (0, 1, 3, 2)))
    d["lw"] = g("L1_ffn_linear_w")
    d["pes_row"] = g("L1_ffn_per_expert_scale")[None, :]
    return d


def kernel(**inputs):
    x = np.ascontiguousarray(np.asarray(inputs["L0_attn_x"], np.float32))
    x2d = x.reshape(B * N, D)
    wd = make_weight_inputs(inputs)
    nc = build_nc(stage=99)
    in_maps = []
    for c in range(B):
        m = {"xall": x2d, "xown": np.ascontiguousarray(x2d[c * N:(c + 1) * N])}
        m.update(make_consts(c))
        m.update(wd)
        in_maps.append(m)
    res = run_bass_kernel_spmd(nc, in_maps, core_ids=list(range(B)))
    outs = [res.results[c]["out"] for c in range(B)]
    return np.stack(outs, axis=0)
